# version 10
# baseline (speedup 1.0000x reference)
import numpy as np
import concourse.bass as bass
import concourse.mybir as mybir
from concourse.bass_utils import run_bass_kernel_spmd

F32 = mybir.dt.float32
BF16 = mybir.dt.bfloat16
AF = mybir.ActivationFunctionType
ALU = mybir.AluOpType
AX = mybir.AxisListType.X

D = 2048
TOK = 2048
DC = 16
EPS = 1e-5
ENGS = ("tensor", "vector", "scalar", "sync", "gpsimd")

DEBUG = False


class KB:
    def __init__(self):
        self.nc = bass.Bass("TRN2", target_bir_lowering=False)
        self.plan = {e: [] for e in ENGS}
        self.cnt = {}
        self.semh = {}
        self.lastwait = {}
        self.pending = {e: [] for e in ENGS}

    def sems(self, *names):
        for n in names:
            self.semh[n] = self.nc.alloc_semaphore(n)
            self.cnt[n] = 0

    def barrier(self):
        snap = [(s, v) for s, v in self.cnt.items() if v > 0]
        for e in ENGS:
            self.pending[e].extend(snap)

    def op(self, eng, fn, wait=(), inc=None, n=1):
        todo = []
        allw = list(self.pending[eng]) + list(wait)
        self.pending[eng] = []
        for s, v in allw:
            if v > 0 and self.lastwait.get((eng, s), 0) < v:
                self.lastwait[(eng, s)] = v
                todo.append((s, v))
        if inc is not None:
            self.cnt[inc] += n
        semh = self.semh

        def run(e, todo=tuple(todo), fn=fn, inc=inc, n=n):
            for s, v in todo:
                e.wait_ge(semh[s], v)
            r = fn(e)
            if inc is not None:
                r.then_inc(semh[inc], n)
        self.plan[eng].append(run)
        return self.cnt[inc] if inc is not None else None

    SERIAL_SEMS = {"sm", "sbr", "scc", "swsp", "swis", "sdbg", "swfp", "swip",
                   "swo0", "swo1"}

    def dma(self, q, out, in_, wait=(), sem="sm"):
        if sem not in self.semh:
            self.sems(sem)
        if sem in self.SERIAL_SEMS and self.cnt[sem] > 0:
            wait = list(wait) + [(sem, self.cnt[sem])]
        return self.op(q, lambda e: e.dma_start(out=out, in_=in_), wait, sem, 16)

    def finish(self):
        with self.nc.Block() as block:
            for eng in ENGS:
                getattr(block, eng)(lambda e, eng=eng: [f(e) for f in self.plan[eng]])
        return self.nc


def v3(ap2d, c):
    return ap2d.rearrange("p (c u) -> p c u", c=c)


def build_nc():
    k = KB()
    nc = k.nc
    k.sems("ts", "vs", "ss", "cc", "sm")

    def din(name, shape):
        return nc.dram_tensor(name, shape, F32, kind="ExternalInput").ap()

    x_in = din("x_in", [TOK, D])
    w_in = din("w_in", [12, 128, 8192])
    b_in = din("b_in", [1, 3 * D])
    w_out = din("w_out", [D, D])
    w_ffnT = din("w_ffnT", [D, D])
    gamma_part = din("gamma_part", [128, DC])
    bq_part = din("bq_part", [128, DC])
    gamma_row = din("gamma_row", [1, D])
    beta_row = din("beta_row", [1, D])
    b_ffn = din("b_ffn", [1, D])
    b_out_r = din("b_out_r", [1, D])
    atall_in = din("atall", [D, 128])
    btall_in = din("btall", [128, D])
    gwT_in = din("gwT", [D, 8])
    gb_in = din("gb", [1, 8])
    cos_in = din("cos_in", [TOK, 512])
    sin_in = din("sin_in", [TOK, 512])
    ident_in = din("ident_in", [128, 128])
    bsel_in = din("bsel_in", [128, 2])

    y_out = nc.dram_tensor("y_out", [TOK, D], F32, kind="ExternalOutput").ap()
    dbg = {}
    if DEBUG:
        for nm, shape in [("dbg_xn", [TOK, D]), ("dbg_xp", [TOK, D]),
                          ("dbg_ap", [128, 16]), ("dbg_af", [128, 16]),
                          ("dbg_w8", [TOK, 8]), ("dbg_s1", [TOK, D]),
                          ("dbg_x0", [128, D]), ("dbg_st", [128, 32]),
                          ("dbg_s1d", [TOK, D]), ("dbg_af2", [128, 16])]:
            dbg[nm] = nc.dram_tensor(nm, shape, F32, kind="ExternalOutput").ap()
        for nm, shape in [("dbg_qh", [D, TOK]), ("dbg_ql", [D, TOK]),
                          ("dbg_x2h", [D, TOK]), ("dbg_woh", [128, 16384]),
                          ("dbg_wol", [128, 16384])]:
            dbg[nm] = nc.dram_tensor(nm, shape, BF16, kind="ExternalOutput").ap()

    wfh_d = nc.dram_tensor("wfh_d", [4, 128, 8192], BF16).ap()
    wfl_d = nc.dram_tensor("wfl_d", [4, 128, 8192], BF16).ap()
    wih_d = nc.dram_tensor("wih_d", [12, 128, 8192], BF16).ap()
    wil_d = nc.dram_tensor("wil_d", [12, 128, 8192], BF16).ap()
    x2th_d = nc.dram_tensor("x2th_d", [D, TOK], BF16).ap()
    x2tl_d = nc.dram_tensor("x2tl_d", [D, TOK], BF16).ap()
    qth_d = nc.dram_tensor("qth_d", [D, TOK], BF16).ap()
    qtl_d = nc.dram_tensor("qtl_d", [D, TOK], BF16).ap()
    s1_d = nc.dram_tensor("s1_d", [TOK, D], F32).ap()
    cc_in_d = nc.dram_tensor("cc_in_d", [128, 32], F32)
    cc_out_d = nc.dram_tensor("cc_out_d", [128, 32], F32)
    cc_in_d2 = nc.dram_tensor("cc_in_d2", [128, 32], F32)
    cc_out_d2 = nc.dram_tensor("cc_out_d2", [128, 32], F32)

    x2th = x2th_d.rearrange("(c p) t -> p c t", p=128)
    x2tl = x2tl_d.rearrange("(c p) t -> p c t", p=128)
    qth = qth_d.rearrange("(c p) t -> p c t", p=128)
    qtl = qtl_d.rearrange("(c p) t -> p c t", p=128)
    w_ffnT3 = w_ffnT.rearrange("(c p) o -> p c o", p=128)

    SB_TOP = 229344
    F32_COLS = 24064
    B16_COLS = 58288
    base = (SB_TOP - (F32_COLS * 4 + B16_COLS * 2)) // 64 * 64
    f32a = nc.alloc_sbuf_tensor_at("f32_arena", [128, F32_COLS], F32, offset=base).ap()
    b16a = nc.alloc_sbuf_tensor_at("b16_arena", [128, B16_COLS], BF16,
                                   offset=base + F32_COLS * 4).ap()

    class Arena:
        def __init__(self, ap, cols):
            self.ap, self.cols, self.off, self.marks = ap, cols, 0, []

        def alloc(self, n):
            n = (n + 15) // 16 * 16
            assert self.off + n <= self.cols, f"arena overflow {self.off}+{n}>{self.cols}"
            s = self.ap[:, self.off:self.off + n]
            self.off += n
            return s

        def push(self):
            self.marks.append(self.off)

        def pop(self):
            self.off = self.marks.pop()

    fa, ba = Arena(f32a, F32_COLS), Arena(b16a, B16_COLS)

    idt = fa.alloc(128)
    ones_row = fa.alloc(512)
    ones_col = fa.alloc(16)
    bsel = fa.alloc(16)
    gpart = fa.alloc(16)
    bqp = fa.alloc(16)
    a_p = fa.alloc(16)
    a_fin = fa.alloc(16)
    ccs = fa.alloc(32)
    ccs2 = fa.alloc(32)
    rowa = fa.alloc(512)
    rowb = fa.alloc(512)
    rowc = fa.alloc(512)
    fa.push()
    ba.push()

    pb = [nc.alloc_psum_tensor(f"pb{i}", [128, 512], F32).ap() for i in range(8)]
    bankfree = {i: [] for i in range(8)}

    b_sum = fa.alloc(1024)

    def bsum_row(c):
        p = [0, 32, 64, 0][c]
        off = 512 if c == 3 else 0
        return b_sum[p:p + 1, off:off + 512]
    bog_rep = fa.alloc(2048)

    k.dma("sync", idt[:, :], ident_in[:, :])
    k.dma("sync", bsel[:, 0:2], bsel_in[:, :])
    k.dma("sync", gpart[:, 0:DC], gamma_part[:, :])
    k.dma("sync", bqp[:, 0:DC], bq_part[:, :])
    k.op("vector", lambda e: e.memset(ones_row[:, :], 1.0))
    k.op("vector", lambda e: e.memset(st_[:, :], 0.0))
    k.op("vector", lambda e: e.memset(g_sc[:, :], 0.0))
    k.op("vector", lambda e: e.memset(ones_col[:, 0:1], 1.0), inc="vs")

    for j in range(4):
        sl = slice(j * 512, (j + 1) * 512)
        prev_v = k.cnt["vs"]
        k.dma("sync", rowa[0:1, :], gamma_row[0:1, sl], wait=[("vs", prev_v)])
        k.dma("sync", rowb[0:1, :], beta_row[0:1, sl])
        k.dma("sync", rowc[[0, 32, 64, 0][j]:[0, 32, 64, 0][j] + 1, :], b_ffn[0:1, sl])
        ldj = k.cnt["sm"]
        k.op("vector", lambda e: e.reciprocal(rowa[0:1, :], rowa[0:1, :]),
             wait=[("sm", ldj), ("vs", k.cnt["vs"])], inc="vs")
        k.op("vector", lambda e: e.tensor_mul(rowb[0:1, :], rowb[0:1, :], rowa[0:1, :]),
             wait=[("vs", k.cnt["vs"])], inc="vs")
        evd = k.cnt["vs"]
        k.op("tensor", lambda e: e.matmul(pb[7][:, :], ones_row[0:1, 0:128], rowb[0:1, :],
                                          start=True, stop=True),
             wait=[("vs", evd)] + bankfree[7], inc="ts")
        evt = k.cnt["ts"]
        k.op("vector", lambda e, sl=sl: e.tensor_copy(bog_rep[:, sl], pb[7][:, :]),
             wait=[("ts", evt)], inc="vs")
        bankfree[7] = [("vs", k.cnt["vs"])]
        pj_ = [0, 32, 64, 0][j]
        k.dma("sync", rowa[pj_:pj_ + 1, :], b_out_r[0:1, sl], wait=[("vs", evd)])
        ldj2 = k.cnt["sm"]
        k.op("vector", lambda e, j=j, pj_=pj_: e.tensor_tensor(
            bsum_row(j), rowa[pj_:pj_ + 1, :], rowc[pj_:pj_ + 1, :], ALU.add),
            wait=[("sm", ldj2)], inc="vs")
    ev_consts_v = k.cnt["vs"]

    xb = [fa.alloc(2048), fa.alloc(2048)]
    xn_t = fa.alloc(2048)
    zz = fa.alloc(2048)
    xxn = [fa.alloc(2048), fa.alloc(2048)]
    wsp = fa.alloc(4096)
    s1s = [fa.alloc(512), fa.alloc(512)]
    st_ = fa.alloc(32)
    xa_f2 = [fa.alloc(144), fa.alloc(144)]
    g_sc = fa.alloc(48)
    dbg_buf = fa.alloc(32) if DEBUG else None

    xnT_h = ba.alloc(4096)
    xnT_l = ba.alloc(4096)
    x2s_h = [ba.alloc(2048), ba.alloc(2048)]
    x2s_l = [ba.alloc(2048), ba.alloc(2048)]
    wf_h = [ba.alloc(8192), ba.alloc(8192)]
    wf_l = [ba.alloc(8192), ba.alloc(8192)]
    at_h = ba.alloc(2048)
    at_l = ba.alloc(2048)
    bt_h = ba.alloc(2048)
    bt_l = ba.alloc(2048)
    gw_h = ba.alloc(128)
    gw_l = ba.alloc(128)
    xas_h = [ba.alloc(128), ba.alloc(128)]
    xas_l = [ba.alloc(128), ba.alloc(128)]

    def small_split(dst_h, dst_l, ncols, load_fn):
        ld = load_fn()
        k.op("scalar", lambda e: e.activation(dst_h, wsp[:, 0:ncols], AF.Copy),
             wait=[("sm", ld)], inc="ss")
        eh = k.cnt["ss"]
        k.op("vector", lambda e: e.tensor_sub(dst_l, wsp[:, 0:ncols], dst_h),
             wait=[("ss", eh), ("sm", ld)], inc="vs")
        return k.cnt["vs"]

    ev_w = small_split(at_h[:, :], at_l[:, :], 2048,
                       lambda: k.dma("sync", v3(wsp[:, 0:2048], DC),
                                     atall_in.rearrange("(c p) r -> p c r", p=128)))
    ev_w = small_split(bt_h[:, :], bt_l[:, :], 2048,
                       lambda: k.dma("sync", wsp[:, 0:2048], btall_in[:, :],
                                     wait=[("vs", ev_w)]))
    ev_w = small_split(gw_h[:, :], gw_l[:, :], 128,
                       lambda: k.dma("sync", v3(wsp[:, 0:128], DC),
                                     gwT_in.rearrange("(c p) g -> p c g", p=128),
                                     wait=[("vs", ev_w)]))
    k.dma("sync", rowa[0:1, 0:8], gb_in[0:1, :])
    ld_gb = k.cnt["sm"]
    ev_wsp_free = ev_w

    ev_x_free = {}
    ev_xxn_free = {}
    ev_qg_x2s = {0: 0, 1: 0}
    ev_xn_tp = 0
    ev_zz_tp = 0
    ev_qg_x2 = 0
    ev_wf_used = {0: 0, 1: 0}
    ev_wf_persist = {}
    ev_s1_free = {0: 0, 1: 0}
    ev_xas_used = 0
    ev_xnT_consumed = 0

    S = dict(sum1=0, sq1=1, mu1=2, var1=3, r1=4, nmr1=5, r1p=6, sum2=7, sq2=8,
             mu2=9, var2=10, r2=11, l1=12, l2=13, d12=14, g1=15, g2=16, tmp=17, rc=18,
             r1p0=20, r1p1=21, nmr0=22, nmr1_1=23)

    def st(nm):
        i = S[nm]
        return st_[:, i:i + 1]

    def svop(fn, wait=()):
        return k.op("vector", fn, wait=list(wait) + [("vs", k.cnt["vs"])], inc="vs")

    RGT, REQ, RANK = g_sc[:, 0:8], g_sc[:, 8:16], g_sc[:, 16:24]
    M1, M2, W8 = g_sc[:, 24:32], g_sc[:, 32:40], g_sc[:, 40:48]

    def transpose_group(src, dcs, bi, dst_h3, dst_l3, wait_first, wait_evac):
        bank = pb[bi]
        for c, dc in enumerate(dcs):
            k.op("tensor", lambda e, c=c, dc=dc: e.matmul(
                bank[:, c * 128:(c + 1) * 128], src[:, dc * 128:(dc + 1) * 128],
                idt[:, :], start=(c == 0), stop=(c == 3), is_transpose=True),
                wait=(list(wait_first) + bankfree[bi]) if c == 0 else (),
                inc="ts" if c == 3 else None)
        evt = k.cnt["ts"]
        k.op("scalar", lambda e: e.activation(dst_h3, v3(bank[:, :], 4), AF.Copy),
             wait=[("ts", evt)] + list(wait_evac), inc="ss")
        ehh = k.cnt["ss"]
        k.op("vector", lambda e: e.tensor_sub(dst_l3, v3(bank[:, :], 4), dst_h3),
             wait=[("ts", evt), ("ss", ehh)] + list(wait_evac), inc="vs")
        bankfree[bi] = [("ss", ehh), ("vs", k.cnt["vs"])]

    ev_xnT_tile = {}
    pend_xas = {}
    ev_xas_consumed = {0: 0, 1: 0}
    TPB = [6, 7, 0, 1, 2, 3]
    tp_i = [0]

    def next_tpb():
        b = TPB[tp_i[0] % 6]
        tp_i[0] += 1
        return b
    ln_ev = {}
    last_xxn_ss = [0]

    def emit_ln_tile(tt):
        t = tt % 2
        xbuf = xb[t]
        wl = ev_x_free.get(tt - 2, [("vs", ev_consts_v)])
        sx = "sx%d" % (tt % 2)
        ldx = k.dma("gpsimd", xbuf[:, :], x_in[tt * 128:tt * 128 + 128, :],
                    wait=wl, sem=sx)
        svop(lambda e, xbuf=xbuf: e.reduce_sum(st("sum1"), xbuf[:, :], axis=AX),
             wait=[(sx, ldx), ("ss", last_xxn_ss[0])])
        k.op("scalar", lambda e, xbuf=xbuf: e.activation(
            wsp[:, 0:2048], xbuf[:, :], AF.Square, accum_out=st("sq1")),
            wait=[(sx, ldx), ("vs", ev_wsp_free)], inc="ss")
        ev_sq1 = k.cnt["ss"]
        svop(lambda e: e.tensor_single_scalar(st("mu1"), st("sum1"), 1.0 / D, ALU.mult))
        svop(lambda e: e.tensor_single_scalar(st("sq1"), st("sq1"), 1.0 / D, ALU.mult),
             wait=[("ss", ev_sq1)])
        svop(lambda e: e.tensor_mul(st("var1"), st("mu1"), st("mu1")))
        svop(lambda e: e.tensor_sub(st("var1"), st("sq1"), st("var1")))
        svop(lambda e: e.tensor_single_scalar(st("var1"), st("var1"), EPS, ALU.add))
        svop(lambda e: e.reciprocal(st("rc"), st("var1")))
        ev_v1 = k.cnt["vs"]
        k.op("scalar", lambda e: e.activation(st("r1"), st("rc"), AF.Sqrt),
             wait=[("vs", ev_v1)], inc="ss")
        ev_r1 = k.cnt["ss"]
        svop(lambda e: e.tensor_mul(st("nmr1"), st("mu1"), st("r1")),
             wait=[("ss", ev_r1)])
        svop(lambda e: e.tensor_single_scalar(st("nmr1"), st("nmr1"), -1.0, ALU.mult))
        svop(lambda e: e.tensor_single_scalar(st("r1p"), st("r1"), 1.0, ALU.add))
        ev_ln1 = k.cnt["vs"]
        k.op("scalar", lambda e, xbuf=xbuf: e.activation(
            xn_t[:, :], xbuf[:, :], AF.Identity, bias=st("nmr1"), scale=st("r1")),
            wait=[("vs", ev_ln1), ("ts", ev_xn_tp)], inc="ss")
        ev_xn = k.cnt["ss"]
        svop(lambda e: e.reduce_sum(st("sum2"), xn_t[:, :], axis=AX),
             wait=[("ss", ev_xn)])
        k.op("scalar", lambda e: e.activation(wsp[:, 0:2048], xn_t[:, :], AF.Square,
                                              accum_out=st("sq2")),
             wait=[("ss", ev_xn)], inc="ss")
        ev_sq2 = k.cnt["ss"]
        svop(lambda e: e.tensor_single_scalar(st("mu2"), st("sum2"), 1.0 / D, ALU.mult))
        svop(lambda e: e.tensor_single_scalar(st("sq2"), st("sq2"), 1.0 / D, ALU.mult),
             wait=[("ss", ev_sq2)])
        svop(lambda e: e.tensor_mul(st("var2"), st("mu2"), st("mu2")))
        svop(lambda e: e.tensor_sub(st("var2"), st("sq2"), st("var2")))
        svop(lambda e: e.tensor_single_scalar(st("var2"), st("var2"), EPS, ALU.add))
        svop(lambda e: e.reciprocal(st("rc"), st("var2")))
        ev_v2 = k.cnt["vs"]
        k.op("scalar", lambda e: e.activation(st("r2"), st("rc"), AF.Sqrt),
             wait=[("vs", ev_v2)], inc="ss")
        ev_r2 = k.cnt["ss"]
        k.op("vector", lambda e: e.tensor_scalar(
            out=zz[:, :], in0=xn_t[:, :], scalar1=st("mu2"), scalar2=st("r2"),
            op0=ALU.subtract, op1=ALU.mult),
            wait=[("ts", ev_zz_tp), ("ss", ev_r2)], inc="vs")
        k.op("vector", lambda e: e.tensor_tensor(zz[:, :], zz[:, :], bog_rep[:, :],
                                                 ALU.add),
             wait=[("vs", k.cnt["vs"])], inc="vs")
        ev_xp = k.cnt["vs"]
        ln_ev[tt] = dict(sx=sx, ldx=ldx, xn=ev_xn, xp=ev_xp)

    for sw in range(8):
        ev_tile_xas = {}
        for t in range(2):
            tt = sw * 2 + t
            tok0 = tt * 128
            xbuf = xb[t]
            if tt not in ln_ev:
                emit_ln_tile(tt)
            lev = ln_ev.pop(tt)
            sx, ldx, ev_xn, ev_xp = lev["sx"], lev["ldx"], lev["xn"], lev["xp"]
            k.op("scalar", lambda e, xbuf=xbuf, t=t: e.activation(
                xxn[t][:, :], xbuf[:, :], AF.Identity, bias=st("nmr1"), scale=st("r1p")),
                wait=[("vs", ev_xxn_free.get(tt - 2, 0))], inc="ss")
            ev_xxn_w = k.cnt["ss"]
            last_xxn_ss[0] = ev_xxn_w
            if DEBUG:
                if tt == 0:
                    k.dma("gpsimd", dbg["dbg_x0"], xbuf[:, :], wait=[(sx, ldx)], sem="sdbg")
                    k.op("vector", lambda e: e.tensor_copy(dbg_buf[:, 0:32], st_[:, 0:32]),
                         inc="vs")
                    k.dma("gpsimd", dbg["dbg_st"], dbg_buf[:, 0:32],
                          wait=[("vs", k.cnt["vs"])])
                k.dma("gpsimd", dbg["dbg_xn"][tok0:tok0 + 128, :], xn_t[:, :],
                      wait=[("ss", ev_xn)])
                k.dma("gpsimd", dbg["dbg_xp"][tok0:tok0 + 128, :], zz[:, :],
                      wait=[("vs", ev_xp)])

            xnT_h16 = v3(xnT_h[:, :], DC)
            xnT_l16 = v3(xnT_l[:, :], DC)
            for g in range(4):
                dsl = (slice(None), slice(4 * g, 4 * g + 4), slice(t * 128, t * 128 + 128))
                transpose_group(xn_t, [4 * g + c for c in range(4)], next_tpb(),
                                xnT_h16[dsl], xnT_l16[dsl],
                                [("ss", ev_xn)], [("ts", ev_xnT_consumed)])
            ev_xn_tp = k.cnt["ts"]
            ev_xnT = (k.cnt["ss"], k.cnt["vs"])
            ev_xnT_tile[tt] = [("ss", k.cnt["ss"]), ("vs", k.cnt["vs"])]

            xa_f = xa_f2[t]
            LOG = xa_f[:, 128:136]
            MSK = xa_f[:, 136:144]
            ev_xa_prev_used = ev_xas_used
            nmm = 0
            for dc in range(DC):
                lh = xnT_h[:, dc * 256 + t * 128: dc * 256 + t * 128 + 128]
                ll = xnT_l[:, dc * 256 + t * 128: dc * 256 + t * 128 + 128]
                for lw, rwa, rwg in ((lh, at_h, gw_h), (lh, at_l, gw_l), (ll, at_h, gw_h)):
                    wl0 = ([("ss", ev_xnT[0]), ("vs", ev_xnT[1]), ("sm", ld_gb)]
                           + bankfree[4]) if nmm == 0 else ()
                    k.op("tensor", lambda e, lw=lw, rw=rwa, dc=dc, nmm=nmm: e.matmul(
                        pb[4][:, 0:128], lw, rw[:, dc * 128:(dc + 1) * 128],
                        start=(nmm == 0), stop=False), wait=wl0)
                    k.op("tensor", lambda e, lw=lw, rw=rwg, dc=dc: e.matmul(
                        pb[4][:, 128:136], lw, rw[:, dc * 8:(dc + 1) * 8],
                        start=False, stop=False))
                    nmm += 1
            k.op("tensor", lambda e: e.matmul(pb[4][:, 128:136], ones_row[0:1, 0:128],
                                              rowa[0:1, 0:8], start=False, stop=True),
                 inc="ts")
            ev_xa_t = k.cnt["ts"]
            svop(lambda e, xa_f=xa_f: e.tensor_copy(xa_f[:, 0:136], pb[4][:, 0:136]),
                 wait=[("ts", ev_xa_t), ("ts", ev_xa_prev_used)])
            bankfree[4] = [("vs", k.cnt["vs"])]
            ev_evac = k.cnt["vs"]
            MSK2 = st_[:, 24:32]
            for e8 in range(8):
                svop(lambda e, e8=e8, LOG=LOG, MSK=MSK: e.tensor_scalar(
                    out=MSK, in0=LOG, scalar1=LOG[:, e8:e8 + 1], scalar2=None,
                    op0=ALU.is_gt, op1=ALU.add, accum_out=RGT[:, e8:e8 + 1]),
                    wait=[("vs", ev_evac)] if e8 == 0 else ())
            svop(lambda e: e.memset(REQ[:, 0:1], 0.0))
            for e8 in range(1, 8):
                svop(lambda e, e8=e8, LOG=LOG, MSK=MSK: e.tensor_scalar(
                    out=MSK[:, 0:e8], in0=LOG[:, 0:e8], scalar1=LOG[:, e8:e8 + 1],
                    scalar2=None, op0=ALU.is_equal, op1=ALU.add,
                    accum_out=REQ[:, e8:e8 + 1]))
            svop(lambda e: e.tensor_add(RANK, RGT, REQ))
            svop(lambda e: e.tensor_single_scalar(M1, RANK, 0.5, ALU.is_lt))
            svop(lambda e: e.tensor_single_scalar(M2, RANK, 1.0, ALU.is_equal))
            svop(lambda e, MSK=MSK, LOG=LOG: e.tensor_mul(MSK, M1, LOG))
            svop(lambda e, MSK2=MSK2, LOG=LOG: e.tensor_mul(MSK2, M2, LOG))
            svop(lambda e, MSK=MSK: e.reduce_sum(st("l1"), MSK, axis=AX))
            svop(lambda e, MSK2=MSK2: e.reduce_sum(st("l2"), MSK2, axis=AX))
            svop(lambda e: e.tensor_sub(st("d12"), st("l1"), st("l2")))
            ev_d12 = k.cnt["vs"]
            k.op("scalar", lambda e: e.activation(st("g1"), st("d12"), AF.Sigmoid),
                 wait=[("vs", ev_d12)], inc="ss")
            ev_g1 = k.cnt["ss"]
            svop(lambda e: e.tensor_scalar(
                out=st("g2"), in0=st("g1"), scalar1=-1.0, scalar2=1.0,
                op0=ALU.mult, op1=ALU.add), wait=[("ss", ev_g1)])
            svop(lambda e: e.tensor_scalar_mul(W8, M1, st("g1")))
            svop(lambda e: e.scalar_tensor_tensor(
                out=W8, in0=M2, scalar=st("g2"), in1=W8, op0=ALU.mult, op1=ALU.add))
            ev_w8 = k.cnt["vs"]
            if DEBUG:
                k.dma("gpsimd", dbg["dbg_w8"][tok0:tok0 + 128, :], W8,
                      wait=[("vs", ev_w8)])
            for e8 in range(8):
                k.op("vector", lambda e, e8=e8, xa_f=xa_f: e.tensor_scalar_mul(
                    xa_f[:, e8 * 16:(e8 + 1) * 16], xa_f[:, e8 * 16:(e8 + 1) * 16],
                    W8[:, e8:e8 + 1]), wait=[("vs", ev_w8)],
                    inc="vs" if e8 == 7 else None)
            svop(lambda e: e.tensor_copy(st("tmp"), st("tmp")))
            pend_xas[t] = (xa_f, k.cnt["vs"])
            ev_x_free[tt] = [("vs", k.cnt["vs"]), ("ss", ev_xxn_w)]

            xsl = tt % 2
            for g in range(4):
                transpose_group(zz, [4 * g + c for c in range(4)], next_tpb(),
                                v3(x2s_h[xsl][:, g * 512:(g + 1) * 512], 4),
                                v3(x2s_l[xsl][:, g * 512:(g + 1) * 512], 4),
                                [("vs", ev_xp)], [("sx2%d" % xsl, ev_qg_x2s[xsl])])
            ev_zz_tp = k.cnt["ts"]
            ev_x2s = (k.cnt["ss"], k.cnt["vs"])
            k.dma("gpsimd", x2th[:, :, tok0:tok0 + 128], v3(x2s_h[xsl][:, :], DC),
                  wait=[("ss", ev_x2s[0]), ("vs", ev_x2s[1])], sem="sx2%d" % xsl)
            k.dma("gpsimd", x2tl[:, :, tok0:tok0 + 128], v3(x2s_l[xsl][:, :], DC),
                  sem="sx2%d" % xsl)
            ev_qg_x2s[xsl] = k.cnt["sx2%d" % xsl]

        for t in range(2):
            tt = sw * 2 + t
            xa_f, ev_xas_v = pend_xas[t]
            k.op("tensor", lambda e, xa_f=xa_f: e.matmul(
                pb[5][:, 0:128], xa_f[:, 0:128], idt[:, :],
                start=True, stop=True, is_transpose=True),
                wait=[("vs", ev_xas_v)] + bankfree[5], inc="ts")
            ev_xas_used = k.cnt["ts"]
            k.op("scalar", lambda e, t=t: e.activation(xas_h[t][:, :], pb[5][:, 0:128], AF.Copy),
                 wait=[("ts", ev_xas_used), ("ts", ev_xas_consumed[t])], inc="ss")
            eh = k.cnt["ss"]
            k.op("vector", lambda e, t=t: e.tensor_sub(xas_l[t][:, :], pb[5][:, 0:128],
                                                       xas_h[t][:, :]),
                 wait=[("ts", ev_xas_used), ("ss", eh)], inc="vs")
            bankfree[5] = [("vs", k.cnt["vs"])]
            ev_tile_xas[tt] = [("ss", eh), ("vs", k.cnt["vs"])]

        if 1 <= sw < 7:
            emit_ln_tile(2 * (sw + 1))

        for o in range(4):
            slot = o % 2
            if sw == 0:
                for h in range(2):
                    osl = slice(o * 512 + h * 256, o * 512 + h * 256 + 256)
                    ld = k.dma("sync", v3(wsp[:, 0:4096], DC), w_ffnT3[:, :, osl],
                               wait=[("vs", k.cnt["vs"]), ("ts", ev_wf_used[slot]),
                                     ("swfp", k.cnt.get("swfp", 0))], sem="swsp")
                    dst_h = v3(wf_h[slot][:, :], DC)[:, :, h * 256:h * 256 + 256]
                    dst_l = v3(wf_l[slot][:, :], DC)[:, :, h * 256:h * 256 + 256]
                    k.op("scalar", lambda e, dst_h=dst_h: e.activation(
                        dst_h, v3(wsp[:, 0:4096], DC), AF.Copy), wait=[("swsp", ld)], inc="ss")
                    eh = k.cnt["ss"]
                    k.op("vector", lambda e, dst_l=dst_l, dst_h=dst_h: e.tensor_sub(
                        dst_l, v3(wsp[:, 0:4096], DC), dst_h), wait=[("ss", eh)], inc="vs")
                k.dma("gpsimd", wfh_d[o], wf_h[slot][:, :],
                      wait=[("ss", k.cnt["ss"])], sem="swfp")
                k.dma("gpsimd", wfl_d[o], wf_l[slot][:, :],
                      wait=[("vs", k.cnt["vs"])], sem="swfp")
                ev_wf_persist[o] = k.cnt["swfp"]
                ev_wf_ld = [("vs", k.cnt["vs"]), ("ss", k.cnt["ss"])]
                ev_wsp_free = k.cnt["vs"]
            else:
                k.dma("sync", wf_h[slot][:, :], wfh_d[o],
                      wait=[("ts", ev_wf_used[slot]), ("swfp", k.cnt["swfp"])],
                      sem="swf%d" % slot)
                k.dma("scalar", wf_l[slot][:, :], wfl_d[o],
                      wait=[("ts", ev_wf_used[slot]), ("swfp", k.cnt["swfp"])],
                      sem="swg%d" % slot)
                ev_wf_ld = [("swf%d" % slot, k.cnt["swf%d" % slot]),
                            ("swg%d" % slot, k.cnt["swg%d" % slot])]
            for t in range(2):
                tt = sw * 2 + t
                tok0 = tt * 128
                bi = (2 * o + t) % 4
                bank = pb[bi]
                bsl = slice(o * 512, o * 512 + 512)
                wl = (ev_wf_ld + bankfree[bi] + ev_xnT_tile[tt])
                po = [0, 32, 64, 0][o]
                k.op("tensor", lambda e, bank=bank, o=o, po=po: e.matmul(
                    bank[:, :], ones_row[po:po + 1, 0:128], bsum_row(o),
                    start=True, stop=False), wait=wl)
                for dc in range(DC):
                    lh = xnT_h[:, dc * 256 + t * 128: dc * 256 + t * 128 + 128]
                    ll = xnT_l[:, dc * 256 + t * 128: dc * 256 + t * 128 + 128]
                    rh = wf_h[slot][:, dc * 512:(dc + 1) * 512]
                    rl = wf_l[slot][:, dc * 512:(dc + 1) * 512]
                    for lw, rw in ((lh, rh), (lh, rl), (ll, rh)):
                        k.op("tensor", lambda e, lw=lw, rw=rw, bank=bank: e.matmul(
                            bank[:, :], lw, rw, start=False, stop=False))
                lora_ops = ((xas_h[t], bt_h), (xas_h[t], bt_l), (xas_l[t], bt_h))
                for li, (lw, rw) in enumerate(lora_ops):
                    k.op("tensor", lambda e, lw=lw, rw=rw, bank=bank, bsl=bsl, li=li: e.matmul(
                        bank[:, :], lw[:, :], rw[:, bsl], start=False, stop=(li == 2)),
                        wait=ev_tile_xas[tt] if li == 0 else (),
                        inc="ts" if li == 2 else None)
                ev_b = k.cnt["ts"]
                ev_xas_consumed[t] = ev_b
                if t == 1:
                    ev_wf_used[slot] = ev_b
                s1slot = t
                k.op("vector", lambda e, bank=bank, t=t, bsl=bsl, s1slot=s1slot: e.tensor_tensor(
                    s1s[s1slot][:, :], bank[:, :], xxn[t][:, bsl], ALU.add),
                    wait=[("ts", ev_b), ("ss1%d" % s1slot, ev_s1_free[s1slot])], inc="vs")
                bankfree[bi] = [("vs", k.cnt["vs"])]
                ev_s1v = k.cnt["vs"]
                k.dma("gpsimd", s1_d[tok0:tok0 + 128, bsl], s1s[s1slot][:, :],
                      wait=[("vs", ev_s1v)], sem="ss1%d" % s1slot)
                ev_s1_free[s1slot] = k.cnt["ss1%d" % s1slot]
                if DEBUG:
                    k.dma("gpsimd", dbg["dbg_s1"][tok0:tok0 + 128, bsl], s1s[s1slot][:, :],
                          sem="ss1%d" % s1slot)
                    ev_s1_free[s1slot] = k.cnt["ss1%d" % s1slot]
                if o == 3:
                    ev_xxn_free[tt] = ev_s1v
        ev_xnT_consumed = k.cnt["ts"]

    ev_1a_end_x2 = [("sx2%d" % s, k.cnt.get("sx2%d" % s, 0)) for s in range(2)]

    fa.pop()
    ba.pop()
    fa.push()
    ba.push()
    k.barrier()

    wi_stage = fa.alloc(4096)
    P4 = [fa.alloc(2048) for _ in range(4)]
    kvs = fa.alloc(2560)
    ks_, vsb = kvs[:, 0:512], kvs[:, 512:1024]
    krot = kvs[:, 1024:1536]
    m4 = kvs[:, 1536:2560]
    ssc = m4[:, 0:512]
    cs_b = [fa.alloc(1024) for _ in range(4)]
    nsq = fa.alloc(48)
    a_acc = fa.alloc(2048)
    sqw = fa.alloc(512)
    k.op("vector", lambda e: e.memset(a_acc[:, :], 0.0), inc="vs")

    x2q_h = ba.alloc(8192)
    x2q_l = ba.alloc(8192)
    wi_h = [ba.alloc(8192), ba.alloc(8192)]
    wi_l = [ba.alloc(8192), ba.alloc(8192)]
    qt_sh = [ba.alloc(512), ba.alloc(512)]
    qt_sl = [ba.alloc(512), ba.alloc(512)]

    CHUNK_ORDER = [0, 1, 2, 3, 4, 8, 5, 9, 6, 10, 7, 11]
    ev_wi_used = {0: 0, 1: 0}
    ev_wi_persist = {}
    ev_wi_persist_all = 0
    ev_qt_free = {0: 0, 1: 0}
    ev_x2q_used = 0
    ev_cs_free = {0: 0, 1: 0, 2: 0, 3: 0}
    ev_aacc_v = ev_consts_v
    ev_P_free = {0: 0, 1: 0, 2: 0, 3: 0}
    next_q_wl = {}
    ldb_k = {}
    ldb_v = {}

    for q in range(4):
        qtok = q * 512
        k.dma("sync", v3(x2q_h[:, :], DC), x2th[:, :, qtok:qtok + 512],
              wait=ev_1a_end_x2 + [("ts", ev_x2q_used)], sem="sx2qh")
        k.dma("gpsimd", v3(x2q_l[:, :], DC), x2tl[:, :, qtok:qtok + 512],
              wait=[("ts", ev_x2q_used)], sem="sx2ql")
        ev_x2q_w = [("sx2qh", k.cnt["sx2qh"]), ("sx2ql", k.cnt["sx2ql"])]
        if q == 0:
            k.dma("scalar", rowb[0:1, :], b_in[0:1, 4 * 512:4 * 512 + 512],
                  wait=[("vs", ev_consts_v)], sem="sbr")
            ldb_k[0] = k.cnt["sbr"]
            k.dma("scalar", rowc[0:1, :], b_in[0:1, 8 * 512:8 * 512 + 512], sem="sbr")
            ldb_v[0] = k.cnt["sbr"]

        def load_chunk(n, q=q):
            j = CHUNK_ORDER[n % 12]
            slot = n % 2
            if q == 0 and n < 12:
                for h in range(2):
                    ld = k.dma("sync", wi_stage[:, 0:4096],
                               w_in[j][:, h * 4096:h * 4096 + 4096],
                               wait=[("vs", k.cnt["vs"]), ("ss", k.cnt["ss"]),
                                     ("ts", ev_wi_used[slot]),
                                     ("swip", k.cnt.get("swip", 0))], sem="swis")
                    for dcl in range(8):
                        dc = h * 8 + dcl
                        seg = wi_stage[:, dcl * 512:(dcl + 1) * 512]
                        dh = wi_h[slot][:, dc * 512:(dc + 1) * 512]
                        k.op("scalar", lambda e, seg=seg, dh=dh, dc=dc: e.activation(
                            dh, seg, AF.Identity, scale=gpart[:, dc:dc + 1]),
                            wait=[("swis", ld)] if dcl == 0 else (),
                            inc="ss" if dcl == 7 else None)
                    evh = k.cnt["ss"]
                    for dcl in range(8):
                        dc = h * 8 + dcl
                        seg = wi_stage[:, dcl * 512:(dcl + 1) * 512]
                        dh = wi_h[slot][:, dc * 512:(dc + 1) * 512]
                        dl = wi_l[slot][:, dc * 512:(dc + 1) * 512]
                        k.op("vector", lambda e, seg=seg, dh=dh, dl=dl, dc=dc: e.scalar_tensor_tensor(
                            out=dl, in0=seg, scalar=gpart[:, dc:dc + 1], in1=dh,
                            op0=ALU.mult, op1=ALU.subtract),
                            wait=[("ss", evh)] if dcl == 0 else (),
                            inc="vs" if dcl == 7 else None)
                k.dma("gpsimd", wih_d[j], wi_h[slot][:, :], wait=[("ss", k.cnt["ss"])],
                      sem="swip")
                k.dma("gpsimd", wil_d[j], wi_l[slot][:, :], wait=[("vs", k.cnt["vs"])],
                      sem="swip")
                ev_wi_persist[n] = k.cnt["swip"]
                return [("vs", k.cnt["vs"]), ("ss", k.cnt["ss"])]
            else:
                pw = k.cnt["swip"] if (q == 0) else ev_wi_persist_all
                k.dma("sync", wi_h[slot][:, :], wih_d[j],
                      wait=[("ts", ev_wi_used[slot]), ("swip", pw)],
                      sem="swi%d" % slot)
                k.dma("scalar", wi_l[slot][:, :], wil_d[j],
                      wait=[("ts", ev_wi_used[slot]), ("swip", pw)],
                      sem="swl%d" % slot)
                return [("swi%d" % slot, k.cnt["swi%d" % slot]),
                        ("swl%d" % slot, k.cnt["swl%d" % slot])]

        if q == 0:
            ev_wl = {0: load_chunk(0), 1: load_chunk(1)}
        else:
            ev_wl = dict(next_q_wl)
        nload = 2

        for wj in range(4):
            slot = wj % 2
            wlc = ev_wl[slot]
            for qfl in range(4):
                qf = 4 * wj + qfl
                bi = qf % 2
                bank = pb[bi]
                wl = list(wlc) + bankfree[bi] + ev_x2q_w
                nmm = 0
                for dc in range(DC):
                    lhh = wi_h[slot][:, dc * 512 + qfl * 128: dc * 512 + qfl * 128 + 128]
                    lhl = wi_l[slot][:, dc * 512 + qfl * 128: dc * 512 + qfl * 128 + 128]
                    rhh = x2q_h[:, dc * 512:(dc + 1) * 512]
                    rhl = x2q_l[:, dc * 512:(dc + 1) * 512]
                    for lw, rw in ((lhh, rhh), (lhh, rhl), (lhl, rhh)):
                        nmm += 1
                        k.op("tensor", lambda e, lw=lw, rw=rw, bank=bank, nmm=nmm: e.matmul(
                            bank[:, :], lw, rw, start=(nmm == 1), stop=(nmm == 48)),
                            wait=wl if nmm == 1 else (),
                            inc="ts" if nmm == 48 else None)
                ev_qmm = k.cnt["ts"]
                if qfl == 3:
                    ev_wi_used[slot] = ev_qmm
                    if nload < 12:
                        ev_wl[nload % 2] = load_chunk(nload)
                        nload += 1
                qs_ = qf % 2
                k.op("scalar", lambda e, bank=bank, qs_=qs_, qf=qf: e.activation(
                    qt_sh[qs_][:, :], bank[:, :], AF.Identity,
                    bias=bqp[:, qf:qf + 1], scale=1.0),
                    wait=[("ts", ev_qmm), ("sqt%d" % qs_, ev_qt_free[qs_])], inc="ss")
                eh = k.cnt["ss"]
                k.op("vector", lambda e, bank=bank, qs_=qs_, qf=qf: e.scalar_tensor_tensor(
                    out=qt_sl[qs_][:, :], in0=bank[:, :], scalar=bqp[:, qf:qf + 1],
                    in1=qt_sh[qs_][:, :], op0=ALU.add, op1=ALU.subtract),
                    wait=[("ts", ev_qmm), ("ss", eh), ("sqt%d" % qs_, ev_qt_free[qs_])],
                    inc="vs")
                bankfree[bi] = [("vs", k.cnt["vs"])]
                k.dma("gpsimd", qth[:, qf, qtok:qtok + 512], qt_sh[qs_][:, :],
                      wait=[("ss", eh), ("vs", k.cnt["vs"])], sem="sqt%d" % qs_)
                k.dma("gpsimd", qtl[:, qf, qtok:qtok + 512], qt_sl[qs_][:, :],
                      sem="sqt%d" % qs_)
                ev_qt_free[qs_] = k.cnt["sqt%d" % qs_]

        kv_banks = [2, 3, 4, 5, 6, 7]
        kvb_i = 0
        for i in range(4):
            tok0 = qtok + i * 128
            k.dma("gpsimd", cs_b[i][:, 0:512], cos_in[tok0:tok0 + 128, :],
                  wait=[("vs", ev_cs_free[i])], sem="scs%d" % i)
            k.dma("gpsimd", cs_b[i][:, 512:1024], sin_in[tok0:tok0 + 128, :],
                  sem="scs%d" % i)
            ev_cs_free[("ld", i)] = k.cnt["scs%d" % i]
        kbank = {}
        for pj in range(4):
            nk_idx, nv_idx = 4 + 2 * pj, 5 + 2 * pj
            slot_kc, slot_vc = nk_idx % 2, nv_idx % 2
            jk, jv = CHUNK_ORDER[nk_idx], CHUNK_ORDER[nv_idx]
            if q == 0:
                rw_k = [("sbr", ldb_k[pj])]
                rw_v = [("sbr", ldb_v[pj])]
                browk, browv = rowb, rowc
            else:
                rw_k = rw_v = [("sm", ev_kvrows)]
                browk = wi_stage[0:1, (jk - 4) * 512:(jk - 3) * 512]
                browv = wi_stage[0:1, (jv - 4) * 512:(jv - 3) * 512]
            ev_K = {}
            for i in range(4):
                bk = kv_banks[kvb_i % 6]; kvb_i += 1
                kbank[i] = bk
                bank = pb[bk]
                wl = list(bankfree[bk]) + rw_k
                if i == 0:
                    wl = wl + list(ev_wl[slot_kc]) + ev_x2q_w
                k.op("tensor", lambda e, bank=bank, brow=browk: e.matmul(
                    bank[:, :], ones_row[0:1, 0:128], brow[0:1, :],
                    start=True, stop=False), wait=wl)
                nmm = 0
                for dc in range(DC):
                    lhh = x2q_h[:, dc * 512 + i * 128: dc * 512 + i * 128 + 128]
                    lhl = x2q_l[:, dc * 512 + i * 128: dc * 512 + i * 128 + 128]
                    rhh = wi_h[slot_kc][:, dc * 512:(dc + 1) * 512]
                    rhl = wi_l[slot_kc][:, dc * 512:(dc + 1) * 512]
                    for lw, rw in ((lhh, rhh), (lhh, rhl), (lhl, rhh)):
                        nmm += 1
                        k.op("tensor", lambda e, lw=lw, rw=rw, bank=bank, nmm=nmm: e.matmul(
                            bank[:, :], lw, rw, start=False, stop=(nmm == 48)),
                            inc="ts" if nmm == 48 else None)
                ev_K[i] = k.cnt["ts"]
            ev_wi_used[slot_kc] = ev_K[3]
            if q == 0 and pj < 3:
                jk2 = CHUNK_ORDER[nk_idx + 2]
                k.dma("scalar", rowb[0:1, :], b_in[0:1, jk2 * 512:jk2 * 512 + 512],
                      wait=[("ts", ev_K[3])], sem="sbr")
                ldb_k[pj + 1] = k.cnt["sbr"]
            if nload < 12:
                ev_wl[nload % 2] = load_chunk(nload)
                nload += 1
            elif q < 3:
                next_q_wl[0] = load_chunk(12)
            ev_V = {}
            for i in range(4):
                bv = kv_banks[kvb_i % 6]; kvb_i += 1
                bank = pb[bv]
                wl = list(bankfree[bv]) + rw_v
                if i == 0:
                    wl = wl + list(ev_wl[slot_vc]) + ev_x2q_w
                k.op("tensor", lambda e, bank=bank, brow=browv: e.matmul(
                    bank[:, :], ones_row[0:1, 0:128], brow[0:1, :],
                    start=True, stop=False), wait=wl)
                nmm = 0
                for dc in range(DC):
                    lhh = x2q_h[:, dc * 512 + i * 128: dc * 512 + i * 128 + 128]
                    lhl = x2q_l[:, dc * 512 + i * 128: dc * 512 + i * 128 + 128]
                    rhh = wi_h[slot_vc][:, dc * 512:(dc + 1) * 512]
                    rhl = wi_l[slot_vc][:, dc * 512:(dc + 1) * 512]
                    for lw, rw in ((lhh, rhh), (lhh, rhl), (lhl, rhh)):
                        nmm += 1
                        k.op("tensor", lambda e, lw=lw, rw=rw, bank=bank, nmm=nmm: e.matmul(
                            bank[:, :], lw, rw, start=False, stop=(nmm == 48)),
                            inc="ts" if nmm == 48 else None)
                ev_V[i] = k.cnt["ts"]
                pk, pv = pb[kbank[i]], pb[bv]
                k.op("vector", lambda e, pk=pk: e.tensor_copy(ks_[:, :], pk[:, :]),
                     wait=[("ts", ev_K[i]), ("vs", k.cnt["vs"])], inc="vs")
                ev_cpk = k.cnt["vs"]
                bankfree[kbank[i]] = [("vs", ev_cpk)]
                k.op("scalar", lambda e, i=i, pj=pj: e.activation(
                    sqw[:, :], ks_[:, :], AF.Square,
                    accum_out=nsq[:, i * 4 + pj:i * 4 + pj + 1]),
                    wait=[("vs", ev_cpk)], inc="ss")
                ev_sqK = k.cnt["ss"]
                psl = slice(pj * 512, pj * 512 + 512)
                if pj < 2:
                    cth = cs_b[i][:, pj * 256:pj * 256 + 256]
                    sth = cs_b[i][:, 512 + pj * 256:512 + pj * 256 + 256]
                    ke, ko = ks_[:, 0:512:2], ks_[:, 1:512:2]
                    ve, vo = vsb[:, 0:512:2], vsb[:, 1:512:2]
                    re_, ro_ = krot[:, 0:512:2], krot[:, 1:512:2]
                    ma, mb_ = m4[:, 0:256], m4[:, 256:512]
                    mc, md = m4[:, 512:768], m4[:, 768:1024]
                    k.op("vector", lambda e, ke=ke, cth=cth: e.tensor_mul(ma, ke, cth),
                         wait=[("scs%d" % i, ev_cs_free[("ld", i)]),
                               ("vs", k.cnt["vs"])])
                    k.op("vector", lambda e, ko=ko, sth=sth: e.tensor_mul(mb_, ko, sth))
                    k.op("vector", lambda e, ko=ko, cth=cth: e.tensor_mul(mc, ko, cth))
                    k.op("vector", lambda e, ke=ke, sth=sth: e.tensor_mul(md, ke, sth),
                         inc="vs")
                    k.op("vector", lambda e, re_=re_: e.tensor_sub(re_, ma, mb_),
                         wait=[("vs", k.cnt["vs"])])
                    k.op("vector", lambda e, ro_=ro_: e.tensor_add(ro_, mc, md), inc="vs")
                    k.op("vector", lambda e, pv=pv: e.tensor_copy(vsb[:, :], pv[:, :]),
                         wait=[("ts", ev_V[i]), ("vs", k.cnt["vs"])], inc="vs")
                    ev_cpv = k.cnt["vs"]
                    bankfree[bv] = [("vs", ev_cpv)]
                    k.op("scalar", lambda e, i=i, pj=pj, psl=psl: e.activation(
                        P4[i][:, psl], vsb[:, :], AF.Square,
                        accum_out=nsq[:, 16 + i * 4 + pj:17 + i * 4 + pj]),
                        wait=[("vs", ev_cpv), ("vs", ev_P_free[i])], inc="ss")
                    ev_sqV = k.cnt["ss"]
                    k.op("vector", lambda e, ve=ve, cth=cth: e.tensor_mul(ma, ve, cth),
                         wait=[("ss", ev_sqK), ("vs", k.cnt["vs"])])
                    k.op("vector", lambda e, vo=vo, sth=sth: e.tensor_mul(mb_, vo, sth))
                    k.op("vector", lambda e, vo=vo, cth=cth: e.tensor_mul(mc, vo, cth))
                    k.op("vector", lambda e, ve=ve, sth=sth: e.tensor_mul(md, ve, sth),
                         inc="vs")
                    k.op("vector", lambda e, ke=ke: e.tensor_sub(ke, ma, mb_),
                         wait=[("vs", k.cnt["vs"])])
                    k.op("vector", lambda e, ko=ko: e.tensor_add(ko, mc, md), inc="vs")
                    k.op("vector", lambda e, i=i, psl=psl: e.tensor_mul(
                        P4[i][:, psl], krot[:, :], ks_[:, :]),
                        wait=[("vs", k.cnt["vs"]), ("ss", ev_sqV)], inc="vs")
                    if pj == 1:
                        ev_cs_free[i] = k.cnt["vs"]
                else:
                    k.op("vector", lambda e, pv=pv: e.tensor_copy(vsb[:, :], pv[:, :]),
                         wait=[("ts", ev_V[i]), ("vs", k.cnt["vs"])], inc="vs")
                    ev_cpv = k.cnt["vs"]
                    bankfree[bv] = [("vs", ev_cpv)]
                    k.op("scalar", lambda e, i=i, pj=pj, psl=psl: e.activation(
                        P4[i][:, psl], vsb[:, :], AF.Square,
                        accum_out=nsq[:, 16 + i * 4 + pj:17 + i * 4 + pj]),
                        wait=[("vs", ev_cpv), ("vs", ev_P_free[i])], inc="ss")
                    ev_sqV = k.cnt["ss"]
                    k.op("vector", lambda e, i=i, psl=psl: e.tensor_mul(
                        P4[i][:, psl], ks_[:, :], vsb[:, :]),
                        wait=[("ss", ev_sqV), ("vs", k.cnt["vs"])], inc="vs")
                if pj == 3:
                    k.op("vector", lambda e, i=i: e.reduce_sum(
                        nsq[:, 32:33], nsq[:, i * 4:i * 4 + 4], axis=AX),
                        wait=[("ss", ev_sqV), ("vs", k.cnt["vs"])], inc="vs")
                    svop(lambda e, i=i: e.reduce_sum(
                        nsq[:, 33:34], nsq[:, 16 + i * 4:16 + i * 4 + 4], axis=AX))
                    svop(lambda e: e.tensor_mul(nsq[:, 34:35], nsq[:, 32:33],
                                                nsq[:, 33:34]))
                    svop(lambda e: e.reciprocal(nsq[:, 36:37], nsq[:, 34:35]))
                    ev_np = k.cnt["vs"]
                    k.op("scalar", lambda e: e.activation(nsq[:, 35:36], nsq[:, 36:37],
                                                          AF.Sqrt),
                         wait=[("vs", ev_np)], inc="ss")
                    ev_st2 = k.cnt["ss"]
                    k.op("vector", lambda e, i=i: e.scalar_tensor_tensor(
                        out=a_acc[:, :], in0=P4[i][:, :], scalar=nsq[:, 35:36],
                        in1=a_acc[:, :], op0=ALU.mult, op1=ALU.add),
                        wait=[("ss", ev_st2)], inc="vs")
                    ev_aacc_v = k.cnt["vs"]
                    ev_P_free[i] = ev_aacc_v
            ev_wi_used[slot_vc] = ev_V[3]
            if q == 0 and pj < 3:
                jv2 = CHUNK_ORDER[nv_idx + 2]
                k.dma("scalar", rowc[0:1, :], b_in[0:1, jv2 * 512:jv2 * 512 + 512],
                      wait=[("ts", ev_V[3])], sem="sbr")
                ldb_v[pj + 1] = k.cnt["sbr"]
            if nload < 12:
                ev_wl[nload % 2] = load_chunk(nload)
                nload += 1
            elif q < 3:
                next_q_wl[1] = load_chunk(13)
        ev_x2q_used = k.cnt["ts"]
        if q == 2:
            for c in range(DC):
                k.op("tensor", lambda e, c=c: e.matmul(
                    pb[0][:, c:c + 1], a_acc[:, c * 128:(c + 1) * 128], ones_col[:, 0:1],
                    start=(c == 0), stop=(c == DC - 1)),
                    wait=([("vs", ev_aacc_v)] + bankfree[0]) if c == 0 else (),
                    inc="ts" if c == DC - 1 else None)
            ev_ap1 = k.cnt["ts"]
            k.op("vector", lambda e: e.tensor_copy(a_p[:, 0:16], pb[0][:, 0:16]),
                 wait=[("ts", ev_ap1), ("vs", k.cnt["vs"])], inc="vs")
            bankfree[0] = [("vs", k.cnt["vs"])]
            k.op("vector", lambda e: e.tensor_scalar_mul(ccs[:, 0:16], a_p[:, 0:16],
                                                         bsel[:, 0:1]),
                 wait=[("vs", k.cnt["vs"])])
            k.op("vector", lambda e: e.tensor_scalar_mul(ccs[:, 16:32], a_p[:, 0:16],
                                                         bsel[:, 1:2]), inc="vs")
            ev_ccs1 = k.cnt["vs"]
            k.dma("gpsimd", cc_in_d.ap(), ccs[:, 0:32], wait=[("vs", ev_ccs1)], sem="scc")
            ev_ccin1 = k.cnt["scc"]
            k.op("gpsimd", lambda e: e.collective_compute(
                "AllReduce", ALU.add, replica_groups=[list(range(8))],
                ins=[cc_in_d.ap().opt()], outs=[cc_out_d.ap().opt()]),
                wait=[("scc", ev_ccin1)], inc="cc")
            k.op("vector", lambda e: e.memset(a_acc[:, :], 0.0),
                 wait=[("ts", ev_ap1), ("vs", k.cnt["vs"])], inc="vs")
            ev_aacc_v = k.cnt["vs"]
        if q == 0:
            ev_wi_persist_all = k.cnt["swip"]
            k.dma("sync", v3(wi_stage[0:1, 0:4096], 8),
                  b_in.rearrange("z (c n) -> z c n", c=12)[:, 4:12, :],
                  wait=[("vs", k.cnt["vs"]), ("ss", k.cnt["ss"])], sem="sm")
            ev_kvrows = k.cnt["sm"]

    for c in range(DC):
        k.op("tensor", lambda e, c=c: e.matmul(
            pb[0][:, c:c + 1], a_acc[:, c * 128:(c + 1) * 128], ones_col[:, 0:1],
            start=(c == 0), stop=(c == DC - 1)),
            wait=([("vs", ev_aacc_v)] + bankfree[0]) if c == 0 else (),
            inc="ts" if c == DC - 1 else None)
    ev_ap_t = k.cnt["ts"]
    k.op("vector", lambda e: e.tensor_copy(a_p[:, 0:16], pb[0][:, 0:16]),
         wait=[("ts", ev_ap_t)], inc="vs")
    bankfree[0] = [("vs", k.cnt["vs"])]
    k.op("vector", lambda e: e.tensor_scalar_mul(ccs[:, 0:16], a_p[:, 0:16], bsel[:, 0:1]),
         wait=[("scc", k.cnt["scc"]), ("vs", k.cnt["vs"])])
    k.op("vector", lambda e: e.tensor_scalar_mul(ccs[:, 16:32], a_p[:, 0:16], bsel[:, 1:2]),
         inc="vs")
    ev_ccs = k.cnt["vs"]
    k.dma("gpsimd", cc_in_d2.ap(), ccs[:, 0:32], wait=[("vs", ev_ccs)], sem="scc")
    ev_ccin = k.cnt["scc"]
    k.op("gpsimd", lambda e: e.collective_compute(
        "AllReduce", ALU.add, replica_groups=[list(range(8))],
        ins=[cc_in_d2.ap().opt()], outs=[cc_out_d2.ap().opt()]),
        wait=[("scc", ev_ccin)], inc="cc")
    k.dma("sync", ccs[:, 0:32], cc_out_d.ap(),
          wait=[("cc", 1), ("scc", ev_ccin)], sem="sccr")
    k.dma("sync", ccs2[:, 0:32], cc_out_d2.ap(), wait=[("cc", 2)], sem="sccr")
    ev_ccb = k.cnt["sccr"]
    k.op("vector", lambda e: e.tensor_tensor(ccs[:, 0:32], ccs[:, 0:32], ccs2[:, 0:32],
                                             ALU.add),
         wait=[("sccr", ev_ccb), ("vs", k.cnt["vs"])], inc="vs")
    k.op("vector", lambda e: e.tensor_scalar_mul(a_fin[:, 0:16], ccs[:, 0:16], bsel[:, 0:1]),
         wait=[("vs", k.cnt["vs"])], inc="vs")
    k.op("vector", lambda e: e.scalar_tensor_tensor(
        out=a_fin[:, 0:16], in0=ccs[:, 16:32], scalar=bsel[:, 1:2], in1=a_fin[:, 0:16],
        op0=ALU.mult, op1=ALU.add), wait=[("vs", k.cnt["vs"])], inc="vs")
    k.op("vector", lambda e: e.tensor_scalar(
        out=a_fin[:, 0:16], in0=a_fin[:, 0:16], scalar1=0.5, scalar2=0.5,
        op0=ALU.mult, op1=ALU.add), wait=[("vs", k.cnt["vs"])], inc="vs")
    ev_afin = k.cnt["vs"]
    if DEBUG:
        k.dma("gpsimd", dbg["dbg_ap"], a_p[:, 0:16], wait=[("vs", ev_afin)], sem="sdbg")
        k.dma("gpsimd", dbg["dbg_af"], a_fin[:, 0:16], sem="sdbg")

    fa.pop()
    ba.pop()
    k.barrier()
    wo_st = [fa.alloc(1024) for _ in range(16)]
    s1b = [fa.alloc(1024), fa.alloc(1024)]
    outb = [fa.alloc(1024), fa.alloc(1024)]
    woA_h = ba.alloc(16384)
    woA_l = ba.alloc(16384)
    qt_h = [ba.alloc(4096), ba.alloc(4096)]
    qt_l = [ba.alloc(4096), ba.alloc(4096)]

    ev_qt_used = {0: 0, 1: 0}
    ev_s1_used = {0: 0, 1: 0}
    ev_out_st = {0: 0, 1: 0}
    ld_qt_g = {}

    for half in range(2):
        hsl = slice(half * 1024, half * 1024 + 1024)
        qt_all = [("sqt0", k.cnt.get("sqt0", 0)), ("sqt1", k.cnt.get("sqt1", 0))]
        for g in range(2):
            gslot = g % 2
            k.dma("gpsimd", v3(qt_h[gslot][:, :], DC), qth[:, :, g * 256:(g + 1) * 256],
                  wait=qt_all + [("ts", ev_qt_used[gslot])], sem="spq%d" % gslot)
            k.dma("gpsimd", v3(qt_l[gslot][:, :], DC), qtl[:, :, g * 256:(g + 1) * 256],
                  sem="spq%d" % gslot)
            ld_qt_g[g] = k.cnt["spq%d" % gslot]
        ts_prev_half = k.cnt["ts"] if half else 0
        ev_wo_spl = {}
        for dc in range(DC):
            wsl = wo_st[dc]
            wl_ld = [("vs", ev_wo_spl0[dc])] if half else []
            ld = k.dma("scalar", wsl[:, :], w_out[dc * 128:dc * 128 + 128, hsl],
                       wait=wl_ld, sem="swo%d" % dc)
            k.op("scalar", lambda e, wsl=wsl, dc=dc: e.activation(
                woA_h[:, dc * 1024:(dc + 1) * 1024], wsl[:, :], AF.Identity,
                scale=a_fin[:, dc:dc + 1]),
                wait=[("swo%d" % dc, ld), ("vs", ev_afin), ("ts", ts_prev_half)],
                inc="ss")
            eh = k.cnt["ss"]
            k.op("vector", lambda e, wsl=wsl, dc=dc: e.scalar_tensor_tensor(
                out=woA_l[:, dc * 1024:(dc + 1) * 1024], in0=wsl[:, :],
                scalar=a_fin[:, dc:dc + 1], in1=woA_h[:, dc * 1024:(dc + 1) * 1024],
                op0=ALU.mult, op1=ALU.subtract),
                wait=[("ss", eh)], inc="vs")
            ev_wo_spl[dc] = k.cnt["vs"]
        ev_wo_spl0 = dict(ev_wo_spl)
        if DEBUG and half == 1:
            k.dma("gpsimd", dbg["dbg_woh"][:, :], woA_h[:, :],
                  wait=[("vs", k.cnt["vs"]), ("ss", k.cnt["ss"])])
            k.dma("gpsimd", dbg["dbg_wol"][:, :], woA_l[:, :], sem="sdbg")
            k.dma("gpsimd", dbg["dbg_af2"][:, :], a_fin[:, 0:16], sem="sdbg")

        for tt in range(16):
            tok0 = tt * 128
            g = tt // 2
            gslot = g % 2
            u = tt % 2
            slot = tt % 2
            s1_all = ([("ss10", k.cnt.get("ss10", 0)), ("ss11", k.cnt.get("ss11", 0))]
                      if (half == 0 and tt < 2) else [])
            k.dma("sync", s1b[slot][:, :], s1_d[tok0:tok0 + 128, hsl],
                  wait=s1_all + [("vs", ev_s1_used[slot])], sem="sps%d" % slot)
            ld_s1 = k.cnt["sps%d" % slot]
            for oc in range(2):
                bi = 2 * slot + oc
                bank = pb[bi]
                wl = bankfree[bi] + [("spq%d" % gslot, ld_qt_g[g])]
                nmm = 0
                for qc in range(DC):
                    lhh = qt_h[gslot][:, qc * 256 + u * 128: qc * 256 + u * 128 + 128]
                    lhl = qt_l[gslot][:, qc * 256 + u * 128: qc * 256 + u * 128 + 128]
                    rhh = woA_h[:, qc * 1024 + oc * 512: qc * 1024 + oc * 512 + 512]
                    rhl = woA_l[:, qc * 1024 + oc * 512: qc * 1024 + oc * 512 + 512]
                    for lw, rw in ((lhh, rhh), (lhh, rhl), (lhl, rhh)):
                        wlq = (wl if nmm == 0 else []) + (
                            [("vs", ev_wo_spl[qc])] if nmm % 3 == 0 else [])
                        k.op("tensor", lambda e, lw=lw, rw=rw, bank=bank, nmm=nmm: e.matmul(
                            bank[:, :], lw, rw, start=(nmm == 0), stop=(nmm == 47)),
                            wait=wlq,
                            inc="ts" if nmm == 47 else None)
                        nmm += 1
                ev_mm = k.cnt["ts"]
                k.op("vector", lambda e, bank=bank, slot=slot, oc=oc: e.tensor_tensor(
                    outb[slot][:, oc * 512:(oc + 1) * 512], bank[:, :],
                    s1b[slot][:, oc * 512:(oc + 1) * 512], ALU.add),
                    wait=[("ts", ev_mm), ("sps%d" % slot, ld_s1),
                          ("sy%d" % slot, ev_out_st[slot])], inc="vs")
                bankfree[bi] = [("vs", k.cnt["vs"])]
            if u == 1:
                ev_qt_used[gslot] = k.cnt["ts"]
                gn = g + 2
                if gn <= 7:
                    k.dma("gpsimd", v3(qt_h[gslot][:, :], DC),
                          qth[:, :, gn * 256:(gn + 1) * 256],
                          wait=[("ts", ev_qt_used[gslot])], sem="spq%d" % gslot)
                    k.dma("gpsimd", v3(qt_l[gslot][:, :], DC),
                          qtl[:, :, gn * 256:(gn + 1) * 256], sem="spq%d" % gslot)
                    ld_qt_g[gn] = k.cnt["spq%d" % gslot]
            ev_s1_used[slot] = k.cnt["vs"]
            k.dma("gpsimd", y_out[tok0:tok0 + 128, hsl], outb[slot][:, :],
                  wait=[("vs", k.cnt["vs"])], sem="sy%d" % slot)
            ev_out_st[slot] = k.cnt["sy%d" % slot]

    if DEBUG:
        k.dma("gpsimd", dbg["dbg_qh"][:, :], qth_d[0:D, :], wait=[("vs", k.cnt["vs"])], sem="sdbg")
        k.dma("gpsimd", dbg["dbg_ql"][:, :], qtl_d[0:D, :], sem="sdbg")
        k.dma("gpsimd", dbg["dbg_s1d"][:, :], s1_d[:, :], sem="sdbg")
        k.dma("gpsimd", dbg["dbg_x2h"][:, :], x2th_d[0:D, :], sem="sdbg")
    k.op("gpsimd", lambda e: e.wait_ge(k.semh["sy0"], k.cnt["sy0"]))
    k.op("gpsimd", lambda e: e.wait_ge(k.semh["sy1"], k.cnt["sy1"]))
    if DEBUG and "sdbg" in k.cnt:
        k.op("gpsimd", lambda e: e.wait_ge(k.semh["sdbg"], k.cnt["sdbg"]))
    return k.finish()


_NC_CACHE = None


def _get_nc():
    global _NC_CACHE
    if _NC_CACHE is None:
        _NC_CACHE = build_nc()
    return _NC_CACHE


def kernel(**inputs):
    X = np.asarray(inputs["L0_attn_X"], dtype=np.float32)
    w_in_raw = np.asarray(inputs["L0_attn_W_in"], np.float32)
    w_in = np.ascontiguousarray(
        w_in_raw.reshape(DC, 128, 12, 512).transpose(2, 1, 0, 3).reshape(12, 128, 8192))
    b_in = np.asarray(inputs["L0_attn_b_in"], np.float32).reshape(1, 3 * D)
    w_out = np.ascontiguousarray(np.asarray(inputs["L0_attn_W_out"], np.float32))
    b_out = np.asarray(inputs["L0_attn_b_out"], np.float32).reshape(1, D)
    gamma = np.asarray(inputs["L0_attn_gamma"], np.float32)
    beta = np.asarray(inputs["L0_attn_beta"], np.float32)
    w_ffn = np.asarray(inputs["L0_ffn_W"], np.float32)
    b_ffn = np.asarray(inputs["L0_ffn_b"], np.float32).reshape(1, D)
    lora_A = np.asarray(inputs["L0_ffn_lora_A"], np.float32)
    lora_B = np.asarray(inputs["L0_ffn_lora_B"], np.float32)
    gate_w = np.asarray(inputs["L0_ffn_gate_w"], np.float32)
    gate_b = np.asarray(inputs["L0_ffn_gate_b"], np.float32).reshape(1, 8)

    w_ffnT = np.ascontiguousarray(w_ffn.T)
    atall = np.ascontiguousarray(np.transpose(lora_A, (2, 0, 1)).reshape(D, 128))
    btall = np.ascontiguousarray(np.transpose(lora_B, (0, 2, 1)).reshape(128, D))
    gwT = np.ascontiguousarray(gate_w.T)
    gamma_part = np.ascontiguousarray(gamma.reshape(DC, 128).T)
    bq_part = np.ascontiguousarray(b_in[0, 0:D].reshape(DC, 128).T)
    ident = np.eye(128, dtype=np.float32)

    inv = 10000.0 ** (-np.arange(0, 1024, 2, dtype=np.float64) / 1024.0)
    in_maps = []
    for c in range(8):
        b = c // 4
        t0 = (c % 4) * TOK
        pos = np.arange(t0, t0 + TOK, dtype=np.float64)
        ang = pos[:, None] * inv[None, :]
        bsel_np = np.zeros((128, 2), np.float32)
        bsel_np[:, b] = 1.0
        in_maps.append({
            "x_in": np.ascontiguousarray(X[b, t0:t0 + TOK]),
            "w_in": w_in, "b_in": b_in, "w_out": w_out,
            "w_ffnT": w_ffnT, "gamma_part": gamma_part, "bq_part": bq_part,
            "gamma_row": gamma.reshape(1, D), "beta_row": beta.reshape(1, D),
            "b_ffn": b_ffn, "b_out_r": b_out,
            "atall": atall, "btall": btall, "gwT": gwT, "gb": gate_b,
            "cos_in": np.cos(ang).astype(np.float32),
            "sin_in": np.sin(ang).astype(np.float32),
            "ident_in": ident, "bsel_in": bsel_np,
        })
    nc = _get_nc()
    res = run_bass_kernel_spmd(nc, in_maps, core_ids=list(range(8)))
    out = np.empty((2, 8192, D), np.float32)
    for c in range(8):
        out[c // 4, (c % 4) * TOK:(c % 4 + 1) * TOK] = res.results[c]["y_out"]
    return out


# version 11
# speedup vs baseline: 1.0290x; 1.0290x over previous
import numpy as np
import concourse.bass as bass
import concourse.mybir as mybir
from concourse.bass_utils import run_bass_kernel_spmd

F32 = mybir.dt.float32
BF16 = mybir.dt.bfloat16
AF = mybir.ActivationFunctionType
ALU = mybir.AluOpType
AX = mybir.AxisListType.X

D = 2048
TOK = 2048
DC = 16
EPS = 1e-5
ENGS = ("tensor", "vector", "scalar", "sync", "gpsimd")

DEBUG = False


class KB:
    def __init__(self):
        self.nc = bass.Bass("TRN2", target_bir_lowering=False)
        self.plan = {e: [] for e in ENGS}
        self.cnt = {}
        self.semh = {}
        self.lastwait = {}
        self.pending = {e: [] for e in ENGS}

    def sems(self, *names):
        for n in names:
            self.semh[n] = self.nc.alloc_semaphore(n)
            self.cnt[n] = 0

    def barrier(self):
        snap = [(s, v) for s, v in self.cnt.items() if v > 0]
        for e in ENGS:
            self.pending[e].extend(snap)

    def op(self, eng, fn, wait=(), inc=None, n=1):
        todo = []
        allw = list(self.pending[eng]) + list(wait)
        self.pending[eng] = []
        for s, v in allw:
            if v > 0 and self.lastwait.get((eng, s), 0) < v:
                self.lastwait[(eng, s)] = v
                todo.append((s, v))
        if inc is not None:
            self.cnt[inc] += n
        semh = self.semh

        def run(e, todo=tuple(todo), fn=fn, inc=inc, n=n):
            for s, v in todo:
                e.wait_ge(semh[s], v)
            r = fn(e)
            if inc is not None:
                r.then_inc(semh[inc], n)
        self.plan[eng].append(run)
        return self.cnt[inc] if inc is not None else None

    SERIAL_SEMS = {"sm", "sbr", "scc", "swsp", "swis", "sdbg", "swfp", "swip",
                   "swo0", "swo1"}

    def dma(self, q, out, in_, wait=(), sem="sm"):
        if sem not in self.semh:
            self.sems(sem)
        if sem in self.SERIAL_SEMS and self.cnt[sem] > 0:
            wait = list(wait) + [(sem, self.cnt[sem])]
        return self.op(q, lambda e: e.dma_start(out=out, in_=in_), wait, sem, 16)

    def finish(self):
        with self.nc.Block() as block:
            for eng in ENGS:
                getattr(block, eng)(lambda e, eng=eng: [f(e) for f in self.plan[eng]])
        return self.nc


def v3(ap2d, c):
    return ap2d.rearrange("p (c u) -> p c u", c=c)


def build_nc():
    k = KB()
    nc = k.nc
    k.sems("ts", "vs", "ss", "cc", "sm")

    def din(name, shape):
        return nc.dram_tensor(name, shape, F32, kind="ExternalInput").ap()

    x_in = din("x_in", [TOK, D])
    w_in = din("w_in", [12, 128, 8192])
    b_in = din("b_in", [1, 3 * D])
    w_out = din("w_out", [D, D])
    w_ffnT = din("w_ffnT", [D, D])
    gamma_part = din("gamma_part", [128, DC])
    bq_part = din("bq_part", [128, DC])
    gamma_row = din("gamma_row", [1, D])
    beta_row = din("beta_row", [1, D])
    b_ffn = din("b_ffn", [1, D])
    b_out_r = din("b_out_r", [1, D])
    atall_in = din("atall", [D, 128])
    btall_in = din("btall", [128, D])
    gwT_in = din("gwT", [D, 8])
    gb_in = din("gb", [1, 8])
    cos_in = din("cos_in", [TOK, 512])
    sin_in = din("sin_in", [TOK, 512])
    ident_in = din("ident_in", [128, 128])
    bsel_in = din("bsel_in", [128, 2])

    y_out = nc.dram_tensor("y_out", [TOK, D], F32, kind="ExternalOutput").ap()
    dbg = {}
    if DEBUG:
        for nm, shape in [("dbg_xn", [TOK, D]), ("dbg_xp", [TOK, D]),
                          ("dbg_ap", [128, 16]), ("dbg_af", [128, 16]),
                          ("dbg_w8", [TOK, 8]), ("dbg_s1", [TOK, D]),
                          ("dbg_x0", [128, D]), ("dbg_st", [128, 32]),
                          ("dbg_s1d", [TOK, D]), ("dbg_af2", [128, 16])]:
            dbg[nm] = nc.dram_tensor(nm, shape, F32, kind="ExternalOutput").ap()
        for nm, shape in [("dbg_qh", [D, TOK]), ("dbg_ql", [D, TOK]),
                          ("dbg_x2h", [D, TOK]), ("dbg_woh", [128, 16384]),
                          ("dbg_wol", [128, 16384])]:
            dbg[nm] = nc.dram_tensor(nm, shape, BF16, kind="ExternalOutput").ap()

    wfh_d = nc.dram_tensor("wfh_d", [4, 128, 8192], BF16).ap()
    wfl_d = nc.dram_tensor("wfl_d", [4, 128, 8192], BF16).ap()
    wih_d = nc.dram_tensor("wih_d", [12, 128, 8192], BF16).ap()
    wil_d = nc.dram_tensor("wil_d", [12, 128, 8192], BF16).ap()
    x2th_d = nc.dram_tensor("x2th_d", [D, TOK], BF16).ap()
    x2tl_d = nc.dram_tensor("x2tl_d", [D, TOK], BF16).ap()
    qth_d = nc.dram_tensor("qth_d", [D, TOK], BF16).ap()
    qtl_d = nc.dram_tensor("qtl_d", [D, TOK], BF16).ap()
    s1_d = nc.dram_tensor("s1_d", [TOK, D], F32).ap()
    cc_in_d = nc.dram_tensor("cc_in_d", [128, 32], F32)
    cc_out_d = nc.dram_tensor("cc_out_d", [128, 32], F32)
    cc_in_d2 = nc.dram_tensor("cc_in_d2", [128, 32], F32)
    cc_out_d2 = nc.dram_tensor("cc_out_d2", [128, 32], F32)

    x2th = x2th_d.rearrange("(c p) t -> p c t", p=128)
    x2tl = x2tl_d.rearrange("(c p) t -> p c t", p=128)
    qth = qth_d.rearrange("(c p) t -> p c t", p=128)
    qtl = qtl_d.rearrange("(c p) t -> p c t", p=128)
    w_ffnT3 = w_ffnT.rearrange("(c p) o -> p c o", p=128)

    SB_TOP = 229344
    F32_COLS = 24064
    B16_COLS = 58288
    base = (SB_TOP - (F32_COLS * 4 + B16_COLS * 2)) // 64 * 64
    f32a = nc.alloc_sbuf_tensor_at("f32_arena", [128, F32_COLS], F32, offset=base).ap()
    b16a = nc.alloc_sbuf_tensor_at("b16_arena", [128, B16_COLS], BF16,
                                   offset=base + F32_COLS * 4).ap()

    class Arena:
        def __init__(self, ap, cols):
            self.ap, self.cols, self.off, self.marks = ap, cols, 0, []

        def alloc(self, n):
            n = (n + 15) // 16 * 16
            assert self.off + n <= self.cols, f"arena overflow {self.off}+{n}>{self.cols}"
            s = self.ap[:, self.off:self.off + n]
            self.off += n
            return s

        def push(self):
            self.marks.append(self.off)

        def pop(self):
            self.off = self.marks.pop()

    fa, ba = Arena(f32a, F32_COLS), Arena(b16a, B16_COLS)

    idt = fa.alloc(128)
    ones_row = fa.alloc(512)
    ones_col = fa.alloc(16)
    bsel = fa.alloc(16)
    gpart = fa.alloc(16)
    bqp = fa.alloc(16)
    a_p = fa.alloc(16)
    a_fin = fa.alloc(16)
    ccs = fa.alloc(32)
    ccs2 = fa.alloc(32)
    rowa = fa.alloc(512)
    rowb = fa.alloc(512)
    rowc = fa.alloc(512)
    fa.push()
    ba.push()

    pb = [nc.alloc_psum_tensor(f"pb{i}", [128, 512], F32).ap() for i in range(8)]
    bankfree = {i: [] for i in range(8)}

    b_sum = fa.alloc(1024)

    def bsum_row(c):
        p = [0, 32, 64, 0][c]
        off = 512 if c == 3 else 0
        return b_sum[p:p + 1, off:off + 512]
    bog_rep = fa.alloc(2048)

    k.dma("sync", idt[:, :], ident_in[:, :])
    k.dma("sync", bsel[:, 0:2], bsel_in[:, :])
    k.dma("sync", gpart[:, 0:DC], gamma_part[:, :])
    k.dma("sync", bqp[:, 0:DC], bq_part[:, :])
    k.op("vector", lambda e: e.memset(ones_row[:, :], 1.0))
    k.op("vector", lambda e: e.memset(st_[:, :], 0.0))
    k.op("vector", lambda e: e.memset(g_sc[:, :], 0.0))
    k.op("vector", lambda e: e.memset(ones_col[:, 0:1], 1.0), inc="vs")

    for j in range(4):
        sl = slice(j * 512, (j + 1) * 512)
        prev_v = k.cnt["vs"]
        k.dma("sync", rowa[0:1, :], gamma_row[0:1, sl], wait=[("vs", prev_v)])
        k.dma("sync", rowb[0:1, :], beta_row[0:1, sl])
        k.dma("sync", rowc[[0, 32, 64, 0][j]:[0, 32, 64, 0][j] + 1, :], b_ffn[0:1, sl])
        ldj = k.cnt["sm"]
        k.op("vector", lambda e: e.reciprocal(rowa[0:1, :], rowa[0:1, :]),
             wait=[("sm", ldj), ("vs", k.cnt["vs"])], inc="vs")
        k.op("vector", lambda e: e.tensor_mul(rowb[0:1, :], rowb[0:1, :], rowa[0:1, :]),
             wait=[("vs", k.cnt["vs"])], inc="vs")
        evd = k.cnt["vs"]
        k.op("tensor", lambda e: e.matmul(pb[7][:, :], ones_row[0:1, 0:128], rowb[0:1, :],
                                          start=True, stop=True),
             wait=[("vs", evd)] + bankfree[7], inc="ts")
        evt = k.cnt["ts"]
        k.op("vector", lambda e, sl=sl: e.tensor_copy(bog_rep[:, sl], pb[7][:, :]),
             wait=[("ts", evt)], inc="vs")
        bankfree[7] = [("vs", k.cnt["vs"])]
        pj_ = [0, 32, 64, 0][j]
        k.dma("sync", rowa[pj_:pj_ + 1, :], b_out_r[0:1, sl], wait=[("vs", evd)])
        ldj2 = k.cnt["sm"]
        k.op("vector", lambda e, j=j, pj_=pj_: e.tensor_tensor(
            bsum_row(j), rowa[pj_:pj_ + 1, :], rowc[pj_:pj_ + 1, :], ALU.add),
            wait=[("sm", ldj2)], inc="vs")
    ev_consts_v = k.cnt["vs"]

    xb = [fa.alloc(2048), fa.alloc(2048)]
    xn_t = fa.alloc(2048)
    zz = fa.alloc(2048)
    xxn = [fa.alloc(2048), fa.alloc(2048)]
    wsp = fa.alloc(4096)
    s1s = [fa.alloc(512), fa.alloc(512)]
    st_ = fa.alloc(32)
    xa_f2 = [fa.alloc(144), fa.alloc(144)]
    g_sc = fa.alloc(48)
    dbg_buf = fa.alloc(32) if DEBUG else None

    xnT_h = ba.alloc(4096)
    xnT_l = ba.alloc(4096)
    x2s_h = [ba.alloc(2048), ba.alloc(2048)]
    x2s_l = [ba.alloc(2048), ba.alloc(2048)]
    wf_h = [ba.alloc(8192), ba.alloc(8192)]
    wf_l = [ba.alloc(8192), ba.alloc(8192)]
    at_h = ba.alloc(2048)
    at_l = ba.alloc(2048)
    bt_h = ba.alloc(2048)
    bt_l = ba.alloc(2048)
    gw_h = ba.alloc(128)
    gw_l = ba.alloc(128)
    xas_h = [ba.alloc(128), ba.alloc(128)]
    xas_l = [ba.alloc(128), ba.alloc(128)]

    def small_split(dst_h, dst_l, ncols, load_fn):
        ld = load_fn()
        k.op("scalar", lambda e: e.activation(dst_h, wsp[:, 0:ncols], AF.Copy),
             wait=[("sm", ld)], inc="ss")
        eh = k.cnt["ss"]
        k.op("vector", lambda e: e.tensor_sub(dst_l, wsp[:, 0:ncols], dst_h),
             wait=[("ss", eh), ("sm", ld)], inc="vs")
        return k.cnt["vs"]

    ev_w = small_split(at_h[:, :], at_l[:, :], 2048,
                       lambda: k.dma("sync", v3(wsp[:, 0:2048], DC),
                                     atall_in.rearrange("(c p) r -> p c r", p=128)))
    ev_w = small_split(bt_h[:, :], bt_l[:, :], 2048,
                       lambda: k.dma("sync", wsp[:, 0:2048], btall_in[:, :],
                                     wait=[("vs", ev_w)]))
    ev_w = small_split(gw_h[:, :], gw_l[:, :], 128,
                       lambda: k.dma("sync", v3(wsp[:, 0:128], DC),
                                     gwT_in.rearrange("(c p) g -> p c g", p=128),
                                     wait=[("vs", ev_w)]))
    k.dma("sync", rowa[0:1, 0:8], gb_in[0:1, :])
    ld_gb = k.cnt["sm"]
    ev_wsp_free = ev_w

    ev_x_free = {}
    ev_xxn_free = {}
    ev_qg_x2s = {0: 0, 1: 0}
    ev_xn_tp = 0
    ev_zz_tp = 0
    ev_qg_x2 = 0
    ev_wf_used = {0: 0, 1: 0}
    ev_wf_persist = {}
    ev_s1_free = {0: 0, 1: 0}
    ev_xas_used = 0
    ev_xnT_consumed = 0

    S = dict(sum1=0, sq1=1, mu1=2, var1=3, r1=4, nmr1=5, r1p=6, sum2=7, sq2=8,
             mu2=9, var2=10, r2=11, l1=12, l2=13, d12=14, g1=15, g2=16, tmp=17, rc=18,
             r1p0=20, r1p1=21, nmr0=22, nmr1_1=23)

    def st(nm):
        i = S[nm]
        return st_[:, i:i + 1]

    def svop(fn, wait=()):
        return k.op("vector", fn, wait=list(wait) + [("vs", k.cnt["vs"])], inc="vs")

    RGT, REQ, RANK = g_sc[:, 0:8], g_sc[:, 8:16], g_sc[:, 16:24]
    M1, M2, W8 = g_sc[:, 24:32], g_sc[:, 32:40], g_sc[:, 40:48]

    def transpose_group(src, dcs, bi, dst_h3, dst_l3, wait_first, wait_evac):
        bank = pb[bi]
        for c, dc in enumerate(dcs):
            k.op("tensor", lambda e, c=c, dc=dc: e.matmul(
                bank[:, c * 128:(c + 1) * 128], src[:, dc * 128:(dc + 1) * 128],
                idt[:, :], start=(c == 0), stop=(c == 3), is_transpose=True),
                wait=(list(wait_first) + bankfree[bi]) if c == 0 else (),
                inc="ts" if c == 3 else None)
        evt = k.cnt["ts"]
        k.op("scalar", lambda e: e.activation(dst_h3, v3(bank[:, :], 4), AF.Copy),
             wait=[("ts", evt)] + list(wait_evac), inc="ss")
        ehh = k.cnt["ss"]
        k.op("vector", lambda e: e.tensor_sub(dst_l3, v3(bank[:, :], 4), dst_h3),
             wait=[("ts", evt), ("ss", ehh)] + list(wait_evac), inc="vs")
        bankfree[bi] = [("ss", ehh), ("vs", k.cnt["vs"])]

    ev_xnT_tile = {}
    pend_xas = {}
    ev_xas_consumed = {0: 0, 1: 0}
    TPB = [6, 7, 0, 1, 2, 3]
    tp_i = [0]

    def next_tpb():
        b = TPB[tp_i[0] % 6]
        tp_i[0] += 1
        return b
    ln_ev = {}
    ld_x = {}
    last_xxn_ss = [0]

    def pre_ldx(tt):
        t = tt % 2
        wl = ev_x_free.get(tt - 2, [("vs", ev_consts_v)])
        ld_x[tt] = k.dma("gpsimd", xb[t][:, :], x_in[tt * 128:tt * 128 + 128, :],
                         wait=wl, sem="sx%d" % t)

    def emit_ln_tile(tt):
        t = tt % 2
        xbuf = xb[t]
        if tt not in ld_x:
            pre_ldx(tt)
        sx = "sx%d" % (tt % 2)
        ldx = ld_x.pop(tt)
        svop(lambda e, xbuf=xbuf: e.reduce_sum(st("sum1"), xbuf[:, :], axis=AX),
             wait=[(sx, ldx), ("ss", last_xxn_ss[0])])
        k.op("scalar", lambda e, xbuf=xbuf: e.activation(
            wsp[:, 0:2048], xbuf[:, :], AF.Square, accum_out=st("sq1")),
            wait=[(sx, ldx), ("vs", ev_wsp_free)], inc="ss")
        ev_sq1 = k.cnt["ss"]
        svop(lambda e: e.tensor_single_scalar(st("mu1"), st("sum1"), 1.0 / D, ALU.mult))
        svop(lambda e: e.tensor_single_scalar(st("sq1"), st("sq1"), 1.0 / D, ALU.mult),
             wait=[("ss", ev_sq1)])
        svop(lambda e: e.tensor_mul(st("var1"), st("mu1"), st("mu1")))
        svop(lambda e: e.tensor_sub(st("var1"), st("sq1"), st("var1")))
        svop(lambda e: e.tensor_single_scalar(st("var1"), st("var1"), EPS, ALU.add))
        svop(lambda e: e.reciprocal(st("rc"), st("var1")))
        ev_v1 = k.cnt["vs"]
        k.op("scalar", lambda e: e.activation(st("r1"), st("rc"), AF.Sqrt),
             wait=[("vs", ev_v1)], inc="ss")
        ev_r1 = k.cnt["ss"]
        svop(lambda e: e.tensor_mul(st("nmr1"), st("mu1"), st("r1")),
             wait=[("ss", ev_r1)])
        svop(lambda e: e.tensor_single_scalar(st("nmr1"), st("nmr1"), -1.0, ALU.mult))
        svop(lambda e: e.tensor_single_scalar(st("r1p"), st("r1"), 1.0, ALU.add))
        ev_ln1 = k.cnt["vs"]
        k.op("scalar", lambda e, xbuf=xbuf: e.activation(
            xn_t[:, :], xbuf[:, :], AF.Identity, bias=st("nmr1"), scale=st("r1")),
            wait=[("vs", ev_ln1), ("ts", ev_xn_tp)], inc="ss")
        ev_xn = k.cnt["ss"]
        svop(lambda e: e.reduce_sum(st("sum2"), xn_t[:, :], axis=AX),
             wait=[("ss", ev_xn)])
        k.op("scalar", lambda e: e.activation(wsp[:, 0:2048], xn_t[:, :], AF.Square,
                                              accum_out=st("sq2")),
             wait=[("ss", ev_xn)], inc="ss")
        ev_sq2 = k.cnt["ss"]
        svop(lambda e: e.tensor_single_scalar(st("mu2"), st("sum2"), 1.0 / D, ALU.mult))
        svop(lambda e: e.tensor_single_scalar(st("sq2"), st("sq2"), 1.0 / D, ALU.mult),
             wait=[("ss", ev_sq2)])
        svop(lambda e: e.tensor_mul(st("var2"), st("mu2"), st("mu2")))
        svop(lambda e: e.tensor_sub(st("var2"), st("sq2"), st("var2")))
        svop(lambda e: e.tensor_single_scalar(st("var2"), st("var2"), EPS, ALU.add))
        svop(lambda e: e.reciprocal(st("rc"), st("var2")))
        ev_v2 = k.cnt["vs"]
        k.op("scalar", lambda e: e.activation(st("r2"), st("rc"), AF.Sqrt),
             wait=[("vs", ev_v2)], inc="ss")
        ev_r2 = k.cnt["ss"]
        k.op("vector", lambda e: e.tensor_scalar(
            out=zz[:, :], in0=xn_t[:, :], scalar1=st("mu2"), scalar2=st("r2"),
            op0=ALU.subtract, op1=ALU.mult),
            wait=[("ts", ev_zz_tp), ("ss", ev_r2)], inc="vs")
        k.op("vector", lambda e: e.tensor_tensor(zz[:, :], zz[:, :], bog_rep[:, :],
                                                 ALU.add),
             wait=[("vs", k.cnt["vs"])], inc="vs")
        ev_xp = k.cnt["vs"]
        ln_ev[tt] = dict(sx=sx, ldx=ldx, xn=ev_xn, xp=ev_xp)

    for sw in range(8):
        ev_tile_xas = {}
        for t in range(2):
            tt = sw * 2 + t
            tok0 = tt * 128
            xbuf = xb[t]
            if tt not in ln_ev:
                emit_ln_tile(tt)
            lev = ln_ev.pop(tt)
            sx, ldx, ev_xn, ev_xp = lev["sx"], lev["ldx"], lev["xn"], lev["xp"]
            k.op("scalar", lambda e, xbuf=xbuf, t=t: e.activation(
                xxn[t][:, :], xbuf[:, :], AF.Identity, bias=st("nmr1"), scale=st("r1p")),
                wait=[("vs", ev_xxn_free.get(tt - 2, 0))], inc="ss")
            ev_xxn_w = k.cnt["ss"]
            last_xxn_ss[0] = ev_xxn_w
            if DEBUG:
                if tt == 0:
                    k.dma("gpsimd", dbg["dbg_x0"], xbuf[:, :], wait=[(sx, ldx)], sem="sdbg")
                    k.op("vector", lambda e: e.tensor_copy(dbg_buf[:, 0:32], st_[:, 0:32]),
                         inc="vs")
                    k.dma("gpsimd", dbg["dbg_st"], dbg_buf[:, 0:32],
                          wait=[("vs", k.cnt["vs"])])
                k.dma("gpsimd", dbg["dbg_xn"][tok0:tok0 + 128, :], xn_t[:, :],
                      wait=[("ss", ev_xn)])
                k.dma("gpsimd", dbg["dbg_xp"][tok0:tok0 + 128, :], zz[:, :],
                      wait=[("vs", ev_xp)])

            xnT_h16 = v3(xnT_h[:, :], DC)
            xnT_l16 = v3(xnT_l[:, :], DC)
            for g in range(4):
                dsl = (slice(None), slice(4 * g, 4 * g + 4), slice(t * 128, t * 128 + 128))
                transpose_group(xn_t, [4 * g + c for c in range(4)], next_tpb(),
                                xnT_h16[dsl], xnT_l16[dsl],
                                [("ss", ev_xn)], [("ts", ev_xnT_consumed)])
            ev_xn_tp = k.cnt["ts"]
            ev_xnT = (k.cnt["ss"], k.cnt["vs"])
            ev_xnT_tile[tt] = [("ss", k.cnt["ss"]), ("vs", k.cnt["vs"])]

            xa_f = xa_f2[t]
            LOG = xa_f[:, 128:136]
            MSK = xa_f[:, 136:144]
            ev_xa_prev_used = ev_xas_used
            nmm = 0
            for dc in range(DC):
                lh = xnT_h[:, dc * 256 + t * 128: dc * 256 + t * 128 + 128]
                ll = xnT_l[:, dc * 256 + t * 128: dc * 256 + t * 128 + 128]
                for lw, rwa, rwg in ((lh, at_h, gw_h), (lh, at_l, gw_l), (ll, at_h, gw_h)):
                    wl0 = ([("ss", ev_xnT[0]), ("vs", ev_xnT[1]), ("sm", ld_gb)]
                           + bankfree[4]) if nmm == 0 else ()
                    k.op("tensor", lambda e, lw=lw, rw=rwa, dc=dc, nmm=nmm: e.matmul(
                        pb[4][:, 0:128], lw, rw[:, dc * 128:(dc + 1) * 128],
                        start=(nmm == 0), stop=False), wait=wl0)
                    k.op("tensor", lambda e, lw=lw, rw=rwg, dc=dc: e.matmul(
                        pb[4][:, 128:136], lw, rw[:, dc * 8:(dc + 1) * 8],
                        start=False, stop=False))
                    nmm += 1
            k.op("tensor", lambda e: e.matmul(pb[4][:, 128:136], ones_row[0:1, 0:128],
                                              rowa[0:1, 0:8], start=False, stop=True),
                 inc="ts")
            ev_xa_t = k.cnt["ts"]
            svop(lambda e, xa_f=xa_f: e.tensor_copy(xa_f[:, 0:136], pb[4][:, 0:136]),
                 wait=[("ts", ev_xa_t), ("ts", ev_xa_prev_used)])
            bankfree[4] = [("vs", k.cnt["vs"])]
            ev_evac = k.cnt["vs"]
            MSK2 = st_[:, 24:32]
            for e8 in range(8):
                svop(lambda e, e8=e8, LOG=LOG, MSK=MSK: e.tensor_scalar(
                    out=MSK, in0=LOG, scalar1=LOG[:, e8:e8 + 1], scalar2=None,
                    op0=ALU.is_gt, op1=ALU.add, accum_out=RGT[:, e8:e8 + 1]),
                    wait=[("vs", ev_evac)] if e8 == 0 else ())
            svop(lambda e: e.memset(REQ[:, 0:1], 0.0))
            for e8 in range(1, 8):
                svop(lambda e, e8=e8, LOG=LOG, MSK=MSK: e.tensor_scalar(
                    out=MSK[:, 0:e8], in0=LOG[:, 0:e8], scalar1=LOG[:, e8:e8 + 1],
                    scalar2=None, op0=ALU.is_equal, op1=ALU.add,
                    accum_out=REQ[:, e8:e8 + 1]))
            svop(lambda e: e.tensor_add(RANK, RGT, REQ))
            svop(lambda e: e.tensor_single_scalar(M1, RANK, 0.5, ALU.is_lt))
            svop(lambda e: e.tensor_single_scalar(M2, RANK, 1.0, ALU.is_equal))
            svop(lambda e, MSK=MSK, LOG=LOG: e.tensor_mul(MSK, M1, LOG))
            svop(lambda e, MSK2=MSK2, LOG=LOG: e.tensor_mul(MSK2, M2, LOG))
            svop(lambda e, MSK=MSK: e.reduce_sum(st("l1"), MSK, axis=AX))
            svop(lambda e, MSK2=MSK2: e.reduce_sum(st("l2"), MSK2, axis=AX))
            svop(lambda e: e.tensor_sub(st("d12"), st("l1"), st("l2")))
            ev_d12 = k.cnt["vs"]
            k.op("scalar", lambda e: e.activation(st("g1"), st("d12"), AF.Sigmoid),
                 wait=[("vs", ev_d12)], inc="ss")
            ev_g1 = k.cnt["ss"]
            svop(lambda e: e.tensor_scalar(
                out=st("g2"), in0=st("g1"), scalar1=-1.0, scalar2=1.0,
                op0=ALU.mult, op1=ALU.add), wait=[("ss", ev_g1)])
            svop(lambda e: e.tensor_scalar_mul(W8, M1, st("g1")))
            svop(lambda e: e.scalar_tensor_tensor(
                out=W8, in0=M2, scalar=st("g2"), in1=W8, op0=ALU.mult, op1=ALU.add))
            ev_w8 = k.cnt["vs"]
            if DEBUG:
                k.dma("gpsimd", dbg["dbg_w8"][tok0:tok0 + 128, :], W8,
                      wait=[("vs", ev_w8)])
            for e8 in range(8):
                k.op("vector", lambda e, e8=e8, xa_f=xa_f: e.tensor_scalar_mul(
                    xa_f[:, e8 * 16:(e8 + 1) * 16], xa_f[:, e8 * 16:(e8 + 1) * 16],
                    W8[:, e8:e8 + 1]), wait=[("vs", ev_w8)],
                    inc="vs" if e8 == 7 else None)
            svop(lambda e: e.tensor_copy(st("tmp"), st("tmp")))
            pend_xas[t] = (xa_f, k.cnt["vs"])
            ev_x_free[tt] = [("vs", k.cnt["vs"]), ("ss", ev_xxn_w)]
            if tt + 2 < 16:
                pre_ldx(tt + 2)

            xsl = tt % 2
            for g in range(4):
                transpose_group(zz, [4 * g + c for c in range(4)], next_tpb(),
                                v3(x2s_h[xsl][:, g * 512:(g + 1) * 512], 4),
                                v3(x2s_l[xsl][:, g * 512:(g + 1) * 512], 4),
                                [("vs", ev_xp)], [("sx2%d" % xsl, ev_qg_x2s[xsl])])
            ev_zz_tp = k.cnt["ts"]
            ev_x2s = (k.cnt["ss"], k.cnt["vs"])
            k.dma("gpsimd", x2th[:, :, tok0:tok0 + 128], v3(x2s_h[xsl][:, :], DC),
                  wait=[("ss", ev_x2s[0]), ("vs", ev_x2s[1])], sem="sx2%d" % xsl)
            k.dma("gpsimd", x2tl[:, :, tok0:tok0 + 128], v3(x2s_l[xsl][:, :], DC),
                  sem="sx2%d" % xsl)
            ev_qg_x2s[xsl] = k.cnt["sx2%d" % xsl]

        for t in range(2):
            tt = sw * 2 + t
            xa_f, ev_xas_v = pend_xas[t]
            k.op("tensor", lambda e, xa_f=xa_f: e.matmul(
                pb[5][:, 0:128], xa_f[:, 0:128], idt[:, :],
                start=True, stop=True, is_transpose=True),
                wait=[("vs", ev_xas_v)] + bankfree[5], inc="ts")
            ev_xas_used = k.cnt["ts"]
            k.op("scalar", lambda e, t=t: e.activation(xas_h[t][:, :], pb[5][:, 0:128], AF.Copy),
                 wait=[("ts", ev_xas_used), ("ts", ev_xas_consumed[t])], inc="ss")
            eh = k.cnt["ss"]
            k.op("vector", lambda e, t=t: e.tensor_sub(xas_l[t][:, :], pb[5][:, 0:128],
                                                       xas_h[t][:, :]),
                 wait=[("ts", ev_xas_used), ("ss", eh)], inc="vs")
            bankfree[5] = [("vs", k.cnt["vs"])]
            ev_tile_xas[tt] = [("ss", eh), ("vs", k.cnt["vs"])]

        if 1 <= sw < 7:
            emit_ln_tile(2 * (sw + 1))

        for o in range(4):
            slot = o % 2
            if sw == 0:
                for h in range(2):
                    osl = slice(o * 512 + h * 256, o * 512 + h * 256 + 256)
                    ld = k.dma("sync", v3(wsp[:, 0:4096], DC), w_ffnT3[:, :, osl],
                               wait=[("vs", k.cnt["vs"]), ("ts", ev_wf_used[slot]),
                                     ("swfp", k.cnt.get("swfp", 0))], sem="swsp")
                    dst_h = v3(wf_h[slot][:, :], DC)[:, :, h * 256:h * 256 + 256]
                    dst_l = v3(wf_l[slot][:, :], DC)[:, :, h * 256:h * 256 + 256]
                    k.op("scalar", lambda e, dst_h=dst_h: e.activation(
                        dst_h, v3(wsp[:, 0:4096], DC), AF.Copy), wait=[("swsp", ld)], inc="ss")
                    eh = k.cnt["ss"]
                    k.op("vector", lambda e, dst_l=dst_l, dst_h=dst_h: e.tensor_sub(
                        dst_l, v3(wsp[:, 0:4096], DC), dst_h), wait=[("ss", eh)], inc="vs")
                k.dma("gpsimd", wfh_d[o], wf_h[slot][:, :],
                      wait=[("ss", k.cnt["ss"])], sem="swfp")
                k.dma("gpsimd", wfl_d[o], wf_l[slot][:, :],
                      wait=[("vs", k.cnt["vs"])], sem="swfp")
                ev_wf_persist[o] = k.cnt["swfp"]
                ev_wf_ld = [("vs", k.cnt["vs"]), ("ss", k.cnt["ss"])]
                ev_wsp_free = k.cnt["vs"]
            else:
                k.dma("sync", wf_h[slot][:, :], wfh_d[o],
                      wait=[("ts", ev_wf_used[slot]), ("swfp", k.cnt["swfp"])],
                      sem="swf%d" % slot)
                k.dma("scalar", wf_l[slot][:, :], wfl_d[o],
                      wait=[("ts", ev_wf_used[slot]), ("swfp", k.cnt["swfp"])],
                      sem="swg%d" % slot)
                ev_wf_ld = [("swf%d" % slot, k.cnt["swf%d" % slot]),
                            ("swg%d" % slot, k.cnt["swg%d" % slot])]
            for t in range(2):
                tt = sw * 2 + t
                tok0 = tt * 128
                bi = (2 * o + t) % 4
                bank = pb[bi]
                bsl = slice(o * 512, o * 512 + 512)
                wl = (ev_wf_ld + bankfree[bi] + ev_xnT_tile[tt])
                po = [0, 32, 64, 0][o]
                k.op("tensor", lambda e, bank=bank, o=o, po=po: e.matmul(
                    bank[:, :], ones_row[po:po + 1, 0:128], bsum_row(o),
                    start=True, stop=False), wait=wl)
                for dc in range(DC):
                    lh = xnT_h[:, dc * 256 + t * 128: dc * 256 + t * 128 + 128]
                    ll = xnT_l[:, dc * 256 + t * 128: dc * 256 + t * 128 + 128]
                    rh = wf_h[slot][:, dc * 512:(dc + 1) * 512]
                    rl = wf_l[slot][:, dc * 512:(dc + 1) * 512]
                    for lw, rw in ((lh, rh), (lh, rl), (ll, rh)):
                        k.op("tensor", lambda e, lw=lw, rw=rw, bank=bank: e.matmul(
                            bank[:, :], lw, rw, start=False, stop=False))
                lora_ops = ((xas_h[t], bt_h), (xas_h[t], bt_l), (xas_l[t], bt_h))
                for li, (lw, rw) in enumerate(lora_ops):
                    k.op("tensor", lambda e, lw=lw, rw=rw, bank=bank, bsl=bsl, li=li: e.matmul(
                        bank[:, :], lw[:, :], rw[:, bsl], start=False, stop=(li == 2)),
                        wait=ev_tile_xas[tt] if li == 0 else (),
                        inc="ts" if li == 2 else None)
                ev_b = k.cnt["ts"]
                ev_xas_consumed[t] = ev_b
                if t == 1:
                    ev_wf_used[slot] = ev_b
                s1slot = t
                k.op("vector", lambda e, bank=bank, t=t, bsl=bsl, s1slot=s1slot: e.tensor_tensor(
                    s1s[s1slot][:, :], bank[:, :], xxn[t][:, bsl], ALU.add),
                    wait=[("ts", ev_b), ("ss1%d" % s1slot, ev_s1_free[s1slot])], inc="vs")
                bankfree[bi] = [("vs", k.cnt["vs"])]
                ev_s1v = k.cnt["vs"]
                k.dma("gpsimd", s1_d[tok0:tok0 + 128, bsl], s1s[s1slot][:, :],
                      wait=[("vs", ev_s1v)], sem="ss1%d" % s1slot)
                ev_s1_free[s1slot] = k.cnt["ss1%d" % s1slot]
                if DEBUG:
                    k.dma("gpsimd", dbg["dbg_s1"][tok0:tok0 + 128, bsl], s1s[s1slot][:, :],
                          sem="ss1%d" % s1slot)
                    ev_s1_free[s1slot] = k.cnt["ss1%d" % s1slot]
                if o == 3:
                    ev_xxn_free[tt] = ev_s1v
        ev_xnT_consumed = k.cnt["ts"]

    ev_1a_end_x2 = [("sx2%d" % s, k.cnt.get("sx2%d" % s, 0)) for s in range(2)]

    fa.pop()
    ba.pop()
    fa.push()
    ba.push()
    k.barrier()

    wi_stage = fa.alloc(4096)
    P4 = [fa.alloc(2048) for _ in range(4)]
    kvs = fa.alloc(2560)
    ks_, vsb = kvs[:, 0:512], kvs[:, 512:1024]
    krot = kvs[:, 1024:1536]
    m4 = kvs[:, 1536:2560]
    ssc = m4[:, 0:512]
    cs_b = [fa.alloc(1024) for _ in range(4)]
    nsq = fa.alloc(48)
    a_acc = fa.alloc(2048)
    sqw = fa.alloc(512)
    k.op("vector", lambda e: e.memset(a_acc[:, :], 0.0), inc="vs")

    x2q_h = ba.alloc(8192)
    x2q_l = ba.alloc(8192)
    wi_h = [ba.alloc(8192), ba.alloc(8192)]
    wi_l = [ba.alloc(8192), ba.alloc(8192)]
    qt_sh = [ba.alloc(512), ba.alloc(512)]
    qt_sl = [ba.alloc(512), ba.alloc(512)]

    CHUNK_ORDER = [0, 1, 2, 3, 4, 8, 5, 9, 6, 10, 7, 11]
    ev_wi_used = {0: 0, 1: 0}
    ev_wi_persist = {}
    ev_wi_persist_all = 0
    ev_qt_free = {0: 0, 1: 0}
    ev_x2q_used = 0
    ev_cs_free = {0: 0, 1: 0, 2: 0, 3: 0}
    ev_aacc_v = ev_consts_v
    ev_P_free = {0: 0, 1: 0, 2: 0, 3: 0}
    next_q_wl = {}
    ldb_k = {}
    ldb_v = {}

    for q in range(4):
        qtok = q * 512
        k.dma("sync", v3(x2q_h[:, :], DC), x2th[:, :, qtok:qtok + 512],
              wait=ev_1a_end_x2 + [("ts", ev_x2q_used)], sem="sx2qh")
        k.dma("gpsimd", v3(x2q_l[:, :], DC), x2tl[:, :, qtok:qtok + 512],
              wait=[("ts", ev_x2q_used)], sem="sx2ql")
        ev_x2q_w = [("sx2qh", k.cnt["sx2qh"]), ("sx2ql", k.cnt["sx2ql"])]
        if q == 0:
            k.dma("scalar", rowb[0:1, :], b_in[0:1, 4 * 512:4 * 512 + 512],
                  wait=[("vs", ev_consts_v)], sem="sbr")
            ldb_k[0] = k.cnt["sbr"]
            k.dma("scalar", rowc[0:1, :], b_in[0:1, 8 * 512:8 * 512 + 512], sem="sbr")
            ldb_v[0] = k.cnt["sbr"]

        def load_chunk(n, q=q):
            j = CHUNK_ORDER[n % 12]
            slot = n % 2
            if q == 0 and n < 12:
                for h in range(2):
                    ld = k.dma("sync", wi_stage[:, 0:4096],
                               w_in[j][:, h * 4096:h * 4096 + 4096],
                               wait=[("vs", k.cnt["vs"]), ("ss", k.cnt["ss"]),
                                     ("ts", ev_wi_used[slot]),
                                     ("swip", k.cnt.get("swip", 0))], sem="swis")
                    for dcl in range(8):
                        dc = h * 8 + dcl
                        seg = wi_stage[:, dcl * 512:(dcl + 1) * 512]
                        dh = wi_h[slot][:, dc * 512:(dc + 1) * 512]
                        k.op("scalar", lambda e, seg=seg, dh=dh, dc=dc: e.activation(
                            dh, seg, AF.Identity, scale=gpart[:, dc:dc + 1]),
                            wait=[("swis", ld)] if dcl == 0 else (),
                            inc="ss" if dcl == 7 else None)
                    evh = k.cnt["ss"]
                    for dcl in range(8):
                        dc = h * 8 + dcl
                        seg = wi_stage[:, dcl * 512:(dcl + 1) * 512]
                        dh = wi_h[slot][:, dc * 512:(dc + 1) * 512]
                        dl = wi_l[slot][:, dc * 512:(dc + 1) * 512]
                        k.op("vector", lambda e, seg=seg, dh=dh, dl=dl, dc=dc: e.scalar_tensor_tensor(
                            out=dl, in0=seg, scalar=gpart[:, dc:dc + 1], in1=dh,
                            op0=ALU.mult, op1=ALU.subtract),
                            wait=[("ss", evh)] if dcl == 0 else (),
                            inc="vs" if dcl == 7 else None)
                k.dma("gpsimd", wih_d[j], wi_h[slot][:, :], wait=[("ss", k.cnt["ss"])],
                      sem="swip")
                k.dma("gpsimd", wil_d[j], wi_l[slot][:, :], wait=[("vs", k.cnt["vs"])],
                      sem="swip")
                ev_wi_persist[n] = k.cnt["swip"]
                return [("vs", k.cnt["vs"]), ("ss", k.cnt["ss"])]
            else:
                pw = k.cnt["swip"] if (q == 0) else ev_wi_persist_all
                k.dma("sync", wi_h[slot][:, :], wih_d[j],
                      wait=[("ts", ev_wi_used[slot]), ("swip", pw)],
                      sem="swi%d" % slot)
                k.dma("scalar", wi_l[slot][:, :], wil_d[j],
                      wait=[("ts", ev_wi_used[slot]), ("swip", pw)],
                      sem="swl%d" % slot)
                return [("swi%d" % slot, k.cnt["swi%d" % slot]),
                        ("swl%d" % slot, k.cnt["swl%d" % slot])]

        if q == 0:
            ev_wl = {0: load_chunk(0), 1: load_chunk(1)}
        else:
            ev_wl = dict(next_q_wl)
        nload = 2

        for wj in range(4):
            slot = wj % 2
            wlc = ev_wl[slot]
            for qfl in range(4):
                qf = 4 * wj + qfl
                bi = qf % 2
                bank = pb[bi]
                wl = list(wlc) + bankfree[bi] + ev_x2q_w
                nmm = 0
                for dc in range(DC):
                    lhh = wi_h[slot][:, dc * 512 + qfl * 128: dc * 512 + qfl * 128 + 128]
                    lhl = wi_l[slot][:, dc * 512 + qfl * 128: dc * 512 + qfl * 128 + 128]
                    rhh = x2q_h[:, dc * 512:(dc + 1) * 512]
                    rhl = x2q_l[:, dc * 512:(dc + 1) * 512]
                    for lw, rw in ((lhh, rhh), (lhh, rhl), (lhl, rhh)):
                        nmm += 1
                        k.op("tensor", lambda e, lw=lw, rw=rw, bank=bank, nmm=nmm: e.matmul(
                            bank[:, :], lw, rw, start=(nmm == 1), stop=(nmm == 48)),
                            wait=wl if nmm == 1 else (),
                            inc="ts" if nmm == 48 else None)
                ev_qmm = k.cnt["ts"]
                if qfl == 3:
                    ev_wi_used[slot] = ev_qmm
                    if nload < 12:
                        ev_wl[nload % 2] = load_chunk(nload)
                        nload += 1
                qs_ = qf % 2
                k.op("scalar", lambda e, bank=bank, qs_=qs_, qf=qf: e.activation(
                    qt_sh[qs_][:, :], bank[:, :], AF.Identity,
                    bias=bqp[:, qf:qf + 1], scale=1.0),
                    wait=[("ts", ev_qmm), ("sqt%d" % qs_, ev_qt_free[qs_])], inc="ss")
                eh = k.cnt["ss"]
                k.op("vector", lambda e, bank=bank, qs_=qs_, qf=qf: e.scalar_tensor_tensor(
                    out=qt_sl[qs_][:, :], in0=bank[:, :], scalar=bqp[:, qf:qf + 1],
                    in1=qt_sh[qs_][:, :], op0=ALU.add, op1=ALU.subtract),
                    wait=[("ts", ev_qmm), ("ss", eh), ("sqt%d" % qs_, ev_qt_free[qs_])],
                    inc="vs")
                bankfree[bi] = [("vs", k.cnt["vs"])]
                k.dma("gpsimd", qth[:, qf, qtok:qtok + 512], qt_sh[qs_][:, :],
                      wait=[("ss", eh), ("vs", k.cnt["vs"])], sem="sqt%d" % qs_)
                k.dma("gpsimd", qtl[:, qf, qtok:qtok + 512], qt_sl[qs_][:, :],
                      sem="sqt%d" % qs_)
                ev_qt_free[qs_] = k.cnt["sqt%d" % qs_]

        kv_banks = [2, 3, 4, 5, 6, 7]
        kvb_i = 0
        for i in range(4):
            tok0 = qtok + i * 128
            k.dma("gpsimd", cs_b[i][:, 0:512], cos_in[tok0:tok0 + 128, :],
                  wait=[("vs", ev_cs_free[i])], sem="scs%d" % i)
            k.dma("gpsimd", cs_b[i][:, 512:1024], sin_in[tok0:tok0 + 128, :],
                  sem="scs%d" % i)
            ev_cs_free[("ld", i)] = k.cnt["scs%d" % i]
        kbank = {}
        for pj in range(4):
            nk_idx, nv_idx = 4 + 2 * pj, 5 + 2 * pj
            slot_kc, slot_vc = nk_idx % 2, nv_idx % 2
            jk, jv = CHUNK_ORDER[nk_idx], CHUNK_ORDER[nv_idx]
            if q == 0:
                rw_k = [("sbr", ldb_k[pj])]
                rw_v = [("sbr", ldb_v[pj])]
                browk, browv = rowb, rowc
            else:
                rw_k = rw_v = [("sm", ev_kvrows)]
                browk = wi_stage[0:1, (jk - 4) * 512:(jk - 3) * 512]
                browv = wi_stage[0:1, (jv - 4) * 512:(jv - 3) * 512]
            ev_K = {}
            for i in range(4):
                bk = kv_banks[kvb_i % 6]; kvb_i += 1
                kbank[i] = bk
                bank = pb[bk]
                wl = list(bankfree[bk]) + rw_k
                if i == 0:
                    wl = wl + list(ev_wl[slot_kc]) + ev_x2q_w
                k.op("tensor", lambda e, bank=bank, brow=browk: e.matmul(
                    bank[:, :], ones_row[0:1, 0:128], brow[0:1, :],
                    start=True, stop=False), wait=wl)
                nmm = 0
                for dc in range(DC):
                    lhh = x2q_h[:, dc * 512 + i * 128: dc * 512 + i * 128 + 128]
                    lhl = x2q_l[:, dc * 512 + i * 128: dc * 512 + i * 128 + 128]
                    rhh = wi_h[slot_kc][:, dc * 512:(dc + 1) * 512]
                    rhl = wi_l[slot_kc][:, dc * 512:(dc + 1) * 512]
                    for lw, rw in ((lhh, rhh), (lhh, rhl), (lhl, rhh)):
                        nmm += 1
                        k.op("tensor", lambda e, lw=lw, rw=rw, bank=bank, nmm=nmm: e.matmul(
                            bank[:, :], lw, rw, start=False, stop=(nmm == 48)),
                            inc="ts" if nmm == 48 else None)
                ev_K[i] = k.cnt["ts"]
            ev_wi_used[slot_kc] = ev_K[3]
            if q == 0 and pj < 3:
                jk2 = CHUNK_ORDER[nk_idx + 2]
                k.dma("scalar", rowb[0:1, :], b_in[0:1, jk2 * 512:jk2 * 512 + 512],
                      wait=[("ts", ev_K[3])], sem="sbr")
                ldb_k[pj + 1] = k.cnt["sbr"]
            if nload < 12:
                ev_wl[nload % 2] = load_chunk(nload)
                nload += 1
            elif q < 3:
                next_q_wl[0] = load_chunk(12)
            ev_V = {}
            for i in range(4):
                bv = kv_banks[kvb_i % 6]; kvb_i += 1
                bank = pb[bv]
                wl = list(bankfree[bv]) + rw_v
                if i == 0:
                    wl = wl + list(ev_wl[slot_vc]) + ev_x2q_w
                k.op("tensor", lambda e, bank=bank, brow=browv: e.matmul(
                    bank[:, :], ones_row[0:1, 0:128], brow[0:1, :],
                    start=True, stop=False), wait=wl)
                nmm = 0
                for dc in range(DC):
                    lhh = x2q_h[:, dc * 512 + i * 128: dc * 512 + i * 128 + 128]
                    lhl = x2q_l[:, dc * 512 + i * 128: dc * 512 + i * 128 + 128]
                    rhh = wi_h[slot_vc][:, dc * 512:(dc + 1) * 512]
                    rhl = wi_l[slot_vc][:, dc * 512:(dc + 1) * 512]
                    for lw, rw in ((lhh, rhh), (lhh, rhl), (lhl, rhh)):
                        nmm += 1
                        k.op("tensor", lambda e, lw=lw, rw=rw, bank=bank, nmm=nmm: e.matmul(
                            bank[:, :], lw, rw, start=False, stop=(nmm == 48)),
                            inc="ts" if nmm == 48 else None)
                ev_V[i] = k.cnt["ts"]
                pk, pv = pb[kbank[i]], pb[bv]
                k.op("vector", lambda e, pk=pk: e.tensor_copy(ks_[:, :], pk[:, :]),
                     wait=[("ts", ev_K[i]), ("vs", k.cnt["vs"])], inc="vs")
                ev_cpk = k.cnt["vs"]
                bankfree[kbank[i]] = [("vs", ev_cpk)]
                k.op("scalar", lambda e, i=i, pj=pj: e.activation(
                    sqw[:, :], ks_[:, :], AF.Square,
                    accum_out=nsq[:, i * 4 + pj:i * 4 + pj + 1]),
                    wait=[("vs", ev_cpk)], inc="ss")
                ev_sqK = k.cnt["ss"]
                psl = slice(pj * 512, pj * 512 + 512)
                if pj < 2:
                    cth = cs_b[i][:, pj * 256:pj * 256 + 256]
                    sth = cs_b[i][:, 512 + pj * 256:512 + pj * 256 + 256]
                    ke, ko = ks_[:, 0:512:2], ks_[:, 1:512:2]
                    ve, vo = vsb[:, 0:512:2], vsb[:, 1:512:2]
                    re_, ro_ = krot[:, 0:512:2], krot[:, 1:512:2]
                    ma, mb_ = m4[:, 0:256], m4[:, 256:512]
                    mc, md = m4[:, 512:768], m4[:, 768:1024]
                    k.op("vector", lambda e, ke=ke, cth=cth: e.tensor_mul(ma, ke, cth),
                         wait=[("scs%d" % i, ev_cs_free[("ld", i)]),
                               ("vs", k.cnt["vs"])])
                    k.op("vector", lambda e, ko=ko, sth=sth: e.tensor_mul(mb_, ko, sth))
                    k.op("vector", lambda e, ko=ko, cth=cth: e.tensor_mul(mc, ko, cth))
                    k.op("vector", lambda e, ke=ke, sth=sth: e.tensor_mul(md, ke, sth),
                         inc="vs")
                    k.op("vector", lambda e, re_=re_: e.tensor_sub(re_, ma, mb_),
                         wait=[("vs", k.cnt["vs"])])
                    k.op("vector", lambda e, ro_=ro_: e.tensor_add(ro_, mc, md), inc="vs")
                    k.op("vector", lambda e, pv=pv: e.tensor_copy(vsb[:, :], pv[:, :]),
                         wait=[("ts", ev_V[i]), ("vs", k.cnt["vs"])], inc="vs")
                    ev_cpv = k.cnt["vs"]
                    bankfree[bv] = [("vs", ev_cpv)]
                    k.op("scalar", lambda e, i=i, pj=pj, psl=psl: e.activation(
                        P4[i][:, psl], vsb[:, :], AF.Square,
                        accum_out=nsq[:, 16 + i * 4 + pj:17 + i * 4 + pj]),
                        wait=[("vs", ev_cpv), ("vs", ev_P_free[i])], inc="ss")
                    ev_sqV = k.cnt["ss"]
                    k.op("vector", lambda e, ve=ve, cth=cth: e.tensor_mul(ma, ve, cth),
                         wait=[("ss", ev_sqK), ("vs", k.cnt["vs"])])
                    k.op("vector", lambda e, vo=vo, sth=sth: e.tensor_mul(mb_, vo, sth))
                    k.op("vector", lambda e, vo=vo, cth=cth: e.tensor_mul(mc, vo, cth))
                    k.op("vector", lambda e, ve=ve, sth=sth: e.tensor_mul(md, ve, sth),
                         inc="vs")
                    k.op("vector", lambda e, ke=ke: e.tensor_sub(ke, ma, mb_),
                         wait=[("vs", k.cnt["vs"])])
                    k.op("vector", lambda e, ko=ko: e.tensor_add(ko, mc, md), inc="vs")
                    k.op("vector", lambda e, i=i, psl=psl: e.tensor_mul(
                        P4[i][:, psl], krot[:, :], ks_[:, :]),
                        wait=[("vs", k.cnt["vs"]), ("ss", ev_sqV)], inc="vs")
                    if pj == 1:
                        ev_cs_free[i] = k.cnt["vs"]
                else:
                    k.op("vector", lambda e, pv=pv: e.tensor_copy(vsb[:, :], pv[:, :]),
                         wait=[("ts", ev_V[i]), ("vs", k.cnt["vs"])], inc="vs")
                    ev_cpv = k.cnt["vs"]
                    bankfree[bv] = [("vs", ev_cpv)]
                    k.op("scalar", lambda e, i=i, pj=pj, psl=psl: e.activation(
                        P4[i][:, psl], vsb[:, :], AF.Square,
                        accum_out=nsq[:, 16 + i * 4 + pj:17 + i * 4 + pj]),
                        wait=[("vs", ev_cpv), ("vs", ev_P_free[i])], inc="ss")
                    ev_sqV = k.cnt["ss"]
                    k.op("vector", lambda e, i=i, psl=psl: e.tensor_mul(
                        P4[i][:, psl], ks_[:, :], vsb[:, :]),
                        wait=[("ss", ev_sqV), ("vs", k.cnt["vs"])], inc="vs")
                if pj == 3:
                    k.op("vector", lambda e, i=i: e.reduce_sum(
                        nsq[:, 32:33], nsq[:, i * 4:i * 4 + 4], axis=AX),
                        wait=[("ss", ev_sqV), ("vs", k.cnt["vs"])], inc="vs")
                    svop(lambda e, i=i: e.reduce_sum(
                        nsq[:, 33:34], nsq[:, 16 + i * 4:16 + i * 4 + 4], axis=AX))
                    svop(lambda e: e.tensor_mul(nsq[:, 34:35], nsq[:, 32:33],
                                                nsq[:, 33:34]))
                    svop(lambda e: e.reciprocal(nsq[:, 36:37], nsq[:, 34:35]))
                    ev_np = k.cnt["vs"]
                    k.op("scalar", lambda e: e.activation(nsq[:, 35:36], nsq[:, 36:37],
                                                          AF.Sqrt),
                         wait=[("vs", ev_np)], inc="ss")
                    ev_st2 = k.cnt["ss"]
                    k.op("vector", lambda e, i=i: e.scalar_tensor_tensor(
                        out=a_acc[:, :], in0=P4[i][:, :], scalar=nsq[:, 35:36],
                        in1=a_acc[:, :], op0=ALU.mult, op1=ALU.add),
                        wait=[("ss", ev_st2)], inc="vs")
                    ev_aacc_v = k.cnt["vs"]
                    ev_P_free[i] = ev_aacc_v
            ev_wi_used[slot_vc] = ev_V[3]
            if q == 0 and pj < 3:
                jv2 = CHUNK_ORDER[nv_idx + 2]
                k.dma("scalar", rowc[0:1, :], b_in[0:1, jv2 * 512:jv2 * 512 + 512],
                      wait=[("ts", ev_V[3])], sem="sbr")
                ldb_v[pj + 1] = k.cnt["sbr"]
            if nload < 12:
                ev_wl[nload % 2] = load_chunk(nload)
                nload += 1
            elif q < 3:
                next_q_wl[1] = load_chunk(13)
        ev_x2q_used = k.cnt["ts"]
        if q == 2:
            for c in range(DC):
                k.op("tensor", lambda e, c=c: e.matmul(
                    pb[0][:, c:c + 1], a_acc[:, c * 128:(c + 1) * 128], ones_col[:, 0:1],
                    start=(c == 0), stop=(c == DC - 1)),
                    wait=([("vs", ev_aacc_v)] + bankfree[0]) if c == 0 else (),
                    inc="ts" if c == DC - 1 else None)
            ev_ap1 = k.cnt["ts"]
            k.op("vector", lambda e: e.tensor_copy(a_p[:, 0:16], pb[0][:, 0:16]),
                 wait=[("ts", ev_ap1), ("vs", k.cnt["vs"])], inc="vs")
            bankfree[0] = [("vs", k.cnt["vs"])]
            k.op("vector", lambda e: e.tensor_scalar_mul(ccs[:, 0:16], a_p[:, 0:16],
                                                         bsel[:, 0:1]),
                 wait=[("vs", k.cnt["vs"])])
            k.op("vector", lambda e: e.tensor_scalar_mul(ccs[:, 16:32], a_p[:, 0:16],
                                                         bsel[:, 1:2]), inc="vs")
            ev_ccs1 = k.cnt["vs"]
            k.dma("gpsimd", cc_in_d.ap(), ccs[:, 0:32], wait=[("vs", ev_ccs1)], sem="scc")
            ev_ccin1 = k.cnt["scc"]
            k.op("gpsimd", lambda e: e.collective_compute(
                "AllReduce", ALU.add, replica_groups=[list(range(8))],
                ins=[cc_in_d.ap().opt()], outs=[cc_out_d.ap().opt()]),
                wait=[("scc", ev_ccin1)], inc="cc")
            k.op("vector", lambda e: e.memset(a_acc[:, :], 0.0),
                 wait=[("ts", ev_ap1), ("vs", k.cnt["vs"])], inc="vs")
            ev_aacc_v = k.cnt["vs"]
        if q == 0:
            ev_wi_persist_all = k.cnt["swip"]
            k.dma("sync", v3(wi_stage[0:1, 0:4096], 8),
                  b_in.rearrange("z (c n) -> z c n", c=12)[:, 4:12, :],
                  wait=[("vs", k.cnt["vs"]), ("ss", k.cnt["ss"])], sem="sm")
            ev_kvrows = k.cnt["sm"]

    for c in range(DC):
        k.op("tensor", lambda e, c=c: e.matmul(
            pb[0][:, c:c + 1], a_acc[:, c * 128:(c + 1) * 128], ones_col[:, 0:1],
            start=(c == 0), stop=(c == DC - 1)),
            wait=([("vs", ev_aacc_v)] + bankfree[0]) if c == 0 else (),
            inc="ts" if c == DC - 1 else None)
    ev_ap_t = k.cnt["ts"]
    k.op("vector", lambda e: e.tensor_copy(a_p[:, 0:16], pb[0][:, 0:16]),
         wait=[("ts", ev_ap_t)], inc="vs")
    bankfree[0] = [("vs", k.cnt["vs"])]
    k.op("vector", lambda e: e.tensor_scalar_mul(ccs[:, 0:16], a_p[:, 0:16], bsel[:, 0:1]),
         wait=[("scc", k.cnt["scc"]), ("vs", k.cnt["vs"])])
    k.op("vector", lambda e: e.tensor_scalar_mul(ccs[:, 16:32], a_p[:, 0:16], bsel[:, 1:2]),
         inc="vs")
    ev_ccs = k.cnt["vs"]
    k.dma("gpsimd", cc_in_d2.ap(), ccs[:, 0:32], wait=[("vs", ev_ccs)], sem="scc")
    ev_ccin = k.cnt["scc"]
    k.op("gpsimd", lambda e: e.collective_compute(
        "AllReduce", ALU.add, replica_groups=[list(range(8))],
        ins=[cc_in_d2.ap().opt()], outs=[cc_out_d2.ap().opt()]),
        wait=[("scc", ev_ccin)], inc="cc")
    k.dma("sync", ccs[:, 0:32], cc_out_d.ap(),
          wait=[("cc", 1), ("scc", ev_ccin)], sem="sccr")
    k.dma("sync", ccs2[:, 0:32], cc_out_d2.ap(), wait=[("cc", 2)], sem="sccr")
    ev_ccb = k.cnt["sccr"]
    k.op("vector", lambda e: e.tensor_tensor(ccs[:, 0:32], ccs[:, 0:32], ccs2[:, 0:32],
                                             ALU.add),
         wait=[("sccr", ev_ccb), ("vs", k.cnt["vs"])], inc="vs")
    k.op("vector", lambda e: e.tensor_scalar_mul(a_fin[:, 0:16], ccs[:, 0:16], bsel[:, 0:1]),
         wait=[("vs", k.cnt["vs"])], inc="vs")
    k.op("vector", lambda e: e.scalar_tensor_tensor(
        out=a_fin[:, 0:16], in0=ccs[:, 16:32], scalar=bsel[:, 1:2], in1=a_fin[:, 0:16],
        op0=ALU.mult, op1=ALU.add), wait=[("vs", k.cnt["vs"])], inc="vs")
    k.op("vector", lambda e: e.tensor_scalar(
        out=a_fin[:, 0:16], in0=a_fin[:, 0:16], scalar1=0.5, scalar2=0.5,
        op0=ALU.mult, op1=ALU.add), wait=[("vs", k.cnt["vs"])], inc="vs")
    ev_afin = k.cnt["vs"]
    if DEBUG:
        k.dma("gpsimd", dbg["dbg_ap"], a_p[:, 0:16], wait=[("vs", ev_afin)], sem="sdbg")
        k.dma("gpsimd", dbg["dbg_af"], a_fin[:, 0:16], sem="sdbg")

    fa.pop()
    ba.pop()
    k.barrier()
    wo_st = [fa.alloc(1024) for _ in range(16)]
    s1b = [fa.alloc(1024), fa.alloc(1024)]
    outb = [fa.alloc(1024), fa.alloc(1024)]
    woA_h = ba.alloc(16384)
    woA_l = ba.alloc(16384)
    qt_h = [ba.alloc(4096), ba.alloc(4096)]
    qt_l = [ba.alloc(4096), ba.alloc(4096)]

    ev_qt_used = {0: 0, 1: 0}
    ev_s1_used = {0: 0, 1: 0}
    ev_out_st = {0: 0, 1: 0}
    ld_qt_g = {}

    for half in range(2):
        hsl = slice(half * 1024, half * 1024 + 1024)
        qt_all = [("sqt0", k.cnt.get("sqt0", 0)), ("sqt1", k.cnt.get("sqt1", 0))]
        for g in range(2):
            gslot = g % 2
            k.dma("gpsimd", v3(qt_h[gslot][:, :], DC), qth[:, :, g * 256:(g + 1) * 256],
                  wait=qt_all + [("ts", ev_qt_used[gslot])], sem="spq%d" % gslot)
            k.dma("gpsimd", v3(qt_l[gslot][:, :], DC), qtl[:, :, g * 256:(g + 1) * 256],
                  sem="spq%d" % gslot)
            ld_qt_g[g] = k.cnt["spq%d" % gslot]
        ts_prev_half = k.cnt["ts"] if half else 0
        ev_wo_spl = {}
        for dc in range(DC):
            wsl = wo_st[dc]
            wl_ld = [("vs", ev_wo_spl0[dc])] if half else []
            ld = k.dma("scalar", wsl[:, :], w_out[dc * 128:dc * 128 + 128, hsl],
                       wait=wl_ld, sem="swo%d" % dc)
            k.op("scalar", lambda e, wsl=wsl, dc=dc: e.activation(
                woA_h[:, dc * 1024:(dc + 1) * 1024], wsl[:, :], AF.Identity,
                scale=a_fin[:, dc:dc + 1]),
                wait=[("swo%d" % dc, ld), ("vs", ev_afin), ("ts", ts_prev_half)],
                inc="ss")
            eh = k.cnt["ss"]
            k.op("vector", lambda e, wsl=wsl, dc=dc: e.scalar_tensor_tensor(
                out=woA_l[:, dc * 1024:(dc + 1) * 1024], in0=wsl[:, :],
                scalar=a_fin[:, dc:dc + 1], in1=woA_h[:, dc * 1024:(dc + 1) * 1024],
                op0=ALU.mult, op1=ALU.subtract),
                wait=[("ss", eh)], inc="vs")
            ev_wo_spl[dc] = k.cnt["vs"]
        ev_wo_spl0 = dict(ev_wo_spl)
        if DEBUG and half == 1:
            k.dma("gpsimd", dbg["dbg_woh"][:, :], woA_h[:, :],
                  wait=[("vs", k.cnt["vs"]), ("ss", k.cnt["ss"])])
            k.dma("gpsimd", dbg["dbg_wol"][:, :], woA_l[:, :], sem="sdbg")
            k.dma("gpsimd", dbg["dbg_af2"][:, :], a_fin[:, 0:16], sem="sdbg")

        for tt in range(16):
            tok0 = tt * 128
            g = tt // 2
            gslot = g % 2
            u = tt % 2
            slot = tt % 2
            s1_all = ([("ss10", k.cnt.get("ss10", 0)), ("ss11", k.cnt.get("ss11", 0))]
                      if (half == 0 and tt < 2) else [])
            k.dma("sync", s1b[slot][:, :], s1_d[tok0:tok0 + 128, hsl],
                  wait=s1_all + [("vs", ev_s1_used[slot])], sem="sps%d" % slot)
            ld_s1 = k.cnt["sps%d" % slot]
            for oc in range(2):
                bi = 2 * slot + oc
                bank = pb[bi]
                wl = bankfree[bi] + [("spq%d" % gslot, ld_qt_g[g])]
                nmm = 0
                for qc in range(DC):
                    lhh = qt_h[gslot][:, qc * 256 + u * 128: qc * 256 + u * 128 + 128]
                    lhl = qt_l[gslot][:, qc * 256 + u * 128: qc * 256 + u * 128 + 128]
                    rhh = woA_h[:, qc * 1024 + oc * 512: qc * 1024 + oc * 512 + 512]
                    rhl = woA_l[:, qc * 1024 + oc * 512: qc * 1024 + oc * 512 + 512]
                    for lw, rw in ((lhh, rhh), (lhh, rhl), (lhl, rhh)):
                        wlq = (wl if nmm == 0 else []) + (
                            [("vs", ev_wo_spl[qc])] if nmm % 3 == 0 else [])
                        k.op("tensor", lambda e, lw=lw, rw=rw, bank=bank, nmm=nmm: e.matmul(
                            bank[:, :], lw, rw, start=(nmm == 0), stop=(nmm == 47)),
                            wait=wlq,
                            inc="ts" if nmm == 47 else None)
                        nmm += 1
                ev_mm = k.cnt["ts"]
                k.op("vector", lambda e, bank=bank, slot=slot, oc=oc: e.tensor_tensor(
                    outb[slot][:, oc * 512:(oc + 1) * 512], bank[:, :],
                    s1b[slot][:, oc * 512:(oc + 1) * 512], ALU.add),
                    wait=[("ts", ev_mm), ("sps%d" % slot, ld_s1),
                          ("sy%d" % slot, ev_out_st[slot])], inc="vs")
                bankfree[bi] = [("vs", k.cnt["vs"])]
            if u == 1:
                ev_qt_used[gslot] = k.cnt["ts"]
                gn = g + 2
                if gn <= 7:
                    k.dma("gpsimd", v3(qt_h[gslot][:, :], DC),
                          qth[:, :, gn * 256:(gn + 1) * 256],
                          wait=[("ts", ev_qt_used[gslot])], sem="spq%d" % gslot)
                    k.dma("gpsimd", v3(qt_l[gslot][:, :], DC),
                          qtl[:, :, gn * 256:(gn + 1) * 256], sem="spq%d" % gslot)
                    ld_qt_g[gn] = k.cnt["spq%d" % gslot]
            ev_s1_used[slot] = k.cnt["vs"]
            k.dma("gpsimd", y_out[tok0:tok0 + 128, hsl], outb[slot][:, :],
                  wait=[("vs", k.cnt["vs"])], sem="sy%d" % slot)
            ev_out_st[slot] = k.cnt["sy%d" % slot]

    if DEBUG:
        k.dma("gpsimd", dbg["dbg_qh"][:, :], qth_d[0:D, :], wait=[("vs", k.cnt["vs"])], sem="sdbg")
        k.dma("gpsimd", dbg["dbg_ql"][:, :], qtl_d[0:D, :], sem="sdbg")
        k.dma("gpsimd", dbg["dbg_s1d"][:, :], s1_d[:, :], sem="sdbg")
        k.dma("gpsimd", dbg["dbg_x2h"][:, :], x2th_d[0:D, :], sem="sdbg")
    k.op("gpsimd", lambda e: e.wait_ge(k.semh["sy0"], k.cnt["sy0"]))
    k.op("gpsimd", lambda e: e.wait_ge(k.semh["sy1"], k.cnt["sy1"]))
    if DEBUG and "sdbg" in k.cnt:
        k.op("gpsimd", lambda e: e.wait_ge(k.semh["sdbg"], k.cnt["sdbg"]))
    return k.finish()


_NC_CACHE = None


def _get_nc():
    global _NC_CACHE
    if _NC_CACHE is None:
        _NC_CACHE = build_nc()
    return _NC_CACHE


def kernel(**inputs):
    X = np.asarray(inputs["L0_attn_X"], dtype=np.float32)
    w_in_raw = np.asarray(inputs["L0_attn_W_in"], np.float32)
    w_in = np.ascontiguousarray(
        w_in_raw.reshape(DC, 128, 12, 512).transpose(2, 1, 0, 3).reshape(12, 128, 8192))
    b_in = np.asarray(inputs["L0_attn_b_in"], np.float32).reshape(1, 3 * D)
    w_out = np.ascontiguousarray(np.asarray(inputs["L0_attn_W_out"], np.float32))
    b_out = np.asarray(inputs["L0_attn_b_out"], np.float32).reshape(1, D)
    gamma = np.asarray(inputs["L0_attn_gamma"], np.float32)
    beta = np.asarray(inputs["L0_attn_beta"], np.float32)
    w_ffn = np.asarray(inputs["L0_ffn_W"], np.float32)
    b_ffn = np.asarray(inputs["L0_ffn_b"], np.float32).reshape(1, D)
    lora_A = np.asarray(inputs["L0_ffn_lora_A"], np.float32)
    lora_B = np.asarray(inputs["L0_ffn_lora_B"], np.float32)
    gate_w = np.asarray(inputs["L0_ffn_gate_w"], np.float32)
    gate_b = np.asarray(inputs["L0_ffn_gate_b"], np.float32).reshape(1, 8)

    w_ffnT = np.ascontiguousarray(w_ffn.T)
    atall = np.ascontiguousarray(np.transpose(lora_A, (2, 0, 1)).reshape(D, 128))
    btall = np.ascontiguousarray(np.transpose(lora_B, (0, 2, 1)).reshape(128, D))
    gwT = np.ascontiguousarray(gate_w.T)
    gamma_part = np.ascontiguousarray(gamma.reshape(DC, 128).T)
    bq_part = np.ascontiguousarray(b_in[0, 0:D].reshape(DC, 128).T)
    ident = np.eye(128, dtype=np.float32)

    inv = 10000.0 ** (-np.arange(0, 1024, 2, dtype=np.float64) / 1024.0)
    in_maps = []
    for c in range(8):
        b = c // 4
        t0 = (c % 4) * TOK
        pos = np.arange(t0, t0 + TOK, dtype=np.float64)
        ang = pos[:, None] * inv[None, :]
        bsel_np = np.zeros((128, 2), np.float32)
        bsel_np[:, b] = 1.0
        in_maps.append({
            "x_in": np.ascontiguousarray(X[b, t0:t0 + TOK]),
            "w_in": w_in, "b_in": b_in, "w_out": w_out,
            "w_ffnT": w_ffnT, "gamma_part": gamma_part, "bq_part": bq_part,
            "gamma_row": gamma.reshape(1, D), "beta_row": beta.reshape(1, D),
            "b_ffn": b_ffn, "b_out_r": b_out,
            "atall": atall, "btall": btall, "gwT": gwT, "gb": gate_b,
            "cos_in": np.cos(ang).astype(np.float32),
            "sin_in": np.sin(ang).astype(np.float32),
            "ident_in": ident, "bsel_in": bsel_np,
        })
    nc = _get_nc()
    res = run_bass_kernel_spmd(nc, in_maps, core_ids=list(range(8)))
    out = np.empty((2, 8192, D), np.float32)
    for c in range(8):
        out[c // 4, (c % 4) * TOK:(c % 4 + 1) * TOK] = res.results[c]["y_out"]
    return out


# version 12 (final)
# speedup vs baseline: 1.0605x; 1.0306x over previous
import numpy as np
import concourse.bass as bass
import concourse.mybir as mybir
from concourse.bass_utils import run_bass_kernel_spmd

F32 = mybir.dt.float32
BF16 = mybir.dt.bfloat16
AF = mybir.ActivationFunctionType
ALU = mybir.AluOpType
AX = mybir.AxisListType.X

D = 2048
TOK = 2048
DC = 16
EPS = 1e-5
ENGS = ("tensor", "vector", "scalar", "sync", "gpsimd")

DEBUG = False


class KB:
    def __init__(self):
        self.nc = bass.Bass("TRN2", target_bir_lowering=False)
        self.plan = {e: [] for e in ENGS}
        self.cnt = {}
        self.semh = {}
        self.lastwait = {}
        self.pending = {e: [] for e in ENGS}

    def sems(self, *names):
        for n in names:
            self.semh[n] = self.nc.alloc_semaphore(n)
            self.cnt[n] = 0

    def barrier(self):
        snap = [(s, v) for s, v in self.cnt.items() if v > 0]
        for e in ENGS:
            self.pending[e].extend(snap)

    def op(self, eng, fn, wait=(), inc=None, n=1):
        todo = []
        allw = list(self.pending[eng]) + list(wait)
        self.pending[eng] = []
        for s, v in allw:
            if v > 0 and self.lastwait.get((eng, s), 0) < v:
                self.lastwait[(eng, s)] = v
                todo.append((s, v))
        if inc is not None:
            self.cnt[inc] += n
        semh = self.semh

        def run(e, todo=tuple(todo), fn=fn, inc=inc, n=n):
            for s, v in todo:
                e.wait_ge(semh[s], v)
            r = fn(e)
            if inc is not None:
                r.then_inc(semh[inc], n)
        self.plan[eng].append(run)
        return self.cnt[inc] if inc is not None else None

    SERIAL_SEMS = {"sm", "sbr", "scc", "swsp", "swis", "sdbg", "swfp", "swip",
                   "swo0", "swo1"}

    def dma(self, q, out, in_, wait=(), sem="sm"):
        if sem not in self.semh:
            self.sems(sem)
        if sem in self.SERIAL_SEMS and self.cnt[sem] > 0:
            wait = list(wait) + [(sem, self.cnt[sem])]
        return self.op(q, lambda e: e.dma_start(out=out, in_=in_), wait, sem, 16)

    def finish(self):
        with self.nc.Block() as block:
            for eng in ENGS:
                getattr(block, eng)(lambda e, eng=eng: [f(e) for f in self.plan[eng]])
        return self.nc


def v3(ap2d, c):
    return ap2d.rearrange("p (c u) -> p c u", c=c)


def build_nc():
    k = KB()
    nc = k.nc
    k.sems("ts", "vs", "ss", "cc", "sm")

    def din(name, shape):
        return nc.dram_tensor(name, shape, F32, kind="ExternalInput").ap()

    x_in = din("x_in", [TOK, D])
    w_in = din("w_in", [12, 128, 8192])
    b_in = din("b_in", [1, 3 * D])
    w_out = din("w_out", [D, D])
    w_ffnT = din("w_ffnT", [4, 128, 8192])
    gamma_part = din("gamma_part", [128, DC])
    bq_part = din("bq_part", [128, DC])
    gamma_row = din("gamma_row", [1, D])
    beta_row = din("beta_row", [1, D])
    b_ffn = din("b_ffn", [1, D])
    b_out_r = din("b_out_r", [1, D])
    atall_in = din("atall", [D, 128])
    btall_in = din("btall", [128, D])
    gwT_in = din("gwT", [D, 8])
    gb_in = din("gb", [1, 8])
    cos_in = din("cos_in", [TOK, 512])
    sin_in = din("sin_in", [TOK, 512])
    ident_in = din("ident_in", [128, 128])
    bsel_in = din("bsel_in", [128, 2])

    y_out = nc.dram_tensor("y_out", [TOK, D], F32, kind="ExternalOutput").ap()
    dbg = {}
    if DEBUG:
        for nm, shape in [("dbg_xn", [TOK, D]), ("dbg_xp", [TOK, D]),
                          ("dbg_ap", [128, 16]), ("dbg_af", [128, 16]),
                          ("dbg_w8", [TOK, 8]), ("dbg_s1", [TOK, D]),
                          ("dbg_x0", [128, D]), ("dbg_st", [128, 32]),
                          ("dbg_s1d", [TOK, D]), ("dbg_af2", [128, 16])]:
            dbg[nm] = nc.dram_tensor(nm, shape, F32, kind="ExternalOutput").ap()
        for nm, shape in [("dbg_qh", [D, TOK]), ("dbg_ql", [D, TOK]),
                          ("dbg_x2h", [D, TOK]), ("dbg_woh", [128, 16384]),
                          ("dbg_wol", [128, 16384])]:
            dbg[nm] = nc.dram_tensor(nm, shape, BF16, kind="ExternalOutput").ap()

    wfh_d = nc.dram_tensor("wfh_d", [4, 128, 8192], BF16).ap()
    wfl_d = nc.dram_tensor("wfl_d", [4, 128, 8192], BF16).ap()
    wih_d = nc.dram_tensor("wih_d", [12, 128, 8192], BF16).ap()
    wil_d = nc.dram_tensor("wil_d", [12, 128, 8192], BF16).ap()
    x2th_d = nc.dram_tensor("x2th_d", [D, TOK], BF16).ap()
    x2tl_d = nc.dram_tensor("x2tl_d", [D, TOK], BF16).ap()
    qth_d = nc.dram_tensor("qth_d", [D, TOK], BF16).ap()
    qtl_d = nc.dram_tensor("qtl_d", [D, TOK], BF16).ap()
    s1_d = nc.dram_tensor("s1_d", [TOK, D], F32).ap()
    cc_in_d = nc.dram_tensor("cc_in_d", [128, 32], F32)
    cc_out_d = nc.dram_tensor("cc_out_d", [128, 32], F32)
    cc_in_d2 = nc.dram_tensor("cc_in_d2", [128, 32], F32)
    cc_out_d2 = nc.dram_tensor("cc_out_d2", [128, 32], F32)

    x2th = x2th_d.rearrange("(c p) t -> p c t", p=128)
    x2tl = x2tl_d.rearrange("(c p) t -> p c t", p=128)
    qth = qth_d.rearrange("(c p) t -> p c t", p=128)
    qtl = qtl_d.rearrange("(c p) t -> p c t", p=128)

    SB_TOP = 229344
    F32_COLS = 24064
    B16_COLS = 58288
    base = (SB_TOP - (F32_COLS * 4 + B16_COLS * 2)) // 64 * 64
    f32a = nc.alloc_sbuf_tensor_at("f32_arena", [128, F32_COLS], F32, offset=base).ap()
    b16a = nc.alloc_sbuf_tensor_at("b16_arena", [128, B16_COLS], BF16,
                                   offset=base + F32_COLS * 4).ap()

    class Arena:
        def __init__(self, ap, cols):
            self.ap, self.cols, self.off, self.marks = ap, cols, 0, []

        def alloc(self, n):
            n = (n + 15) // 16 * 16
            assert self.off + n <= self.cols, f"arena overflow {self.off}+{n}>{self.cols}"
            s = self.ap[:, self.off:self.off + n]
            self.off += n
            return s

        def push(self):
            self.marks.append(self.off)

        def pop(self):
            self.off = self.marks.pop()

    fa, ba = Arena(f32a, F32_COLS), Arena(b16a, B16_COLS)

    idt = fa.alloc(128)
    ones_row = fa.alloc(512)
    ones_col = fa.alloc(16)
    bsel = fa.alloc(16)
    gpart = fa.alloc(16)
    bqp = fa.alloc(16)
    a_p = fa.alloc(16)
    a_fin = fa.alloc(16)
    ccs = fa.alloc(32)
    ccs2 = fa.alloc(32)
    rowa = fa.alloc(512)
    rowb = fa.alloc(512)
    rowc = fa.alloc(512)
    fa.push()
    ba.push()

    pb = [nc.alloc_psum_tensor(f"pb{i}", [128, 512], F32).ap() for i in range(8)]
    bankfree = {i: [] for i in range(8)}

    b_sum = fa.alloc(1024)

    def bsum_row(c):
        p = [0, 32, 64, 0][c]
        off = 512 if c == 3 else 0
        return b_sum[p:p + 1, off:off + 512]
    bog_rep = fa.alloc(2048)

    k.dma("sync", idt[:, :], ident_in[:, :])
    k.dma("sync", bsel[:, 0:2], bsel_in[:, :])
    k.dma("sync", gpart[:, 0:DC], gamma_part[:, :])
    k.dma("sync", bqp[:, 0:DC], bq_part[:, :])
    k.op("vector", lambda e: e.memset(ones_row[:, :], 1.0))
    k.op("vector", lambda e: e.memset(st_[:, :], 0.0))
    k.op("vector", lambda e: e.memset(g_sc[:, :], 0.0))
    k.op("vector", lambda e: e.memset(ones_col[:, 0:1], 1.0), inc="vs")

    for j in range(4):
        sl = slice(j * 512, (j + 1) * 512)
        prev_v = k.cnt["vs"]
        k.dma("sync", rowa[0:1, :], gamma_row[0:1, sl], wait=[("vs", prev_v)])
        k.dma("sync", rowb[0:1, :], beta_row[0:1, sl])
        k.dma("sync", rowc[[0, 32, 64, 0][j]:[0, 32, 64, 0][j] + 1, :], b_ffn[0:1, sl])
        ldj = k.cnt["sm"]
        k.op("vector", lambda e: e.reciprocal(rowa[0:1, :], rowa[0:1, :]),
             wait=[("sm", ldj), ("vs", k.cnt["vs"])], inc="vs")
        k.op("vector", lambda e: e.tensor_mul(rowb[0:1, :], rowb[0:1, :], rowa[0:1, :]),
             wait=[("vs", k.cnt["vs"])], inc="vs")
        evd = k.cnt["vs"]
        k.op("tensor", lambda e: e.matmul(pb[7][:, :], ones_row[0:1, 0:128], rowb[0:1, :],
                                          start=True, stop=True),
             wait=[("vs", evd)] + bankfree[7], inc="ts")
        evt = k.cnt["ts"]
        k.op("vector", lambda e, sl=sl: e.tensor_copy(bog_rep[:, sl], pb[7][:, :]),
             wait=[("ts", evt)], inc="vs")
        bankfree[7] = [("vs", k.cnt["vs"])]
        pj_ = [0, 32, 64, 0][j]
        k.dma("sync", rowa[pj_:pj_ + 1, :], b_out_r[0:1, sl], wait=[("vs", evd)])
        ldj2 = k.cnt["sm"]
        k.op("vector", lambda e, j=j, pj_=pj_: e.tensor_tensor(
            bsum_row(j), rowa[pj_:pj_ + 1, :], rowc[pj_:pj_ + 1, :], ALU.add),
            wait=[("sm", ldj2)], inc="vs")
    ev_consts_v = k.cnt["vs"]

    xb = [fa.alloc(2048), fa.alloc(2048)]
    xn_t = fa.alloc(2048)
    zz = fa.alloc(2048)
    xxn = [fa.alloc(2048), fa.alloc(2048)]
    wsp = fa.alloc(4096)
    s1s = [fa.alloc(512), fa.alloc(512)]
    st_ = fa.alloc(32)
    xa_f2 = [fa.alloc(144), fa.alloc(144)]
    g_sc = fa.alloc(48)
    dbg_buf = fa.alloc(32) if DEBUG else None

    xnT_h = ba.alloc(4096)
    xnT_l = ba.alloc(4096)
    x2s_h = [ba.alloc(2048), ba.alloc(2048)]
    x2s_l = [ba.alloc(2048), ba.alloc(2048)]
    wf_h = [ba.alloc(8192), ba.alloc(8192)]
    wf_l = [ba.alloc(8192), ba.alloc(8192)]
    at_h = ba.alloc(2048)
    at_l = ba.alloc(2048)
    bt_h = ba.alloc(2048)
    bt_l = ba.alloc(2048)
    gw_h = ba.alloc(128)
    gw_l = ba.alloc(128)
    xas_h = [ba.alloc(128), ba.alloc(128)]
    xas_l = [ba.alloc(128), ba.alloc(128)]

    def small_split(dst_h, dst_l, ncols, load_fn):
        ld = load_fn()
        k.op("scalar", lambda e: e.activation(dst_h, wsp[:, 0:ncols], AF.Copy),
             wait=[("sm", ld)], inc="ss")
        eh = k.cnt["ss"]
        k.op("vector", lambda e: e.tensor_sub(dst_l, wsp[:, 0:ncols], dst_h),
             wait=[("ss", eh), ("sm", ld)], inc="vs")
        return k.cnt["vs"]

    ev_w = small_split(at_h[:, :], at_l[:, :], 2048,
                       lambda: k.dma("sync", v3(wsp[:, 0:2048], DC),
                                     atall_in.rearrange("(c p) r -> p c r", p=128)))
    ev_w = small_split(bt_h[:, :], bt_l[:, :], 2048,
                       lambda: k.dma("sync", wsp[:, 0:2048], btall_in[:, :],
                                     wait=[("vs", ev_w)]))
    ev_w = small_split(gw_h[:, :], gw_l[:, :], 128,
                       lambda: k.dma("sync", v3(wsp[:, 0:128], DC),
                                     gwT_in.rearrange("(c p) g -> p c g", p=128),
                                     wait=[("vs", ev_w)]))
    k.dma("sync", rowa[0:1, 0:8], gb_in[0:1, :])
    ld_gb = k.cnt["sm"]
    ev_wsp_free = ev_w

    ev_x_free = {}
    ev_xxn_free = {}
    ev_qg_x2s = {0: 0, 1: 0}
    ev_xn_tp = 0
    ev_zz_tp = 0
    ev_qg_x2 = 0
    ev_wf_used = {0: 0, 1: 0}
    ev_wf_persist = {}
    ev_s1_free = {0: 0, 1: 0}
    ev_xas_used = 0
    ev_xnT_consumed = 0

    S = dict(sum1=0, sq1=1, mu1=2, var1=3, r1=4, nmr1=5, r1p=6, sum2=7, sq2=8,
             mu2=9, var2=10, r2=11, l1=12, l2=13, d12=14, g1=15, g2=16, tmp=17, rc=18,
             r1p0=20, r1p1=21, nmr0=22, nmr1_1=23)

    def st(nm):
        i = S[nm]
        return st_[:, i:i + 1]

    def svop(fn, wait=()):
        return k.op("vector", fn, wait=list(wait) + [("vs", k.cnt["vs"])], inc="vs")

    RGT, REQ, RANK = g_sc[:, 0:8], g_sc[:, 8:16], g_sc[:, 16:24]
    M1, M2, W8 = g_sc[:, 24:32], g_sc[:, 32:40], g_sc[:, 40:48]

    def transpose_group(src, dcs, bi, dst_h3, dst_l3, wait_first, wait_evac):
        bank = pb[bi]
        for c, dc in enumerate(dcs):
            k.op("tensor", lambda e, c=c, dc=dc: e.matmul(
                bank[:, c * 128:(c + 1) * 128], src[:, dc * 128:(dc + 1) * 128],
                idt[:, :], start=(c == 0), stop=(c == 3), is_transpose=True),
                wait=(list(wait_first) + bankfree[bi]) if c == 0 else (),
                inc="ts" if c == 3 else None)
        evt = k.cnt["ts"]
        k.op("scalar", lambda e: e.activation(dst_h3, v3(bank[:, :], 4), AF.Copy),
             wait=[("ts", evt)] + list(wait_evac), inc="ss")
        ehh = k.cnt["ss"]
        k.op("vector", lambda e: e.tensor_sub(dst_l3, v3(bank[:, :], 4), dst_h3),
             wait=[("ts", evt), ("ss", ehh)] + list(wait_evac), inc="vs")
        bankfree[bi] = [("ss", ehh), ("vs", k.cnt["vs"])]

    ev_xnT_tile = {}
    pend_xas = {}
    ev_xas_consumed = {0: 0, 1: 0}
    TPB = [6, 7, 0, 1, 2, 3]
    tp_i = [0]

    def next_tpb():
        b = TPB[tp_i[0] % 6]
        tp_i[0] += 1
        return b
    ln_ev = {}
    ld_x = {}
    last_xxn_ss = [0]

    def pre_ldx(tt):
        t = tt % 2
        wl = ev_x_free.get(tt - 2, [("vs", ev_consts_v)])
        ld_x[tt] = k.dma("gpsimd", xb[t][:, :], x_in[tt * 128:tt * 128 + 128, :],
                         wait=wl, sem="sx%d" % t)

    def emit_ln_tile(tt):
        t = tt % 2
        xbuf = xb[t]
        if tt not in ld_x:
            pre_ldx(tt)
        sx = "sx%d" % (tt % 2)
        ldx = ld_x.pop(tt)
        svop(lambda e, xbuf=xbuf: e.reduce_sum(st("sum1"), xbuf[:, :], axis=AX),
             wait=[(sx, ldx), ("ss", last_xxn_ss[0])])
        k.op("scalar", lambda e, xbuf=xbuf: e.activation(
            wsp[:, 0:2048], xbuf[:, :], AF.Square, accum_out=st("sq1")),
            wait=[(sx, ldx), ("vs", ev_wsp_free)], inc="ss")
        ev_sq1 = k.cnt["ss"]
        svop(lambda e: e.tensor_single_scalar(st("mu1"), st("sum1"), 1.0 / D, ALU.mult))
        svop(lambda e: e.tensor_single_scalar(st("sq1"), st("sq1"), 1.0 / D, ALU.mult),
             wait=[("ss", ev_sq1)])
        svop(lambda e: e.tensor_mul(st("var1"), st("mu1"), st("mu1")))
        svop(lambda e: e.tensor_sub(st("var1"), st("sq1"), st("var1")))
        svop(lambda e: e.tensor_single_scalar(st("var1"), st("var1"), EPS, ALU.add))
        svop(lambda e: e.reciprocal(st("rc"), st("var1")))
        ev_v1 = k.cnt["vs"]
        k.op("scalar", lambda e: e.activation(st("r1"), st("rc"), AF.Sqrt),
             wait=[("vs", ev_v1)], inc="ss")
        ev_r1 = k.cnt["ss"]
        svop(lambda e: e.tensor_mul(st("nmr1"), st("mu1"), st("r1")),
             wait=[("ss", ev_r1)])
        svop(lambda e: e.tensor_single_scalar(st("nmr1"), st("nmr1"), -1.0, ALU.mult))
        svop(lambda e: e.tensor_single_scalar(st("r1p"), st("r1"), 1.0, ALU.add))
        ev_ln1 = k.cnt["vs"]
        k.op("scalar", lambda e, xbuf=xbuf: e.activation(
            xn_t[:, :], xbuf[:, :], AF.Identity, bias=st("nmr1"), scale=st("r1")),
            wait=[("vs", ev_ln1), ("ts", ev_xn_tp)], inc="ss")
        ev_xn = k.cnt["ss"]
        svop(lambda e: e.reduce_sum(st("sum2"), xn_t[:, :], axis=AX),
             wait=[("ss", ev_xn)])
        k.op("scalar", lambda e: e.activation(wsp[:, 0:2048], xn_t[:, :], AF.Square,
                                              accum_out=st("sq2")),
             wait=[("ss", ev_xn)], inc="ss")
        ev_sq2 = k.cnt["ss"]
        svop(lambda e: e.tensor_single_scalar(st("mu2"), st("sum2"), 1.0 / D, ALU.mult))
        svop(lambda e: e.tensor_single_scalar(st("sq2"), st("sq2"), 1.0 / D, ALU.mult),
             wait=[("ss", ev_sq2)])
        svop(lambda e: e.tensor_mul(st("var2"), st("mu2"), st("mu2")))
        svop(lambda e: e.tensor_sub(st("var2"), st("sq2"), st("var2")))
        svop(lambda e: e.tensor_single_scalar(st("var2"), st("var2"), EPS, ALU.add))
        svop(lambda e: e.reciprocal(st("rc"), st("var2")))
        ev_v2 = k.cnt["vs"]
        k.op("scalar", lambda e: e.activation(st("r2"), st("rc"), AF.Sqrt),
             wait=[("vs", ev_v2)], inc="ss")
        ev_r2 = k.cnt["ss"]
        k.op("vector", lambda e: e.tensor_scalar(
            out=zz[:, :], in0=xn_t[:, :], scalar1=st("mu2"), scalar2=st("r2"),
            op0=ALU.subtract, op1=ALU.mult),
            wait=[("ts", ev_zz_tp), ("ss", ev_r2)], inc="vs")
        k.op("vector", lambda e: e.tensor_tensor(zz[:, :], zz[:, :], bog_rep[:, :],
                                                 ALU.add),
             wait=[("vs", k.cnt["vs"])], inc="vs")
        ev_xp = k.cnt["vs"]
        ln_ev[tt] = dict(sx=sx, ldx=ldx, xn=ev_xn, xp=ev_xp)

    for sw in range(8):
        ev_tile_xas = {}
        for t in range(2):
            tt = sw * 2 + t
            tok0 = tt * 128
            xbuf = xb[t]
            if tt not in ln_ev:
                emit_ln_tile(tt)
            lev = ln_ev.pop(tt)
            sx, ldx, ev_xn, ev_xp = lev["sx"], lev["ldx"], lev["xn"], lev["xp"]
            k.op("scalar", lambda e, xbuf=xbuf, t=t: e.activation(
                xxn[t][:, :], xbuf[:, :], AF.Identity, bias=st("nmr1"), scale=st("r1p")),
                wait=[("vs", ev_xxn_free.get(tt - 2, 0))], inc="ss")
            ev_xxn_w = k.cnt["ss"]
            last_xxn_ss[0] = ev_xxn_w
            if DEBUG:
                if tt == 0:
                    k.dma("gpsimd", dbg["dbg_x0"], xbuf[:, :], wait=[(sx, ldx)], sem="sdbg")
                    k.op("vector", lambda e: e.tensor_copy(dbg_buf[:, 0:32], st_[:, 0:32]),
                         inc="vs")
                    k.dma("gpsimd", dbg["dbg_st"], dbg_buf[:, 0:32],
                          wait=[("vs", k.cnt["vs"])])
                k.dma("gpsimd", dbg["dbg_xn"][tok0:tok0 + 128, :], xn_t[:, :],
                      wait=[("ss", ev_xn)])
                k.dma("gpsimd", dbg["dbg_xp"][tok0:tok0 + 128, :], zz[:, :],
                      wait=[("vs", ev_xp)])

            xnT_h16 = v3(xnT_h[:, :], DC)
            xnT_l16 = v3(xnT_l[:, :], DC)
            for g in range(4):
                dsl = (slice(None), slice(4 * g, 4 * g + 4), slice(t * 128, t * 128 + 128))
                transpose_group(xn_t, [4 * g + c for c in range(4)], next_tpb(),
                                xnT_h16[dsl], xnT_l16[dsl],
                                [("ss", ev_xn)], [("ts", ev_xnT_consumed)])
            ev_xn_tp = k.cnt["ts"]
            ev_xnT = (k.cnt["ss"], k.cnt["vs"])
            ev_xnT_tile[tt] = [("ss", k.cnt["ss"]), ("vs", k.cnt["vs"])]

            xa_f = xa_f2[t]
            LOG = xa_f[:, 128:136]
            MSK = xa_f[:, 136:144]
            ev_xa_prev_used = ev_xas_used
            nmm = 0
            for dc in range(DC):
                lh = xnT_h[:, dc * 256 + t * 128: dc * 256 + t * 128 + 128]
                ll = xnT_l[:, dc * 256 + t * 128: dc * 256 + t * 128 + 128]
                for lw, rwa, rwg in ((lh, at_h, gw_h), (lh, at_l, gw_l), (ll, at_h, gw_h)):
                    wl0 = ([("ss", ev_xnT[0]), ("vs", ev_xnT[1]), ("sm", ld_gb)]
                           + bankfree[4]) if nmm == 0 else ()
                    k.op("tensor", lambda e, lw=lw, rw=rwa, dc=dc, nmm=nmm: e.matmul(
                        pb[4][:, 0:128], lw, rw[:, dc * 128:(dc + 1) * 128],
                        start=(nmm == 0), stop=False), wait=wl0)
                    k.op("tensor", lambda e, lw=lw, rw=rwg, dc=dc: e.matmul(
                        pb[4][:, 128:136], lw, rw[:, dc * 8:(dc + 1) * 8],
                        start=False, stop=False))
                    nmm += 1
            k.op("tensor", lambda e: e.matmul(pb[4][:, 128:136], ones_row[0:1, 0:128],
                                              rowa[0:1, 0:8], start=False, stop=True),
                 inc="ts")
            ev_xa_t = k.cnt["ts"]
            svop(lambda e, xa_f=xa_f: e.tensor_copy(xa_f[:, 0:136], pb[4][:, 0:136]),
                 wait=[("ts", ev_xa_t), ("ts", ev_xa_prev_used)])
            bankfree[4] = [("vs", k.cnt["vs"])]
            ev_evac = k.cnt["vs"]
            MSK2 = st_[:, 24:32]
            for e8 in range(8):
                svop(lambda e, e8=e8, LOG=LOG, MSK=MSK: e.tensor_scalar(
                    out=MSK, in0=LOG, scalar1=LOG[:, e8:e8 + 1], scalar2=None,
                    op0=ALU.is_gt, op1=ALU.add, accum_out=RGT[:, e8:e8 + 1]),
                    wait=[("vs", ev_evac)] if e8 == 0 else ())
            svop(lambda e: e.memset(REQ[:, 0:1], 0.0))
            for e8 in range(1, 8):
                svop(lambda e, e8=e8, LOG=LOG, MSK=MSK: e.tensor_scalar(
                    out=MSK[:, 0:e8], in0=LOG[:, 0:e8], scalar1=LOG[:, e8:e8 + 1],
                    scalar2=None, op0=ALU.is_equal, op1=ALU.add,
                    accum_out=REQ[:, e8:e8 + 1]))
            svop(lambda e: e.tensor_add(RANK, RGT, REQ))
            svop(lambda e: e.tensor_single_scalar(M1, RANK, 0.5, ALU.is_lt))
            svop(lambda e: e.tensor_single_scalar(M2, RANK, 1.0, ALU.is_equal))
            svop(lambda e, MSK=MSK, LOG=LOG: e.tensor_mul(MSK, M1, LOG))
            svop(lambda e, MSK2=MSK2, LOG=LOG: e.tensor_mul(MSK2, M2, LOG))
            svop(lambda e, MSK=MSK: e.reduce_sum(st("l1"), MSK, axis=AX))
            svop(lambda e, MSK2=MSK2: e.reduce_sum(st("l2"), MSK2, axis=AX))
            svop(lambda e: e.tensor_sub(st("d12"), st("l1"), st("l2")))
            ev_d12 = k.cnt["vs"]
            k.op("scalar", lambda e: e.activation(st("g1"), st("d12"), AF.Sigmoid),
                 wait=[("vs", ev_d12)], inc="ss")
            ev_g1 = k.cnt["ss"]
            svop(lambda e: e.tensor_scalar(
                out=st("g2"), in0=st("g1"), scalar1=-1.0, scalar2=1.0,
                op0=ALU.mult, op1=ALU.add), wait=[("ss", ev_g1)])
            svop(lambda e: e.tensor_scalar_mul(W8, M1, st("g1")))
            svop(lambda e: e.scalar_tensor_tensor(
                out=W8, in0=M2, scalar=st("g2"), in1=W8, op0=ALU.mult, op1=ALU.add))
            ev_w8 = k.cnt["vs"]
            if DEBUG:
                k.dma("gpsimd", dbg["dbg_w8"][tok0:tok0 + 128, :], W8,
                      wait=[("vs", ev_w8)])
            for e8 in range(8):
                k.op("vector", lambda e, e8=e8, xa_f=xa_f: e.tensor_scalar_mul(
                    xa_f[:, e8 * 16:(e8 + 1) * 16], xa_f[:, e8 * 16:(e8 + 1) * 16],
                    W8[:, e8:e8 + 1]), wait=[("vs", ev_w8)],
                    inc="vs" if e8 == 7 else None)
            svop(lambda e: e.tensor_copy(st("tmp"), st("tmp")))
            pend_xas[t] = (xa_f, k.cnt["vs"])
            ev_x_free[tt] = [("vs", k.cnt["vs"]), ("ss", ev_xxn_w)]
            if tt + 2 < 16:
                pre_ldx(tt + 2)

            xsl = tt % 2
            for g in range(4):
                transpose_group(zz, [4 * g + c for c in range(4)], next_tpb(),
                                v3(x2s_h[xsl][:, g * 512:(g + 1) * 512], 4),
                                v3(x2s_l[xsl][:, g * 512:(g + 1) * 512], 4),
                                [("vs", ev_xp)], [("sx2%d" % xsl, ev_qg_x2s[xsl])])
            ev_zz_tp = k.cnt["ts"]
            ev_x2s = (k.cnt["ss"], k.cnt["vs"])
            k.dma("gpsimd", x2th[:, :, tok0:tok0 + 128], v3(x2s_h[xsl][:, :], DC),
                  wait=[("ss", ev_x2s[0]), ("vs", ev_x2s[1])], sem="sx2%d" % xsl)
            k.dma("gpsimd", x2tl[:, :, tok0:tok0 + 128], v3(x2s_l[xsl][:, :], DC),
                  sem="sx2%d" % xsl)
            ev_qg_x2s[xsl] = k.cnt["sx2%d" % xsl]

        for t in range(2):
            tt = sw * 2 + t
            xa_f, ev_xas_v = pend_xas[t]
            k.op("tensor", lambda e, xa_f=xa_f: e.matmul(
                pb[5][:, 0:128], xa_f[:, 0:128], idt[:, :],
                start=True, stop=True, is_transpose=True),
                wait=[("vs", ev_xas_v)] + bankfree[5], inc="ts")
            ev_xas_used = k.cnt["ts"]
            k.op("scalar", lambda e, t=t: e.activation(xas_h[t][:, :], pb[5][:, 0:128], AF.Copy),
                 wait=[("ts", ev_xas_used), ("ts", ev_xas_consumed[t])], inc="ss")
            eh = k.cnt["ss"]
            k.op("vector", lambda e, t=t: e.tensor_sub(xas_l[t][:, :], pb[5][:, 0:128],
                                                       xas_h[t][:, :]),
                 wait=[("ts", ev_xas_used), ("ss", eh)], inc="vs")
            bankfree[5] = [("vs", k.cnt["vs"])]
            ev_tile_xas[tt] = [("ss", eh), ("vs", k.cnt["vs"])]

        if 1 <= sw < 7:
            emit_ln_tile(2 * (sw + 1))

        if sw == 0:
            ev_wsp_half = {0: [("vs", k.cnt["vs"])], 1: [("vs", k.cnt["vs"])]}
        for o in range(4):
            slot = o % 2
            if sw == 0:
                for qtr in range(4):
                    hb = qtr % 2
                    buf = wsp[:, hb * 2048:hb * 2048 + 2048]
                    qn = "swsp" if hb == 0 else "swsq"
                    ld = k.dma("sync" if hb == 0 else "scalar", buf,
                               w_ffnT[o][:, qtr * 2048:qtr * 2048 + 2048],
                               wait=ev_wsp_half[hb], sem=qn)
                    for dcl in range(4):
                        dc = qtr * 4 + dcl
                        seg = buf[:, dcl * 512:(dcl + 1) * 512]
                        dh = wf_h[slot][:, dc * 512:(dc + 1) * 512]
                        k.op("scalar", lambda e, seg=seg, dh=dh: e.activation(
                            dh, seg, AF.Copy),
                            wait=([(qn, ld), ("ts", ev_wf_used[slot]),
                                   ("swfp", k.cnt.get("swfp", 0))]
                                  if dcl == 0 else ()),
                            inc="ss" if dcl == 3 else None)
                    evh = k.cnt["ss"]
                    for dcl in range(4):
                        dc = qtr * 4 + dcl
                        seg = buf[:, dcl * 512:(dcl + 1) * 512]
                        dh = wf_h[slot][:, dc * 512:(dc + 1) * 512]
                        dl = wf_l[slot][:, dc * 512:(dc + 1) * 512]
                        k.op("vector", lambda e, seg=seg, dh=dh, dl=dl: e.tensor_sub(
                            dl, seg, dh),
                            wait=[("ss", evh)] if dcl == 0 else (),
                            inc="vs" if dcl == 3 else None)
                    ev_wsp_half[hb] = [("vs", k.cnt["vs"])]
                k.dma("gpsimd", wfh_d[o], wf_h[slot][:, :],
                      wait=[("ss", k.cnt["ss"])], sem="swfp")
                k.dma("gpsimd", wfl_d[o], wf_l[slot][:, :],
                      wait=[("vs", k.cnt["vs"])], sem="swfp")
                ev_wf_persist[o] = k.cnt["swfp"]
                ev_wf_ld = [("vs", k.cnt["vs"]), ("ss", k.cnt["ss"])]
                ev_wsp_free = k.cnt["vs"]
            else:
                k.dma("sync", wf_h[slot][:, :], wfh_d[o],
                      wait=[("ts", ev_wf_used[slot]), ("swfp", k.cnt["swfp"])],
                      sem="swf%d" % slot)
                k.dma("scalar", wf_l[slot][:, :], wfl_d[o],
                      wait=[("ts", ev_wf_used[slot]), ("swfp", k.cnt["swfp"])],
                      sem="swg%d" % slot)
                ev_wf_ld = [("swf%d" % slot, k.cnt["swf%d" % slot]),
                            ("swg%d" % slot, k.cnt["swg%d" % slot])]
            for t in range(2):
                tt = sw * 2 + t
                tok0 = tt * 128
                bi = (2 * o + t) % 4
                bank = pb[bi]
                bsl = slice(o * 512, o * 512 + 512)
                wl = (ev_wf_ld + bankfree[bi] + ev_xnT_tile[tt])
                po = [0, 32, 64, 0][o]
                k.op("tensor", lambda e, bank=bank, o=o, po=po: e.matmul(
                    bank[:, :], ones_row[po:po + 1, 0:128], bsum_row(o),
                    start=True, stop=False), wait=wl)
                for dc in range(DC):
                    lh = xnT_h[:, dc * 256 + t * 128: dc * 256 + t * 128 + 128]
                    ll = xnT_l[:, dc * 256 + t * 128: dc * 256 + t * 128 + 128]
                    rh = wf_h[slot][:, dc * 512:(dc + 1) * 512]
                    rl = wf_l[slot][:, dc * 512:(dc + 1) * 512]
                    for lw, rw in ((lh, rh), (lh, rl), (ll, rh)):
                        k.op("tensor", lambda e, lw=lw, rw=rw, bank=bank: e.matmul(
                            bank[:, :], lw, rw, start=False, stop=False))
                lora_ops = ((xas_h[t], bt_h), (xas_h[t], bt_l), (xas_l[t], bt_h))
                for li, (lw, rw) in enumerate(lora_ops):
                    k.op("tensor", lambda e, lw=lw, rw=rw, bank=bank, bsl=bsl, li=li: e.matmul(
                        bank[:, :], lw[:, :], rw[:, bsl], start=False, stop=(li == 2)),
                        wait=ev_tile_xas[tt] if li == 0 else (),
                        inc="ts" if li == 2 else None)
                ev_b = k.cnt["ts"]
                ev_xas_consumed[t] = ev_b
                if t == 1:
                    ev_wf_used[slot] = ev_b
                s1slot = t
                k.op("vector", lambda e, bank=bank, t=t, bsl=bsl, s1slot=s1slot: e.tensor_tensor(
                    s1s[s1slot][:, :], bank[:, :], xxn[t][:, bsl], ALU.add),
                    wait=[("ts", ev_b), ("ss1%d" % s1slot, ev_s1_free[s1slot])], inc="vs")
                bankfree[bi] = [("vs", k.cnt["vs"])]
                ev_s1v = k.cnt["vs"]
                k.dma("gpsimd", s1_d[tok0:tok0 + 128, bsl], s1s[s1slot][:, :],
                      wait=[("vs", ev_s1v)], sem="ss1%d" % s1slot)
                ev_s1_free[s1slot] = k.cnt["ss1%d" % s1slot]
                if DEBUG:
                    k.dma("gpsimd", dbg["dbg_s1"][tok0:tok0 + 128, bsl], s1s[s1slot][:, :],
                          sem="ss1%d" % s1slot)
                    ev_s1_free[s1slot] = k.cnt["ss1%d" % s1slot]
                if o == 3:
                    ev_xxn_free[tt] = ev_s1v
        ev_xnT_consumed = k.cnt["ts"]

    ev_1a_end_x2 = [("sx2%d" % s, k.cnt.get("sx2%d" % s, 0)) for s in range(2)]

    fa.pop()
    ba.pop()
    fa.push()
    ba.push()
    k.barrier()

    wi_stage = fa.alloc(4096)
    P4 = [fa.alloc(2048) for _ in range(4)]
    kvs = fa.alloc(2560)
    ks_, vsb = kvs[:, 0:512], kvs[:, 512:1024]
    krot = kvs[:, 1024:1536]
    m4 = kvs[:, 1536:2560]
    ssc = m4[:, 0:512]
    cs_b = [fa.alloc(1024) for _ in range(4)]
    nsq = fa.alloc(48)
    a_acc = fa.alloc(2048)
    sqw = fa.alloc(512)
    k.op("vector", lambda e: e.memset(a_acc[:, :], 0.0), inc="vs")

    x2q_h = ba.alloc(8192)
    x2q_l = ba.alloc(8192)
    wi_h = [ba.alloc(8192), ba.alloc(8192)]
    wi_l = [ba.alloc(8192), ba.alloc(8192)]
    qt_sh = [ba.alloc(512), ba.alloc(512)]
    qt_sl = [ba.alloc(512), ba.alloc(512)]

    CHUNK_ORDER = [0, 1, 2, 3, 4, 8, 5, 9, 6, 10, 7, 11]
    ev_wi_used = {0: 0, 1: 0}
    ev_wi_persist = {}
    ev_wi_persist_all = 0
    ev_qt_free = {0: 0, 1: 0}
    ev_x2q_used = 0
    ev_cs_free = {0: 0, 1: 0, 2: 0, 3: 0}
    ev_aacc_v = ev_consts_v
    ev_P_free = {0: 0, 1: 0, 2: 0, 3: 0}
    next_q_wl = {}
    ldb_k = {}
    ldb_v = {}

    for q in range(4):
        qtok = q * 512
        k.dma("sync", v3(x2q_h[:, :], DC), x2th[:, :, qtok:qtok + 512],
              wait=ev_1a_end_x2 + [("ts", ev_x2q_used)], sem="sx2qh")
        k.dma("gpsimd", v3(x2q_l[:, :], DC), x2tl[:, :, qtok:qtok + 512],
              wait=[("ts", ev_x2q_used)], sem="sx2ql")
        ev_x2q_w = [("sx2qh", k.cnt["sx2qh"]), ("sx2ql", k.cnt["sx2ql"])]
        if q == 0:
            ev_wis_half = {0: [], 1: []}
            k.dma("scalar", rowb[0:1, :], b_in[0:1, 4 * 512:4 * 512 + 512],
                  wait=[("vs", ev_consts_v)], sem="sbr")
            ldb_k[0] = k.cnt["sbr"]
            k.dma("scalar", rowc[0:1, :], b_in[0:1, 8 * 512:8 * 512 + 512], sem="sbr")
            ldb_v[0] = k.cnt["sbr"]

        def load_chunk(n, q=q):
            j = CHUNK_ORDER[n % 12]
            slot = n % 2
            if q == 0 and n < 12:
                for qtr in range(4):
                    hb = qtr % 2
                    buf = wi_stage[:, hb * 2048:hb * 2048 + 2048]
                    qn = "swis" if hb == 0 else "swis2"
                    ld = k.dma("sync" if hb == 0 else "scalar", buf,
                               w_in[j][:, qtr * 2048:qtr * 2048 + 2048],
                               wait=ev_wis_half[hb], sem=qn)
                    for dcl in range(4):
                        dc = qtr * 4 + dcl
                        seg = buf[:, dcl * 512:(dcl + 1) * 512]
                        dh = wi_h[slot][:, dc * 512:(dc + 1) * 512]
                        k.op("scalar", lambda e, seg=seg, dh=dh, dc=dc: e.activation(
                            dh, seg, AF.Identity, scale=gpart[:, dc:dc + 1]),
                            wait=([(qn, ld), ("ts", ev_wi_used[slot]),
                                   ("swip", k.cnt.get("swip", 0))]
                                  if dcl == 0 else ()),
                            inc="ss" if dcl == 3 else None)
                    evh = k.cnt["ss"]
                    for dcl in range(4):
                        dc = qtr * 4 + dcl
                        seg = buf[:, dcl * 512:(dcl + 1) * 512]
                        dh = wi_h[slot][:, dc * 512:(dc + 1) * 512]
                        dl = wi_l[slot][:, dc * 512:(dc + 1) * 512]
                        k.op("vector", lambda e, seg=seg, dh=dh, dl=dl, dc=dc: e.scalar_tensor_tensor(
                            out=dl, in0=seg, scalar=gpart[:, dc:dc + 1], in1=dh,
                            op0=ALU.mult, op1=ALU.subtract),
                            wait=[("ss", evh)] if dcl == 0 else (),
                            inc="vs" if dcl == 3 else None)
                    ev_wis_half[hb] = [("vs", k.cnt["vs"])]
                k.dma("gpsimd", wih_d[j], wi_h[slot][:, :], wait=[("ss", k.cnt["ss"])],
                      sem="swip")
                k.dma("gpsimd", wil_d[j], wi_l[slot][:, :], wait=[("vs", k.cnt["vs"])],
                      sem="swip")
                ev_wi_persist[n] = k.cnt["swip"]
                return [("vs", k.cnt["vs"]), ("ss", k.cnt["ss"])]
            else:
                pw = k.cnt["swip"] if (q == 0) else ev_wi_persist_all
                k.dma("sync", wi_h[slot][:, :], wih_d[j],
                      wait=[("ts", ev_wi_used[slot]), ("swip", pw)],
                      sem="swi%d" % slot)
                k.dma("scalar", wi_l[slot][:, :], wil_d[j],
                      wait=[("ts", ev_wi_used[slot]), ("swip", pw)],
                      sem="swl%d" % slot)
                return [("swi%d" % slot, k.cnt["swi%d" % slot]),
                        ("swl%d" % slot, k.cnt["swl%d" % slot])]

        if q == 0:
            ev_wl = {0: load_chunk(0), 1: load_chunk(1)}
        else:
            ev_wl = dict(next_q_wl)
        nload = 2

        for wj in range(4):
            slot = wj % 2
            wlc = ev_wl[slot]
            for qfl in range(4):
                qf = 4 * wj + qfl
                bi = qf % 2
                bank = pb[bi]
                wl = list(wlc) + bankfree[bi] + ev_x2q_w
                nmm = 0
                for dc in range(DC):
                    lhh = wi_h[slot][:, dc * 512 + qfl * 128: dc * 512 + qfl * 128 + 128]
                    lhl = wi_l[slot][:, dc * 512 + qfl * 128: dc * 512 + qfl * 128 + 128]
                    rhh = x2q_h[:, dc * 512:(dc + 1) * 512]
                    rhl = x2q_l[:, dc * 512:(dc + 1) * 512]
                    for lw, rw in ((lhh, rhh), (lhh, rhl), (lhl, rhh)):
                        nmm += 1
                        k.op("tensor", lambda e, lw=lw, rw=rw, bank=bank, nmm=nmm: e.matmul(
                            bank[:, :], lw, rw, start=(nmm == 1), stop=(nmm == 48)),
                            wait=wl if nmm == 1 else (),
                            inc="ts" if nmm == 48 else None)
                ev_qmm = k.cnt["ts"]
                if qfl == 3:
                    ev_wi_used[slot] = ev_qmm
                    if nload < 12:
                        ev_wl[nload % 2] = load_chunk(nload)
                        nload += 1
                qs_ = qf % 2
                k.op("scalar", lambda e, bank=bank, qs_=qs_, qf=qf: e.activation(
                    qt_sh[qs_][:, :], bank[:, :], AF.Identity,
                    bias=bqp[:, qf:qf + 1], scale=1.0),
                    wait=[("ts", ev_qmm), ("sqt%d" % qs_, ev_qt_free[qs_])], inc="ss")
                eh = k.cnt["ss"]
                k.op("vector", lambda e, bank=bank, qs_=qs_, qf=qf: e.scalar_tensor_tensor(
                    out=qt_sl[qs_][:, :], in0=bank[:, :], scalar=bqp[:, qf:qf + 1],
                    in1=qt_sh[qs_][:, :], op0=ALU.add, op1=ALU.subtract),
                    wait=[("ts", ev_qmm), ("ss", eh), ("sqt%d" % qs_, ev_qt_free[qs_])],
                    inc="vs")
                bankfree[bi] = [("vs", k.cnt["vs"])]
                k.dma("gpsimd", qth[:, qf, qtok:qtok + 512], qt_sh[qs_][:, :],
                      wait=[("ss", eh), ("vs", k.cnt["vs"])], sem="sqt%d" % qs_)
                k.dma("gpsimd", qtl[:, qf, qtok:qtok + 512], qt_sl[qs_][:, :],
                      sem="sqt%d" % qs_)
                ev_qt_free[qs_] = k.cnt["sqt%d" % qs_]

        kv_banks = [2, 3, 4, 5, 6, 7]
        kvb_i = 0
        for i in range(4):
            tok0 = qtok + i * 128
            k.dma("gpsimd", cs_b[i][:, 0:512], cos_in[tok0:tok0 + 128, :],
                  wait=[("vs", ev_cs_free[i])], sem="scs%d" % i)
            k.dma("gpsimd", cs_b[i][:, 512:1024], sin_in[tok0:tok0 + 128, :],
                  sem="scs%d" % i)
            ev_cs_free[("ld", i)] = k.cnt["scs%d" % i]
        kbank = {}
        for pj in range(4):
            nk_idx, nv_idx = 4 + 2 * pj, 5 + 2 * pj
            slot_kc, slot_vc = nk_idx % 2, nv_idx % 2
            jk, jv = CHUNK_ORDER[nk_idx], CHUNK_ORDER[nv_idx]
            if q == 0:
                rw_k = [("sbr", ldb_k[pj])]
                rw_v = [("sbr", ldb_v[pj])]
                browk, browv = rowb, rowc
            else:
                rw_k = rw_v = [("sm", ev_kvrows)]
                browk = wi_stage[0:1, (jk - 4) * 512:(jk - 3) * 512]
                browv = wi_stage[0:1, (jv - 4) * 512:(jv - 3) * 512]
            ev_K = {}
            for i in range(4):
                bk = kv_banks[kvb_i % 6]; kvb_i += 1
                kbank[i] = bk
                bank = pb[bk]
                wl = list(bankfree[bk]) + rw_k
                if i == 0:
                    wl = wl + list(ev_wl[slot_kc]) + ev_x2q_w
                k.op("tensor", lambda e, bank=bank, brow=browk: e.matmul(
                    bank[:, :], ones_row[0:1, 0:128], brow[0:1, :],
                    start=True, stop=False), wait=wl)
                nmm = 0
                for dc in range(DC):
                    lhh = x2q_h[:, dc * 512 + i * 128: dc * 512 + i * 128 + 128]
                    lhl = x2q_l[:, dc * 512 + i * 128: dc * 512 + i * 128 + 128]
                    rhh = wi_h[slot_kc][:, dc * 512:(dc + 1) * 512]
                    rhl = wi_l[slot_kc][:, dc * 512:(dc + 1) * 512]
                    for lw, rw in ((lhh, rhh), (lhh, rhl), (lhl, rhh)):
                        nmm += 1
                        k.op("tensor", lambda e, lw=lw, rw=rw, bank=bank, nmm=nmm: e.matmul(
                            bank[:, :], lw, rw, start=False, stop=(nmm == 48)),
                            inc="ts" if nmm == 48 else None)
                ev_K[i] = k.cnt["ts"]
            ev_wi_used[slot_kc] = ev_K[3]
            if q == 0 and pj < 3:
                jk2 = CHUNK_ORDER[nk_idx + 2]
                k.dma("scalar", rowb[0:1, :], b_in[0:1, jk2 * 512:jk2 * 512 + 512],
                      wait=[("ts", ev_K[3])], sem="sbr")
                ldb_k[pj + 1] = k.cnt["sbr"]
            if nload < 12:
                ev_wl[nload % 2] = load_chunk(nload)
                nload += 1
            elif q < 3:
                next_q_wl[0] = load_chunk(12)
            ev_V = {}
            for i in range(4):
                bv = kv_banks[kvb_i % 6]; kvb_i += 1
                bank = pb[bv]
                wl = list(bankfree[bv]) + rw_v
                if i == 0:
                    wl = wl + list(ev_wl[slot_vc]) + ev_x2q_w
                k.op("tensor", lambda e, bank=bank, brow=browv: e.matmul(
                    bank[:, :], ones_row[0:1, 0:128], brow[0:1, :],
                    start=True, stop=False), wait=wl)
                nmm = 0
                for dc in range(DC):
                    lhh = x2q_h[:, dc * 512 + i * 128: dc * 512 + i * 128 + 128]
                    lhl = x2q_l[:, dc * 512 + i * 128: dc * 512 + i * 128 + 128]
                    rhh = wi_h[slot_vc][:, dc * 512:(dc + 1) * 512]
                    rhl = wi_l[slot_vc][:, dc * 512:(dc + 1) * 512]
                    for lw, rw in ((lhh, rhh), (lhh, rhl), (lhl, rhh)):
                        nmm += 1
                        k.op("tensor", lambda e, lw=lw, rw=rw, bank=bank, nmm=nmm: e.matmul(
                            bank[:, :], lw, rw, start=False, stop=(nmm == 48)),
                            inc="ts" if nmm == 48 else None)
                ev_V[i] = k.cnt["ts"]
                pk, pv = pb[kbank[i]], pb[bv]
                k.op("vector", lambda e, pk=pk: e.tensor_copy(ks_[:, :], pk[:, :]),
                     wait=[("ts", ev_K[i]), ("vs", k.cnt["vs"])], inc="vs")
                ev_cpk = k.cnt["vs"]
                bankfree[kbank[i]] = [("vs", ev_cpk)]
                k.op("scalar", lambda e, i=i, pj=pj: e.activation(
                    sqw[:, :], ks_[:, :], AF.Square,
                    accum_out=nsq[:, i * 4 + pj:i * 4 + pj + 1]),
                    wait=[("vs", ev_cpk)], inc="ss")
                ev_sqK = k.cnt["ss"]
                psl = slice(pj * 512, pj * 512 + 512)
                if pj < 2:
                    cth = cs_b[i][:, pj * 256:pj * 256 + 256]
                    sth = cs_b[i][:, 512 + pj * 256:512 + pj * 256 + 256]
                    ke, ko = ks_[:, 0:512:2], ks_[:, 1:512:2]
                    ve, vo = vsb[:, 0:512:2], vsb[:, 1:512:2]
                    re_, ro_ = krot[:, 0:512:2], krot[:, 1:512:2]
                    ma, mb_ = m4[:, 0:256], m4[:, 256:512]
                    mc, md = m4[:, 512:768], m4[:, 768:1024]
                    k.op("vector", lambda e, ke=ke, cth=cth: e.tensor_mul(ma, ke, cth),
                         wait=[("scs%d" % i, ev_cs_free[("ld", i)]),
                               ("vs", k.cnt["vs"])])
                    k.op("vector", lambda e, ko=ko, sth=sth: e.tensor_mul(mb_, ko, sth))
                    k.op("vector", lambda e, ko=ko, cth=cth: e.tensor_mul(mc, ko, cth))
                    k.op("vector", lambda e, ke=ke, sth=sth: e.tensor_mul(md, ke, sth),
                         inc="vs")
                    k.op("vector", lambda e, re_=re_: e.tensor_sub(re_, ma, mb_),
                         wait=[("vs", k.cnt["vs"])])
                    k.op("vector", lambda e, ro_=ro_: e.tensor_add(ro_, mc, md), inc="vs")
                    k.op("vector", lambda e, pv=pv: e.tensor_copy(vsb[:, :], pv[:, :]),
                         wait=[("ts", ev_V[i]), ("vs", k.cnt["vs"])], inc="vs")
                    ev_cpv = k.cnt["vs"]
                    bankfree[bv] = [("vs", ev_cpv)]
                    k.op("scalar", lambda e, i=i, pj=pj, psl=psl: e.activation(
                        P4[i][:, psl], vsb[:, :], AF.Square,
                        accum_out=nsq[:, 16 + i * 4 + pj:17 + i * 4 + pj]),
                        wait=[("vs", ev_cpv), ("vs", ev_P_free[i])], inc="ss")
                    ev_sqV = k.cnt["ss"]
                    k.op("vector", lambda e, ve=ve, cth=cth: e.tensor_mul(ma, ve, cth),
                         wait=[("ss", ev_sqK), ("vs", k.cnt["vs"])])
                    k.op("vector", lambda e, vo=vo, sth=sth: e.tensor_mul(mb_, vo, sth))
                    k.op("vector", lambda e, vo=vo, cth=cth: e.tensor_mul(mc, vo, cth))
                    k.op("vector", lambda e, ve=ve, sth=sth: e.tensor_mul(md, ve, sth),
                         inc="vs")
                    k.op("vector", lambda e, ke=ke: e.tensor_sub(ke, ma, mb_),
                         wait=[("vs", k.cnt["vs"])])
                    k.op("vector", lambda e, ko=ko: e.tensor_add(ko, mc, md), inc="vs")
                    k.op("vector", lambda e, i=i, psl=psl: e.tensor_mul(
                        P4[i][:, psl], krot[:, :], ks_[:, :]),
                        wait=[("vs", k.cnt["vs"]), ("ss", ev_sqV)], inc="vs")
                    if pj == 1:
                        ev_cs_free[i] = k.cnt["vs"]
                else:
                    k.op("vector", lambda e, pv=pv: e.tensor_copy(vsb[:, :], pv[:, :]),
                         wait=[("ts", ev_V[i]), ("vs", k.cnt["vs"])], inc="vs")
                    ev_cpv = k.cnt["vs"]
                    bankfree[bv] = [("vs", ev_cpv)]
                    k.op("scalar", lambda e, i=i, pj=pj, psl=psl: e.activation(
                        P4[i][:, psl], vsb[:, :], AF.Square,
                        accum_out=nsq[:, 16 + i * 4 + pj:17 + i * 4 + pj]),
                        wait=[("vs", ev_cpv), ("vs", ev_P_free[i])], inc="ss")
                    ev_sqV = k.cnt["ss"]
                    k.op("vector", lambda e, i=i, psl=psl: e.tensor_mul(
                        P4[i][:, psl], ks_[:, :], vsb[:, :]),
                        wait=[("ss", ev_sqV), ("vs", k.cnt["vs"])], inc="vs")
                if pj == 3:
                    k.op("vector", lambda e, i=i: e.reduce_sum(
                        nsq[:, 32:33], nsq[:, i * 4:i * 4 + 4], axis=AX),
                        wait=[("ss", ev_sqV), ("vs", k.cnt["vs"])], inc="vs")
                    svop(lambda e, i=i: e.reduce_sum(
                        nsq[:, 33:34], nsq[:, 16 + i * 4:16 + i * 4 + 4], axis=AX))
                    svop(lambda e: e.tensor_mul(nsq[:, 34:35], nsq[:, 32:33],
                                                nsq[:, 33:34]))
                    svop(lambda e: e.reciprocal(nsq[:, 36:37], nsq[:, 34:35]))
                    ev_np = k.cnt["vs"]
                    k.op("scalar", lambda e: e.activation(nsq[:, 35:36], nsq[:, 36:37],
                                                          AF.Sqrt),
                         wait=[("vs", ev_np)], inc="ss")
                    ev_st2 = k.cnt["ss"]
                    k.op("vector", lambda e, i=i: e.scalar_tensor_tensor(
                        out=a_acc[:, :], in0=P4[i][:, :], scalar=nsq[:, 35:36],
                        in1=a_acc[:, :], op0=ALU.mult, op1=ALU.add),
                        wait=[("ss", ev_st2)], inc="vs")
                    ev_aacc_v = k.cnt["vs"]
                    ev_P_free[i] = ev_aacc_v
            ev_wi_used[slot_vc] = ev_V[3]
            if q == 0 and pj < 3:
                jv2 = CHUNK_ORDER[nv_idx + 2]
                k.dma("scalar", rowc[0:1, :], b_in[0:1, jv2 * 512:jv2 * 512 + 512],
                      wait=[("ts", ev_V[3])], sem="sbr")
                ldb_v[pj + 1] = k.cnt["sbr"]
            if nload < 12:
                ev_wl[nload % 2] = load_chunk(nload)
                nload += 1
            elif q < 3:
                next_q_wl[1] = load_chunk(13)
        ev_x2q_used = k.cnt["ts"]
        if q == 2:
            for c in range(DC):
                k.op("tensor", lambda e, c=c: e.matmul(
                    pb[0][:, c:c + 1], a_acc[:, c * 128:(c + 1) * 128], ones_col[:, 0:1],
                    start=(c == 0), stop=(c == DC - 1)),
                    wait=([("vs", ev_aacc_v)] + bankfree[0]) if c == 0 else (),
                    inc="ts" if c == DC - 1 else None)
            ev_ap1 = k.cnt["ts"]
            k.op("vector", lambda e: e.tensor_copy(a_p[:, 0:16], pb[0][:, 0:16]),
                 wait=[("ts", ev_ap1), ("vs", k.cnt["vs"])], inc="vs")
            bankfree[0] = [("vs", k.cnt["vs"])]
            k.op("vector", lambda e: e.tensor_scalar_mul(ccs[:, 0:16], a_p[:, 0:16],
                                                         bsel[:, 0:1]),
                 wait=[("vs", k.cnt["vs"])])
            k.op("vector", lambda e: e.tensor_scalar_mul(ccs[:, 16:32], a_p[:, 0:16],
                                                         bsel[:, 1:2]), inc="vs")
            ev_ccs1 = k.cnt["vs"]
            k.dma("gpsimd", cc_in_d.ap(), ccs[:, 0:32], wait=[("vs", ev_ccs1)], sem="scc")
            ev_ccin1 = k.cnt["scc"]
            k.op("gpsimd", lambda e: e.collective_compute(
                "AllReduce", ALU.add, replica_groups=[list(range(8))],
                ins=[cc_in_d.ap().opt()], outs=[cc_out_d.ap().opt()]),
                wait=[("scc", ev_ccin1)], inc="cc")
            k.op("vector", lambda e: e.memset(a_acc[:, :], 0.0),
                 wait=[("ts", ev_ap1), ("vs", k.cnt["vs"])], inc="vs")
            ev_aacc_v = k.cnt["vs"]
        if q == 0:
            ev_wi_persist_all = k.cnt["swip"]
            k.dma("sync", v3(wi_stage[0:1, 0:4096], 8),
                  b_in.rearrange("z (c n) -> z c n", c=12)[:, 4:12, :],
                  wait=[("vs", k.cnt["vs"]), ("ss", k.cnt["ss"])], sem="sm")
            ev_kvrows = k.cnt["sm"]

    for c in range(DC):
        k.op("tensor", lambda e, c=c: e.matmul(
            pb[0][:, c:c + 1], a_acc[:, c * 128:(c + 1) * 128], ones_col[:, 0:1],
            start=(c == 0), stop=(c == DC - 1)),
            wait=([("vs", ev_aacc_v)] + bankfree[0]) if c == 0 else (),
            inc="ts" if c == DC - 1 else None)
    ev_ap_t = k.cnt["ts"]
    k.op("vector", lambda e: e.tensor_copy(a_p[:, 0:16], pb[0][:, 0:16]),
         wait=[("ts", ev_ap_t)], inc="vs")
    bankfree[0] = [("vs", k.cnt["vs"])]
    k.op("vector", lambda e: e.tensor_scalar_mul(ccs[:, 0:16], a_p[:, 0:16], bsel[:, 0:1]),
         wait=[("scc", k.cnt["scc"]), ("vs", k.cnt["vs"])])
    k.op("vector", lambda e: e.tensor_scalar_mul(ccs[:, 16:32], a_p[:, 0:16], bsel[:, 1:2]),
         inc="vs")
    ev_ccs = k.cnt["vs"]
    k.dma("gpsimd", cc_in_d2.ap(), ccs[:, 0:32], wait=[("vs", ev_ccs)], sem="scc")
    ev_ccin = k.cnt["scc"]
    k.op("gpsimd", lambda e: e.collective_compute(
        "AllReduce", ALU.add, replica_groups=[list(range(8))],
        ins=[cc_in_d2.ap().opt()], outs=[cc_out_d2.ap().opt()]),
        wait=[("scc", ev_ccin)], inc="cc")
    k.dma("sync", ccs[:, 0:32], cc_out_d.ap(),
          wait=[("cc", 1), ("scc", ev_ccin)], sem="sccr")
    k.dma("sync", ccs2[:, 0:32], cc_out_d2.ap(), wait=[("cc", 2)], sem="sccr")
    ev_ccb = k.cnt["sccr"]
    k.op("vector", lambda e: e.tensor_tensor(ccs[:, 0:32], ccs[:, 0:32], ccs2[:, 0:32],
                                             ALU.add),
         wait=[("sccr", ev_ccb), ("vs", k.cnt["vs"])], inc="vs")
    k.op("vector", lambda e: e.tensor_scalar_mul(a_fin[:, 0:16], ccs[:, 0:16], bsel[:, 0:1]),
         wait=[("vs", k.cnt["vs"])], inc="vs")
    k.op("vector", lambda e: e.scalar_tensor_tensor(
        out=a_fin[:, 0:16], in0=ccs[:, 16:32], scalar=bsel[:, 1:2], in1=a_fin[:, 0:16],
        op0=ALU.mult, op1=ALU.add), wait=[("vs", k.cnt["vs"])], inc="vs")
    k.op("vector", lambda e: e.tensor_scalar(
        out=a_fin[:, 0:16], in0=a_fin[:, 0:16], scalar1=0.5, scalar2=0.5,
        op0=ALU.mult, op1=ALU.add), wait=[("vs", k.cnt["vs"])], inc="vs")
    ev_afin = k.cnt["vs"]
    if DEBUG:
        k.dma("gpsimd", dbg["dbg_ap"], a_p[:, 0:16], wait=[("vs", ev_afin)], sem="sdbg")
        k.dma("gpsimd", dbg["dbg_af"], a_fin[:, 0:16], sem="sdbg")

    fa.pop()
    ba.pop()
    k.barrier()
    wo_st = [fa.alloc(1024) for _ in range(16)]
    s1b = [fa.alloc(1024), fa.alloc(1024)]
    outb = [fa.alloc(1024), fa.alloc(1024)]
    woA_h = ba.alloc(16384)
    woA_l = ba.alloc(16384)
    qt_h = [ba.alloc(4096), ba.alloc(4096)]
    qt_l = [ba.alloc(4096), ba.alloc(4096)]

    ev_qt_used = {0: 0, 1: 0}
    ev_s1_used = {0: 0, 1: 0}
    ev_out_st = {0: 0, 1: 0}
    ld_qt_g = {}

    for half in range(2):
        hsl = slice(half * 1024, half * 1024 + 1024)
        qt_all = [("sqt0", k.cnt.get("sqt0", 0)), ("sqt1", k.cnt.get("sqt1", 0))]
        for g in range(2):
            gslot = g % 2
            k.dma("gpsimd", v3(qt_h[gslot][:, :], DC), qth[:, :, g * 256:(g + 1) * 256],
                  wait=qt_all + [("ts", ev_qt_used[gslot])], sem="spq%d" % gslot)
            k.dma("gpsimd", v3(qt_l[gslot][:, :], DC), qtl[:, :, g * 256:(g + 1) * 256],
                  sem="spq%d" % gslot)
            ld_qt_g[g] = k.cnt["spq%d" % gslot]
        ts_prev_half = k.cnt["ts"] if half else 0
        ev_wo_spl = {}
        for dc in range(DC):
            wsl = wo_st[dc]
            wl_ld = [("vs", ev_wo_spl0[dc])] if half else []
            ld = k.dma("scalar", wsl[:, :], w_out[dc * 128:dc * 128 + 128, hsl],
                       wait=wl_ld, sem="swo%d" % dc)
            k.op("scalar", lambda e, wsl=wsl, dc=dc: e.activation(
                woA_h[:, dc * 1024:(dc + 1) * 1024], wsl[:, :], AF.Identity,
                scale=a_fin[:, dc:dc + 1]),
                wait=[("swo%d" % dc, ld), ("vs", ev_afin), ("ts", ts_prev_half)],
                inc="ss")
            eh = k.cnt["ss"]
            k.op("vector", lambda e, wsl=wsl, dc=dc: e.scalar_tensor_tensor(
                out=woA_l[:, dc * 1024:(dc + 1) * 1024], in0=wsl[:, :],
                scalar=a_fin[:, dc:dc + 1], in1=woA_h[:, dc * 1024:(dc + 1) * 1024],
                op0=ALU.mult, op1=ALU.subtract),
                wait=[("ss", eh)], inc="vs")
            ev_wo_spl[dc] = k.cnt["vs"]
        ev_wo_spl0 = dict(ev_wo_spl)
        if DEBUG and half == 1:
            k.dma("gpsimd", dbg["dbg_woh"][:, :], woA_h[:, :],
                  wait=[("vs", k.cnt["vs"]), ("ss", k.cnt["ss"])])
            k.dma("gpsimd", dbg["dbg_wol"][:, :], woA_l[:, :], sem="sdbg")
            k.dma("gpsimd", dbg["dbg_af2"][:, :], a_fin[:, 0:16], sem="sdbg")

        for tt in range(16):
            tok0 = tt * 128
            g = tt // 2
            gslot = g % 2
            u = tt % 2
            slot = tt % 2
            s1_all = ([("ss10", k.cnt.get("ss10", 0)), ("ss11", k.cnt.get("ss11", 0))]
                      if (half == 0 and tt < 2) else [])
            k.dma("sync", s1b[slot][:, :], s1_d[tok0:tok0 + 128, hsl],
                  wait=s1_all + [("vs", ev_s1_used[slot])], sem="sps%d" % slot)
            ld_s1 = k.cnt["sps%d" % slot]
            for oc in range(2):
                bi = 2 * slot + oc
                bank = pb[bi]
                wl = bankfree[bi] + [("spq%d" % gslot, ld_qt_g[g])]
                nmm = 0
                for qc in range(DC):
                    lhh = qt_h[gslot][:, qc * 256 + u * 128: qc * 256 + u * 128 + 128]
                    lhl = qt_l[gslot][:, qc * 256 + u * 128: qc * 256 + u * 128 + 128]
                    rhh = woA_h[:, qc * 1024 + oc * 512: qc * 1024 + oc * 512 + 512]
                    rhl = woA_l[:, qc * 1024 + oc * 512: qc * 1024 + oc * 512 + 512]
                    for lw, rw in ((lhh, rhh), (lhh, rhl), (lhl, rhh)):
                        wlq = (wl if nmm == 0 else []) + (
                            [("vs", ev_wo_spl[qc])] if nmm % 3 == 0 else [])
                        k.op("tensor", lambda e, lw=lw, rw=rw, bank=bank, nmm=nmm: e.matmul(
                            bank[:, :], lw, rw, start=(nmm == 0), stop=(nmm == 47)),
                            wait=wlq,
                            inc="ts" if nmm == 47 else None)
                        nmm += 1
                ev_mm = k.cnt["ts"]
                k.op("vector", lambda e, bank=bank, slot=slot, oc=oc: e.tensor_tensor(
                    outb[slot][:, oc * 512:(oc + 1) * 512], bank[:, :],
                    s1b[slot][:, oc * 512:(oc + 1) * 512], ALU.add),
                    wait=[("ts", ev_mm), ("sps%d" % slot, ld_s1),
                          ("sy%d" % slot, ev_out_st[slot])], inc="vs")
                bankfree[bi] = [("vs", k.cnt["vs"])]
            if u == 1:
                ev_qt_used[gslot] = k.cnt["ts"]
                gn = g + 2
                if gn <= 7:
                    k.dma("gpsimd", v3(qt_h[gslot][:, :], DC),
                          qth[:, :, gn * 256:(gn + 1) * 256],
                          wait=[("ts", ev_qt_used[gslot])], sem="spq%d" % gslot)
                    k.dma("gpsimd", v3(qt_l[gslot][:, :], DC),
                          qtl[:, :, gn * 256:(gn + 1) * 256], sem="spq%d" % gslot)
                    ld_qt_g[gn] = k.cnt["spq%d" % gslot]
            ev_s1_used[slot] = k.cnt["vs"]
            k.dma("gpsimd", y_out[tok0:tok0 + 128, hsl], outb[slot][:, :],
                  wait=[("vs", k.cnt["vs"])], sem="sy%d" % slot)
            ev_out_st[slot] = k.cnt["sy%d" % slot]

    if DEBUG:
        k.dma("gpsimd", dbg["dbg_qh"][:, :], qth_d[0:D, :], wait=[("vs", k.cnt["vs"])], sem="sdbg")
        k.dma("gpsimd", dbg["dbg_ql"][:, :], qtl_d[0:D, :], sem="sdbg")
        k.dma("gpsimd", dbg["dbg_s1d"][:, :], s1_d[:, :], sem="sdbg")
        k.dma("gpsimd", dbg["dbg_x2h"][:, :], x2th_d[0:D, :], sem="sdbg")
    k.op("gpsimd", lambda e: e.wait_ge(k.semh["sy0"], k.cnt["sy0"]))
    k.op("gpsimd", lambda e: e.wait_ge(k.semh["sy1"], k.cnt["sy1"]))
    if DEBUG and "sdbg" in k.cnt:
        k.op("gpsimd", lambda e: e.wait_ge(k.semh["sdbg"], k.cnt["sdbg"]))
    return k.finish()


_NC_CACHE = None


def _get_nc():
    global _NC_CACHE
    if _NC_CACHE is None:
        _NC_CACHE = build_nc()
    return _NC_CACHE


def kernel(**inputs):
    X = np.asarray(inputs["L0_attn_X"], dtype=np.float32)
    w_in_raw = np.asarray(inputs["L0_attn_W_in"], np.float32)
    w_in = np.ascontiguousarray(
        w_in_raw.reshape(DC, 128, 12, 512).transpose(2, 1, 0, 3).reshape(12, 128, 8192))
    b_in = np.asarray(inputs["L0_attn_b_in"], np.float32).reshape(1, 3 * D)
    w_out = np.ascontiguousarray(np.asarray(inputs["L0_attn_W_out"], np.float32))
    b_out = np.asarray(inputs["L0_attn_b_out"], np.float32).reshape(1, D)
    gamma = np.asarray(inputs["L0_attn_gamma"], np.float32)
    beta = np.asarray(inputs["L0_attn_beta"], np.float32)
    w_ffn = np.asarray(inputs["L0_ffn_W"], np.float32)
    b_ffn = np.asarray(inputs["L0_ffn_b"], np.float32).reshape(1, D)
    lora_A = np.asarray(inputs["L0_ffn_lora_A"], np.float32)
    lora_B = np.asarray(inputs["L0_ffn_lora_B"], np.float32)
    gate_w = np.asarray(inputs["L0_ffn_gate_w"], np.float32)
    gate_b = np.asarray(inputs["L0_ffn_gate_b"], np.float32).reshape(1, 8)

    w_ffnT = np.ascontiguousarray(
        w_ffn.T.reshape(DC, 128, 4, 512).transpose(2, 1, 0, 3).reshape(4, 128, 8192))
    atall = np.ascontiguousarray(np.transpose(lora_A, (2, 0, 1)).reshape(D, 128))
    btall = np.ascontiguousarray(np.transpose(lora_B, (0, 2, 1)).reshape(128, D))
    gwT = np.ascontiguousarray(gate_w.T)
    gamma_part = np.ascontiguousarray(gamma.reshape(DC, 128).T)
    bq_part = np.ascontiguousarray(b_in[0, 0:D].reshape(DC, 128).T)
    ident = np.eye(128, dtype=np.float32)

    inv = 10000.0 ** (-np.arange(0, 1024, 2, dtype=np.float64) / 1024.0)
    in_maps = []
    for c in range(8):
        b = c // 4
        t0 = (c % 4) * TOK
        pos = np.arange(t0, t0 + TOK, dtype=np.float64)
        ang = pos[:, None] * inv[None, :]
        bsel_np = np.zeros((128, 2), np.float32)
        bsel_np[:, b] = 1.0
        in_maps.append({
            "x_in": np.ascontiguousarray(X[b, t0:t0 + TOK]),
            "w_in": w_in, "b_in": b_in, "w_out": w_out,
            "w_ffnT": w_ffnT, "gamma_part": gamma_part, "bq_part": bq_part,
            "gamma_row": gamma.reshape(1, D), "beta_row": beta.reshape(1, D),
            "b_ffn": b_ffn, "b_out_r": b_out,
            "atall": atall, "btall": btall, "gwT": gwT, "gb": gate_b,
            "cos_in": np.cos(ang).astype(np.float32),
            "sin_in": np.sin(ang).astype(np.float32),
            "ident_in": ident, "bsel_in": bsel_np,
        })
    nc = _get_nc()
    res = run_bass_kernel_spmd(nc, in_maps, core_ids=list(range(8)))
    out = np.empty((2, 8192, D), np.float32)
    for c in range(8):
        out[c // 4, (c % 4) * TOK:(c % 4 + 1) * TOK] = res.results[c]["y_out"]
    return out
